# version 09
# baseline (speedup 1.0000x reference)
import numpy as np
from contextlib import ExitStack

import concourse.bass as bass
import concourse.mybir as mybir
from concourse.bass_utils import run_bass_kernel_spmd

F32 = mybir.dt.float32
BF16 = mybir.dt.bfloat16
I32 = mybir.dt.int32
ALU = mybir.AluOpType
AF = mybir.ActivationFunctionType
AX = mybir.AxisListType

NCORES = 8
B_PER_CORE = 2
S = 2048
D = 1024
H = 16
DH = 64
DI = 2816
NT = S // 128
ND = D // 128
NI = DI // 128
QC = 512
NQC = S // QC
EXPB = -20.0
ENGINES = ("sync", "gpsimd", "vector", "scalar", "tensor")
DEBUG_TAPS = False
DEBUG_TAPS_BATCH = 0


class Ctr:
    def __init__(self, stack, nc, name):
        self.h = stack.enter_context(nc.semaphore(name))
        self.v = 0

    def bump(self, n=1):
        self.v += n
        return self.v


def build_program():
    nc = bass.Bass("TRN2", target_bir_lowering=False,
                   detect_race_conditions=False)

    x_d = nc.dram_tensor("x", [B_PER_CORE, S, D], F32, kind="ExternalInput").ap()
    qw_d = nc.dram_tensor("qw", [H, D, DH], F32, kind="ExternalInput").ap()
    kw_d = nc.dram_tensor("kw", [H, D, DH], F32, kind="ExternalInput").ap()
    vw_d = nc.dram_tensor("vw", [H, D, DH], F32, kind="ExternalInput").ap()
    qb_d = nc.dram_tensor("qb", [H, DH], F32, kind="ExternalInput").ap()
    kb_d = nc.dram_tensor("kb", [H, DH], F32, kind="ExternalInput").ap()
    vb_d = nc.dram_tensor("vb", [H, DH], F32, kind="ExternalInput").ap()
    ow_d = nc.dram_tensor("ow", [H, DH, D], F32, kind="ExternalInput").ap()
    ob_d = nc.dram_tensor("ob", [1, D], F32, kind="ExternalInput").ap()
    wg_d = nc.dram_tensor("wg", [DI, D], F32, kind="ExternalInput").ap()
    wu_d = nc.dram_tensor("wu", [DI, D], F32, kind="ExternalInput").ap()
    wd_d = nc.dram_tensor("wd", [D, DI], F32, kind="ExternalInput").ap()
    y_d = nc.dram_tensor("y", [B_PER_CORE, S, D], F32, kind="ExternalOutput").ap()
    wkind = "ExternalOutput" if DEBUG_TAPS else "Internal"
    wgt_d = nc.dram_tensor("wgt", [128, ND, DI], BF16, kind=wkind).ap()
    wut_d = nc.dram_tensor("wut", [128, ND, DI], BF16, kind=wkind).ap()
    wdt_d = nc.dram_tensor("wdt", [128, NI, D], BF16, kind=wkind).ap()
    dbg = {}
    if DEBUG_TAPS:
        for nm, shp, dt2 in (("xnT", [128, ND, S], BF16),
                             ("qt", [128, 8, S], BF16),
                             ("kt", [128, 8, S], BF16),
                             ("vp", [128, NT, H * 65], BF16),
                             ("ao", [128, NT, D], BF16),
                             ("pt", [128, NT, QC], BF16),
                             ("dst", [128, H * NT], F32),
                             ("drc", [128, H * NT], F32),
                             ("rstd", [128, NT], F32),
                             ("nmr", [128, NT], F32),
                             ("xqT", [128, ND, S], BF16),
                             ("hqT", [128, NI, 512], BF16)):
            dbg[nm] = nc.dram_tensor("dbg_" + nm, shp, dt2,
                                     kind="ExternalOutput").ap()

    prog = {e: [] for e in ENGINES}

    def on(eng, fn):
        prog[eng].append(fn)

    QW = 1024

    with ExitStack() as st:
        def sb(name, shape, dtype, **kw):
            return st.enter_context(nc.sbuf_tensor(name, shape, dtype, **kw))

        neg20 = sb("neg20", [128, 1], F32)
        masks = sb("masks", [128, 4, QC], BF16)
        ob_t = sb("ob_t", [128, D], F32)
        vb_t = sb("vb_t", [128, D], F32)
        qb_sb = sb("qb_sb", [128, 8], F32)
        kb_sb = sb("kb_sb", [128, 8], F32)
        st_rstd = sb("st_rstd", [128, NT], F32)
        st_nmr = sb("st_nmr", [128, NT], F32)
        qs = {}

        def alloc_qs(scope, tag):
            qs.clear()
            for k in ("A", "B", "C", "D"):
                qs[k] = scope.enter_context(
                    nc.sbuf_tensor(f"qs_{k}_{tag}", [128, QW], F32))
            qs['E'] = scope.enter_context(
                nc.sbuf_tensor(f"qs_E_{tag}", [128, QW], I32))
            for k in ("bm", "scale", "lns", "rsh", "sc2"):
                qs[k] = scope.enter_context(
                    nc.sbuf_tensor(f"qs_{k}_{tag}", [128, QW // 16], F32))

        sv = Ctr(st, nc, "sv")
        ss = Ctr(st, nc, "ss")
        stn = Ctr(st, nc, "stn")
        sg = Ctr(st, nc, "sg")
        dx = Ctr(st, nc, "dx")
        dtp = Ctr(st, nc, "dtp")
        dc = Ctr(st, nc, "dc")
        bar = Ctr(st, nc, "bar")
        nbar = [0]

        def barrier():
            nbar[0] += 1
            k = nbar[0]
            for e in ENGINES:
                on(e, lambda eng, k=k: (eng.drain().then_inc(bar.h, 1),
                                        eng.wait_ge(bar.h, 5 * k)))

        def wait(eng, ctr, val):
            if val > 0:
                on(eng, lambda E, c=ctr, vv=val: E.wait_ge(c.h, vv))

        def emit_quant(x_ap, out_ap, F):
            nb = F // 16
            xv = x_ap.rearrange("p (b w) -> p b w", w=16)
            A, Bb, C, Dt, E = (qs['A'][:, 0:F], qs['B'][:, 0:F],
                               qs['C'][:, 0:F], qs['D'][:, 0:F],
                               qs['E'][:, 0:F])
            bm, scale, lns, rsh, sc2 = (qs[k][:, 0:nb] for k in
                                        ("bm", "scale", "lns", "rsh", "sc2"))
            v1 = sv.bump()
            on('vector', lambda v, xv=xv, bm=bm, v1=v1: (
                v.tensor_reduce(out=bm, in_=xv, axis=AX.X, op=ALU.max,
                                apply_absolute_value=True).then_inc(sv.h, 1),
                v.wait_ge(sv.h, v1)))
            v2 = sv.bump()
            on('vector', lambda v, bm=bm, scale=scale: (
                v.tensor_scalar(
                    out=scale, in0=bm, scalar1=1.0 / 6.0, scalar2=1e-12,
                    op0=ALU.mult, op1=ALU.max),
                v.drain().then_inc(sv.h, 1)))
            s1 = ss.bump()
            on('scalar', lambda s, scale=scale, lns=lns, rsh=rsh, sc2=sc2,
               v2=v2: (
                s.wait_ge(sv.h, v2),
                s.activation(out=lns, in_=scale, func=AF.Ln, bias=0.0,
                             scale=2.0),
                s.activation(out=rsh, in_=lns, func=AF.Exp, bias=0.0,
                             scale=-1.0),
                s.mul(sc2, scale, 0.5),
                s.drain().then_inc(ss.h, 1)))
            lns2 = qs['lns'][:, 0:nb]
            scale2 = qs['scale'][:, 0:nb]
            nr1 = sv.bump(3)
            vend = sv.bump()

            bm2 = qs['bm'][:, 0:nb]

            def main(v, xv=xv, A=A, Bb=Bb, C=C, Dt=Dt, E=E, rsh=rsh,
                     sc2=sc2, out_ap=out_ap, nb=nb, s1=s1, lns2=lns2,
                     scale2=scale2, nr1=nr1, bm2=bm2):
                v.wait_ge(ss.h, s1)
                v.tensor_tensor(out=lns2, in0=scale2, in1=rsh,
                                op=ALU.mult).then_inc(sv.h, 1)
                v.wait_ge(sv.h, nr1 - 2)
                v.tensor_scalar(out=bm2, in0=lns2, scalar1=-2.0, scalar2=2.0,
                                op0=ALU.mult, op1=ALU.add).then_inc(sv.h, 1)
                v.wait_ge(sv.h, nr1 - 1)
                v.tensor_tensor(out=lns2, in0=rsh, in1=bm2,
                                op=ALU.mult).then_inc(sv.h, 1)
                v.wait_ge(sv.h, nr1)
                rsh_b = lns2.unsqueeze(-1).broadcast_to([128, nb, 16])
                sc2_b = sc2.unsqueeze(-1).broadcast_to([128, nb, 16])
                v.tensor_tensor(out=A.rearrange("p (b w) -> p b w", w=16),
                                in0=xv, in1=rsh_b, op=ALU.mult)
                v.tensor_scalar(out=Bb, in0=A, scalar1=1.0, scalar2=-1.0,
                                op0=ALU.min, op1=ALU.max)
                v.tensor_scalar(out=C, in0=A, scalar1=2.0, scalar2=-2.0,
                                op0=ALU.min, op1=ALU.max)
                v.scalar_tensor_tensor(out=Dt, in0=Bb, scalar=2.0, in1=A,
                                       op0=ALU.mult, op1=ALU.add)
                v.tensor_tensor(out=Bb, in0=Dt, in1=C, op=ALU.add)
                v.tensor_scalar(out=E, in0=Bb, scalar1=8.0, scalar2=0.0,
                                op0=ALU.add, op1=ALU.add)
                v.scalar_tensor_tensor(out=Dt, in0=E, scalar=-7.5, in1=Bb,
                                       op0=ALU.add, op1=ALU.is_lt)
                v.tensor_tensor(out=C, in0=Dt, in1=E, op=ALU.add)
                v.tensor_scalar(out=A, in0=C, scalar1=12.0, scalar2=4.0,
                                op0=ALU.min, op1=ALU.max)
                v.tensor_scalar(out=Dt, in0=C, scalar1=14.0, scalar2=2.0,
                                op0=ALU.min, op1=ALU.max)
                v.scalar_tensor_tensor(out=Bb, in0=C, scalar=2.0, in1=Dt,
                                       op0=ALU.mult, op1=ALU.subtract)
                v.scalar_tensor_tensor(out=C, in0=Bb, scalar=2.0, in1=A,
                                       op0=ALU.mult, op1=ALU.subtract)
                v.scalar_tensor_tensor(
                    out=out_ap.rearrange("p (b w) -> p b w", w=16),
                    in0=C.rearrange("p (b w) -> p b w", w=16), scalar=-8.0,
                    in1=sc2_b, op0=ALU.add, op1=ALU.mult)
                v.drain().then_inc(sv.h, 1)

            on('vector', main)
            return vend

        with ExitStack() as st0:
            iota_t = st0.enter_context(nc.sbuf_tensor("iota_t", [128, QC], F32))
            g1 = sg.bump()
            on('gpsimd', lambda g: (
                g.memset(neg20[:], EXPB),
                g.iota(iota_t[:], [[1, QC]], base=0, channel_multiplier=-1,
                       allow_small_or_imprecise_dtypes=True)
                .then_inc(sg.h, 1)))
            on('sync', lambda sync: (
                sync.dma_start(out=ob_t[:], in_=ob_d.broadcast_to([128, D]))
                .then_inc(dx.h, 16),
                sync.dma_start(out=vb_t[:],
                               in_=vb_d.rearrange("h d -> (h d)").unsqueeze(0)
                               .broadcast_to([128, D])).then_inc(dx.h, 16),
                sync.dma_start(out=qb_sb[:],
                               in_=bass.AP(qb_d.tensor, 0,
                                           [[1, 128], [128, 8]]),
                               allow_slow_non_contiguous=True)
                .then_inc(dx.h, 16),
                sync.dma_start(out=kb_sb[:],
                               in_=bass.AP(kb_d.tensor, 0,
                                           [[1, 128], [128, 8]]),
                               allow_slow_non_contiguous=True)
                .then_inc(dx.h, 16)))
            dx.bump(64)
            setup_dx = dx.v
            wait('vector', sg, g1)
            for m in range(4):
                sv.bump()
                on('vector', lambda v, m=m: v.tensor_scalar(
                    out=masks[:, m, :], in0=iota_t[:], scalar1=float(128 * m),
                    scalar2=0.0, op0=ALU.is_ge, op1=ALU.add)
                   .then_inc(sv.h, 1))
            masks_done = sv.v

            alloc_qs(st0, "w")
            w_in = [st0.enter_context(nc.sbuf_tensor(f"w_in{i}", [128, DI],
                                                     F32)) for i in range(2)]
            w_bf = [st0.enter_context(nc.sbuf_tensor(f"w_bf{i}", [128, DI],
                                                     BF16)) for i in range(2)]
            w_tp = [st0.enter_context(nc.sbuf_tensor(
                f"w_tp{i}", [128, NI, 128], BF16, is_transpose_dest=True))
                for i in range(2)]

            tiles = []
            for (src, rows, cols, dstT) in ((wg_d, DI, D, wgt_d),
                                            (wu_d, DI, D, wut_d),
                                            (wd_d, D, DI, wdt_d)):
                for t in range(rows // 128):
                    tiles.append((src, cols, dstT, t))

            vq_done, tp_done, stw_done, ld_done = [], [], [], []
            for k, (src, cols, dstT, t) in enumerate(tiles):
                nsub = cols // 128
                if k >= 2:
                    wait('sync', sv, vq_done[k - 2])
                on('sync', lambda sync, src=src, t=t, cols=cols, k=k:
                   sync.dma_start(out=w_in[k % 2][:, 0:cols],
                                  in_=src[t * 128:(t + 1) * 128, :])
                   .then_inc(dx.h, 16))
                ld_done.append(dx.bump(16))
                wait('vector', dx, ld_done[k])
                if k >= 2:
                    wait('vector', dtp, tp_done[k - 2])
                for piece in range(0, cols, QW):
                    pw = min(QW, cols - piece)
                    emit_quant(w_in[k % 2][:, piece:piece + pw],
                               w_bf[k % 2][:, piece:piece + pw], pw)
                vq_done.append(sv.v)
                wait('scalar', sv, vq_done[k])
                if k >= 2:
                    wait('scalar', dx, stw_done[k - 2])
                on('scalar', lambda s, k=k, cols=cols, nsub=nsub:
                   s.dma_start(out=w_tp[k % 2][:, 0:nsub, :],
                               in_=w_bf[k % 2][:, 0:cols], transpose=True)
                   .then_inc(dtp.h, 16))
                tp_done.append(dtp.bump(16))
                wait('sync', dtp, tp_done[k])
                on('sync', lambda sync, dstT=dstT, t=t, nsub=nsub, k=k:
                   sync.dma_start(out=dstT[:, 0:nsub, t * 128:(t + 1) * 128],
                                  in_=w_tp[k % 2][:, 0:nsub, :])
                   .then_inc(dx.h, 16))
                stw_done.append(dx.bump(16))

            wait('sync', dx, dx.v)
            wait('sync', dx, setup_dx)
            wait('vector', sv, masks_done)
            barrier()

        xq_tp_hist = []
        out_hist = []

        def plan_batch(b):
            with ExitStack() as stb:
                attn_out = stb.enter_context(nc.sbuf_tensor(
                    f"attn_out_{b}", [128, NT, D], BF16,
                    is_transpose_dest=True))
                xnT = attn_out[:].rearrange("p a c -> p (a c)") \
                    .rearrange("p (u w) -> p u w", u=ND)
                attnv = attn_out[:]

                with ExitStack() as sta:
                    qt_all = sta.enter_context(nc.sbuf_tensor(
                        f"qt_all_{b}", [128, 8, S], BF16))
                    kt_all = sta.enter_context(nc.sbuf_tensor(
                        f"kt_all_{b}", [128, 8, S], BF16))
                    v_perm = sta.enter_context(nc.sbuf_tensor(
                        f"v_perm_{b}", [128, NT, H * 65], BF16))
                    pt = sta.enter_context(nc.sbuf_tensor(
                        f"pt_{b}", [128, NT, QC], BF16))

                    with ExitStack() as stp:
                        x_t = [stp.enter_context(nc.sbuf_tensor(
                            f"x_t{i}_{b}", [128, D], F32)) for i in range(2)]
                        xsq = stp.enter_context(nc.sbuf_tensor(
                            f"xsq_{b}", [128, D], F32))
                        xn_bf = [stp.enter_context(nc.sbuf_tensor(
                            f"xn_bf{i}_{b}", [128, D], BF16))
                            for i in range(2)]
                        sm1 = stp.enter_context(nc.sbuf_tensor(
                            f"sm1_{b}", [128, 1], F32))
                        sm2 = stp.enter_context(nc.sbuf_tensor(
                            f"sm2_{b}", [128, 1], F32))
                        msq = stp.enter_context(nc.sbuf_tensor(
                            f"msq_{b}", [128, 1], F32))
                        var = stp.enter_context(nc.sbuf_tensor(
                            f"var_{b}", [128, 1], F32))
                        lnv = stp.enter_context(nc.sbuf_tensor(
                            f"lnv_{b}", [128, 1], F32))
                        mmean = stp.enter_context(nc.sbuf_tensor(
                            f"mmean_{b}", [128, 1], F32))
                        wbuf = [stp.enter_context(nc.sbuf_tensor(
                            f"wbuf{i}_{b}", [128, ND, 1024], BF16))
                            for i in range(2)]

                        ones_done = sv.bump()
                        on('vector', lambda v: v.memset(
                            bass.AP(v_perm, 64,
                                    [[NT * H * 65, 128], [H * 65, NT],
                                     [65, H]]), 1.0).then_inc(sv.h, 1))

                        tp_x_done = []
                        ln_s3 = []
                        for t in range(NT):
                            if t >= 2:
                                wait('sync', ss, ln_s3[t - 2])
                            on('sync', lambda sync, t=t: sync.dma_start(
                                out=x_t[t % 2][:],
                                in_=x_d[b, t * 128:(t + 1) * 128, :])
                               .then_inc(dx.h, 16))
                            xldt = dx.bump(16)
                            v1 = sv.bump()
                            on('vector', lambda v, t=t, v1=v1, xldt=xldt: (
                                v.wait_ge(dx.h, xldt),
                                v.tensor_reduce(out=sm1[:], in_=x_t[t % 2][:],
                                                axis=AX.X, op=ALU.add)
                                .then_inc(sv.h, 1),
                                v.wait_ge(sv.h, v1)))
                            s1 = ss.bump()
                            on('scalar', lambda s, t=t, xldt=xldt: (
                                s.wait_ge(dx.h, xldt),
                                s.square(xsq[:], x_t[t % 2][:]),
                                s.drain().then_inc(ss.h, 1)))
                            v2 = sv.bump()
                            on('vector', lambda v, s1=s1, v2=v2: (
                                v.wait_ge(ss.h, s1),
                                v.tensor_reduce(out=sm2[:], in_=xsq[:],
                                                axis=AX.X, op=ALU.add)
                                .then_inc(sv.h, 1),
                                v.wait_ge(sv.h, v2)))
                            v3 = sv.bump(4)
                            on('vector', lambda v, v3=v3: (
                                v.tensor_scalar(out=mmean[:], in0=sm1[:],
                                                scalar1=1.0 / D, scalar2=0.0,
                                                op0=ALU.mult, op1=ALU.add)
                                .then_inc(sv.h, 1),
                                v.wait_ge(sv.h, v3 - 3),
                                v.tensor_tensor(out=msq[:], in0=mmean[:],
                                                in1=sm1[:], op=ALU.mult)
                                .then_inc(sv.h, 1),
                                v.wait_ge(sv.h, v3 - 2),
                                v.tensor_tensor(out=var[:], in0=sm2[:],
                                                in1=msq[:], op=ALU.subtract)
                                .then_inc(sv.h, 1),
                                v.wait_ge(sv.h, v3 - 1),
                                v.tensor_scalar(out=var[:], in0=var[:],
                                                scalar1=1.0 / D, scalar2=1e-5,
                                                op0=ALU.mult, op1=ALU.add)
                                .then_inc(sv.h, 1),
                                v.wait_ge(sv.h, v3)))
                            s2 = ss.bump()
                            on('scalar', lambda s, t=t, v3=v3: (
                                s.wait_ge(sv.h, v3),
                                s.activation(out=lnv[:], in_=var[:],
                                             func=AF.Ln, bias=0.0, scale=1.0),
                                s.activation(out=st_rstd[:, t:t + 1],
                                             in_=lnv[:], func=AF.Exp,
                                             bias=0.0, scale=-0.5),
                                s.drain().then_inc(ss.h, 1)))
                            v5 = sv.bump()
                            on('vector', lambda v, t=t, s2=s2: (
                                v.wait_ge(ss.h, s2),
                                v.scalar_tensor_tensor(
                                    out=st_nmr[:, t:t + 1], in0=mmean[:],
                                    scalar=-1.0, in1=st_rstd[:, t:t + 1],
                                    op0=ALU.mult, op1=ALU.mult),
                                v.drain().then_inc(sv.h, 1)))
                            s3 = ss.bump()
                            on('scalar', lambda s, t=t, v5=v5: (
                                s.wait_ge(sv.h, v5),
                                (s.wait_ge(dtp.h, tp_x_done[t - 2])
                                 if t >= 2 else None),
                                s.activation(out=xn_bf[t % 2][:],
                                             in_=x_t[t % 2][:], func=AF.Identity,
                                             bias=st_nmr[:, t:t + 1],
                                             scale=st_rstd[:, t:t + 1]),
                                s.drain().then_inc(ss.h, 1)))
                            ln_s3.append(s3)
                            on('scalar', lambda s, t=t, s3=s3: (
                                s.wait_ge(ss.h, s3),
                                s.dma_start(
                                    out=xnT[:, :, t * 128:(t + 1) * 128],
                                    in_=xn_bf[t % 2][:], transpose=True)
                                .then_inc(dtp.h, 16)))
                            tp_x_done.append(dtp.bump(16))

                        ps_qkv = stp.enter_context(nc.psum_tensor(
                            f"ps_qkv_{b}", [128, 2, 512], F32))

                        def load_qk(j, wdram):
                            for hp in range(8):
                                for hf in range(2):
                                    on('gpsimd', lambda g, wdram=wdram,
                                       wsb=wbuf[j % 2], hp=hp, hf=hf:
                                       g.dma_start(
                                        out=bass.AP(wsb, hp * 128 + hf * 64,
                                                    [[ND * 1024, 128],
                                                     [1024, ND], [1, 64]]),
                                        in_=bass.AP(wdram.tensor,
                                                    (2 * hp + hf) * D * DH,
                                                    [[64, 128],
                                                     [128 * 64, ND],
                                                     [1, 64]]))
                                       .then_inc(dc.h, 16))
                                    dc.bump(16)
                            return dc.v

                        wl_q = load_qk(0, qw_d)
                        wl_k = load_qk(1, kw_d)

                        wait('tensor', dtp, tp_x_done[-1])
                        slot = 0
                        ev_done = []
                        wuse = {}
                        for j, (bsb, dst) in enumerate(((qb_sb, qt_all),
                                                       (kb_sb, kt_all))):
                            wait('tensor', dc, wl_q if j == 0 else wl_k)
                            for hp in range(8):
                                for c in range(4):
                                    sl = slot % 2
                                    if slot >= 2:
                                        e, m = ev_done[slot - 2]
                                        wait('tensor', ss if e == 'ss'
                                             else sv, m)
                                    tv = stn.bump()
                                    on('tensor', lambda tn, j=j, hp=hp, c=c,
                                       sl=sl: [
                                        tn.matmul(
                                            ps_qkv[:, sl, :],
                                            wbuf[j % 2][:, d,
                                                        hp * 128:(hp + 1) * 128],
                                            xnT[:, d, c * 512:(c + 1) * 512],
                                            start=(d == 0),
                                            stop=(d == ND - 1))
                                        for d in range(ND)][-1]
                                       .then_inc(stn.h, 1))
                                    se = ss.bump()
                                    on('scalar', lambda s, bsb=bsb, dst=dst,
                                       hp=hp, c=c, sl=sl, tv=tv: (
                                        s.wait_ge(stn.h, tv),
                                        s.activation(
                                            out=dst[:, hp,
                                                    c * 512:(c + 1) * 512],
                                            in_=ps_qkv[:, sl, :],
                                            func=AF.Identity,
                                            bias=bsb[:, hp:hp + 1],
                                            scale=1.0).then_inc(ss.h, 1)))
                                    ev_done.append(('ss', se))
                                    slot += 1
                            wuse[j] = stn.v
                        wait('gpsimd', stn, wuse[0])
                        for h2 in range(H):
                            on('gpsimd', lambda g, h2=h2: g.dma_start(
                                out=bass.AP(wbuf[0], h2 * 64,
                                            [[ND * 1024, 128], [1024, ND],
                                             [1, 64]]),
                                in_=bass.AP(vw_d.tensor, h2 * D * DH,
                                            [[64, 128], [128 * 64, ND],
                                             [1, 64]]))
                               .then_inc(dc.h, 16))
                            dc.bump(16)
                        wl_v = dc.v
                        wait('tensor', dc, wl_v)
                        wait('vector', sv, ones_done)
                        for t in range(NT):
                            for c in range(2):
                                sl = slot % 2
                                if slot >= 2:
                                    e, m = ev_done[slot - 2]
                                    wait('tensor', ss if e == 'ss' else sv, m)
                                tv = stn.bump()
                                on('tensor', lambda tn, t=t, c=c, sl=sl: [
                                    tn.matmul(
                                        ps_qkv[:, sl, :],
                                        xnT[:, d, t * 128:(t + 1) * 128],
                                        wbuf[0][:, d, c * 512:(c + 1) * 512],
                                        start=(d == 0), stop=(d == ND - 1))
                                    for d in range(ND)][-1]
                                   .then_inc(stn.h, 1))
                                vvl = None
                                for hh in range(8):
                                    h = c * 8 + hh
                                    vvl = sv.bump()
                                    on('vector', lambda v, t=t, sl=sl, hh=hh,
                                       h=h, tv=tv: (
                                        (v.wait_ge(stn.h, tv)
                                         if hh == 0 else None),
                                        v.scalar_tensor_tensor(
                                            out=v_perm[:, t,
                                                       h * 65:h * 65 + 64],
                                            in0=ps_qkv[:, sl,
                                                       hh * 64:(hh + 1) * 64],
                                            scalar=0.0,
                                            in1=vb_t[:, h * 64:(h + 1) * 64],
                                            op0=ALU.add, op1=ALU.add)
                                        .then_inc(sv.h, 1)))
                                ev_done.append(('sv', vvl))
                                slot += 1
                        for (e, m) in ev_done[-2:]:
                            wait('tensor', ss if e == 'ss' else sv, m)
                        barrier()
                        if DEBUG_TAPS and b == DEBUG_TAPS_BATCH:
                            on('sync', lambda sync: (
                                sync.dma_start(out=dbg["xnT"],
                                               in_=attn_out[:].rearrange(
                                                   "p a c -> p (a c)")
                                               .rearrange("p (u w) -> p u w",
                                                          u=ND))
                                .then_inc(dx.h, 16),
                                sync.dma_start(out=dbg["qt"], in_=qt_all[:])
                                .then_inc(dx.h, 16),
                                sync.dma_start(out=dbg["kt"], in_=kt_all[:])
                                .then_inc(dx.h, 16),
                                sync.dma_start(out=dbg["vp"], in_=v_perm[:])
                                .then_inc(dx.h, 16),
                                sync.dma_start(out=dbg["rstd"], in_=st_rstd[:])
                                .then_inc(dx.h, 16),
                                sync.dma_start(out=dbg["nmr"], in_=st_nmr[:])
                                .then_inc(dx.h, 16)))
                            dx.bump(96)
                            wait('sync', dx, dx.v)
                            barrier()

                    with ExitStack() as sta3:
                        esc = sta3.enter_context(nc.sbuf_tensor(
                            f"esc_{b}", [128, QC], BF16))
                        dstage = sta3.enter_context(nc.sbuf_tensor(
                            f"dstage_{b}", [128, H * NT], F32))
                        drec = sta3.enter_context(nc.sbuf_tensor(
                            f"drec_{b}", [128, H * NT], F32))
                        ow_sb = sta3.enter_context(nc.sbuf_tensor(
                            f"ow_sb_{b}", [128, 8, D], BF16))
                        anorm = [sta3.enter_context(nc.sbuf_tensor(
                            f"anorm{i}_{b}", [128, D], BF16))
                            for i in range(2)]
                        avT = [sta3.enter_context(nc.sbuf_tensor(
                            f"avT{i}_{b}", [128, 8, 128], BF16,
                            is_transpose_dest=True)) for i in range(2)]
                        ps_l = sta3.enter_context(nc.psum_tensor(
                            f"ps_l_{b}", [128, 2, 512], F32))
                        ps_pv = sta3.enter_context(nc.psum_tensor(
                            f"ps_pv_{b}", [128, 2, 512], F32))
                        ps_o = sta3.enter_context(nc.psum_tensor(
                            f"ps_o_{b}", [128, 2, 512], F32))

                        for half in range(2):
                            on('gpsimd', lambda g, half=half: g.dma_start(
                                out=ow_sb[half * 64:(half + 1) * 64, :, :],
                                in_=bass.AP(ow_d.tensor, half * DH * D,
                                            [[D, 64], [2 * DH * D, 8],
                                             [1, D]])).then_inc(dc.h, 16))
                            dc.bump(16)
                        ow_dc = dc.v

                        lslot, pvslot = 0, 0
                        l_cons, pv_cons = [], []
                        last_pv_tv = 0
                        for h in range(H):
                            hp, po = h // 2, (h % 2) * 64
                            for c in range(NQC):
                                nkt = 4 * (c + 1)
                                exp_marks = []
                                if last_pv_tv:
                                    wait('scalar', stn, last_pv_tv)
                                    wait('vector', stn, last_pv_tv)
                                for kt in range(nkt):
                                    slq = lslot % 2
                                    if lslot >= 2:
                                        wait('tensor', ss, l_cons[lslot - 2])
                                    tv = stn.bump()
                                    on('tensor', lambda tn, hp=hp, po=po,
                                       kt=kt, c=c, slq=slq: tn.matmul(
                                        ps_l[:, slq, :],
                                        kt_all[po:po + 64, hp,
                                               kt * 128:(kt + 1) * 128],
                                        qt_all[po:po + 64, hp,
                                               c * QC:(c + 1) * QC],
                                        start=True, stop=True)
                                       .then_inc(stn.h, 1))
                                    diag = (kt >= nkt - 4)
                                    se = ss.bump()
                                    if not diag:
                                        on('scalar', lambda s, kt=kt, slq=slq,
                                           tv=tv: (
                                            s.wait_ge(stn.h, tv),
                                            s.activation(
                                                out=pt[:, kt, :],
                                                in_=ps_l[:, slq, :],
                                                func=AF.Exp, bias=neg20[:],
                                                scale=0.125)
                                            .then_inc(ss.h, 1)))
                                        exp_marks.append(('s', se))
                                    else:
                                        on('scalar', lambda s, slq=slq,
                                           tv=tv: (
                                            s.wait_ge(stn.h, tv),
                                            s.activation(
                                                out=esc[:],
                                                in_=ps_l[:, slq, :],
                                                func=AF.Exp, bias=neg20[:],
                                                scale=0.125)
                                            .then_inc(ss.h, 1)))
                                        mv = sv.bump()
                                        mi = kt - (nkt - 4)
                                        on('vector', lambda v, kt=kt, mi=mi,
                                           se=se: (
                                            v.wait_ge(ss.h, se),
                                            v.tensor_tensor(
                                                out=pt[:, kt, :],
                                                in0=esc[:],
                                                in1=masks[:, mi, :],
                                                op=ALU.mult)
                                            .then_inc(sv.h, 1)))
                                        exp_marks.append(('v', mv))
                                        wait('scalar', sv, mv)
                                    l_cons.append(se)
                                    lslot += 1
                                for qt4 in range(4):
                                    qt = c * 4 + qt4
                                    slp = pvslot % 2
                                    if pvslot >= 2:
                                        wait('tensor', sv,
                                             pv_cons[pvslot - 2])
                                    need = exp_marks[:qt + 1]
                                    svn = max([m for k2, m in need
                                               if k2 == 'v'], default=0)
                                    ssn = max([m for k2, m in need
                                               if k2 == 's'], default=0)
                                    if ssn:
                                        wait('tensor', ss, ssn)
                                    if svn:
                                        wait('tensor', sv, svn)
                                    tv = stn.bump()
                                    on('tensor', lambda tn, h=h, qt=qt,
                                       qt4=qt4, slp=slp, nk=qt + 1: [
                                        tn.matmul(
                                            ps_pv[:, slp, 0:65],
                                            pt[:, kt2,
                                               qt4 * 128:(qt4 + 1) * 128],
                                            v_perm[:, kt2,
                                                   h * 65:h * 65 + 65],
                                            start=(kt2 == 0),
                                            stop=(kt2 == nk - 1))
                                        for kt2 in range(nk)][-1]
                                       .then_inc(stn.h, 1))
                                    vv1 = sv.bump(2)
                                    on('vector', lambda v, h=h, qt=qt,
                                       slp=slp, tv=tv: (
                                        v.wait_ge(stn.h, tv),
                                        v.tensor_copy(
                                            out=attnv[:, qt,
                                                      h * 64:(h + 1) * 64],
                                            in_=ps_pv[:, slp, 0:64]),
                                        v.tensor_copy(
                                            out=dstage[:, h * NT + qt:
                                                       h * NT + qt + 1],
                                            in_=ps_pv[:, slp, 64:65])
                                        .then_inc(sv.h, 2)))
                                    pv_cons.append(vv1)
                                    pvslot += 1
                                    last_pv_tv = stn.v
                        vdr = sv.bump()
                        on('vector', lambda v: v.drain().then_inc(sv.h, 1))
                        sdr = ss.bump(2)
                        on('scalar', lambda s, vv=sv.v, sdr=sdr: (
                            s.wait_ge(sv.h, vv),
                            s.activation(out=drec[:], in_=dstage[:],
                                         func=AF.Ln, bias=0.0, scale=1.0)
                            .then_inc(ss.h, 1),
                            s.wait_ge(ss.h, sdr - 1),
                            s.activation(out=drec[:], in_=drec[:],
                                         func=AF.Exp, bias=0.0, scale=-1.0)
                            .then_inc(ss.h, 1),
                            s.wait_ge(ss.h, sdr)))

                        wait('tensor', dc, ow_dc)
                        tpo_done, oev_done, oproj_tv = [], [], []
                        oslot = 0
                        for qt in range(NT):
                            sm = None
                            for h in range(H):
                                sm = ss.bump()
                                on('scalar', lambda s, qt=qt, h=h, sm=sm,
                                   sdr=sdr: (
                                    (s.wait_ge(ss.h, sdr)
                                     if (h == 0 and qt == 0) else None),
                                    (s.wait_ge(dtp.h, tpo_done[qt - 2])
                                     if (h == 0 and qt >= 2) else None),
                                    s.activation(
                                        out=anorm[qt % 2][:,
                                                          h * 64:(h + 1) * 64],
                                        in_=attnv[:, qt, h * 64:(h + 1) * 64],
                                        func=AF.Identity, bias=0.0,
                                        scale=drec[:, h * NT + qt:
                                                   h * NT + qt + 1]),
                                    (s.drain() if h == H - 1 else s.nop())
                                    .then_inc(ss.h, 1)))
                            if qt >= 2:
                                wait('scalar', stn, oproj_tv[qt - 2])
                            on('scalar', lambda s, qt=qt, sm=sm: (
                                s.wait_ge(ss.h, sm),
                                s.dma_start(out=avT[qt % 2][:],
                                            in_=anorm[qt % 2][:],
                                            transpose=True)
                                .then_inc(dtp.h, 16)))
                            tpo_done.append(dtp.bump(16))
                            for ch in range(2):
                                slo = oslot % 2
                                if oslot >= 2:
                                    wait('tensor', sv, oev_done[oslot - 2])
                                if ch == 0:
                                    wait('tensor', dtp, tpo_done[qt])
                                tv = stn.bump()
                                on('tensor', lambda tn, qt=qt, ch=ch,
                                   slo=slo: [
                                    tn.matmul(
                                        ps_o[:, slo, :],
                                        avT[qt % 2][:, hp2, :],
                                        ow_sb[:, hp2, ch * 512:(ch + 1) * 512],
                                        start=(hp2 == 0), stop=(hp2 == 7))
                                    for hp2 in range(8)][-1]
                                   .then_inc(stn.h, 1))
                                vv = sv.bump()
                                on('vector', lambda v, qt=qt, ch=ch, slo=slo,
                                   tv=tv: (
                                    v.wait_ge(stn.h, tv),
                                    v.scalar_tensor_tensor(
                                        out=attn_out[:, qt,
                                                     ch * 512:(ch + 1) * 512],
                                        in0=ps_o[:, slo, :], scalar=0.0,
                                        in1=ob_t[:, ch * 512:(ch + 1) * 512],
                                        op0=ALU.add, op1=ALU.add)
                                    .then_inc(sv.h, 1)))
                                oev_done.append(vv)
                                oslot += 1
                            oproj_tv.append(stn.v)
                        wait('tensor', sv, oev_done[-1])
                        wait('tensor', sv, oev_done[-2])
                        if DEBUG_TAPS and b == DEBUG_TAPS_BATCH:
                            wait('sync', sv, sv.v)
                            on('sync', lambda sync: (
                                sync.dma_start(out=dbg["ao"], in_=attn_out[:])
                                .then_inc(dx.h, 16),
                                sync.dma_start(out=dbg["pt"], in_=pt[:])
                                .then_inc(dx.h, 16),
                                sync.dma_start(out=dbg["dst"], in_=dstage[:])
                                .then_inc(dx.h, 16),
                                sync.dma_start(out=dbg["drc"], in_=drec[:])
                                .then_inc(dx.h, 16)))
                            dx.bump(64)
                            wait('sync', dx, dx.v)
                    barrier()

                with ExitStack() as stf:
                    alloc_qs(stf, f"b{b}")
                    xqT = stf.enter_context(nc.sbuf_tensor(
                        f"xqT_{b}", [128, ND, S], BF16, is_transpose_dest=True))
                    xg = stf.enter_context(nc.sbuf_tensor(
                        f"xg_{b}", [128, 4, D], F32))
                    xnf = [stf.enter_context(nc.sbuf_tensor(
                        f"xnf{i}_{b}", [128, D], F32)) for i in range(2)]
                    xq_bf = [stf.enter_context(nc.sbuf_tensor(
                        f"xq_bf{i}_{b}", [128, D], BF16)) for i in range(2)]
                    hterm = [stf.enter_context(nc.sbuf_tensor(
                        f"hterm{i}_{b}", [128, 256], F32)) for i in range(2)]
                    silu_sb = [stf.enter_context(nc.sbuf_tensor(
                        f"silu{i}_{b}", [128, 256], F32)) for i in range(2)]
                    hq4 = stf.enter_context(nc.sbuf_tensor(
                        f"hq4_{b}", [128, 4, DI], BF16))
                    hqT = stf.enter_context(nc.sbuf_tensor(
                        f"hqT_{b}", [128, NI, 512], BF16,
                        is_transpose_dest=True))
                    wgu = [stf.enter_context(nc.sbuf_tensor(
                        f"wgu{i}_{b}", [128, ND, 512], BF16))
                        for i in range(2)]
                    wdt_sb = [stf.enter_context(nc.sbuf_tensor(
                        f"wdt_sb{i}_{b}", [128, 512], BF16))
                        for i in range(2)]
                    out_sb = [stf.enter_context(nc.sbuf_tensor(
                        f"out_sb{i}_{b}", [128, 512], F32)) for i in range(2)]
                    ps_gu = stf.enter_context(nc.psum_tensor(
                        f"ps_gu_{b}", [128, 4, 512], F32))
                    ps_d = stf.enter_context(nc.psum_tensor(
                        f"ps_d_{b}", [128, 4, 512], F32))

                    NCH = DI // 256
                    hq_tp_hist = []
                    xnf_qv = []
                    for g4 in range(4):
                        t0 = g4 * 4
                        for tt in range(4):
                            t = t0 + tt
                            if xq_tp_hist:
                                wait('sync', dtp, xq_tp_hist[-1])
                            on('sync', lambda sync, t=t, tt=tt: sync.dma_start(
                                out=xg[:, tt, :],
                                in_=x_d[b, t * 128:(t + 1) * 128, :])
                               .then_inc(dx.h, 16))
                            xldt = dx.bump(16)
                            se = ss.bump()
                            on('scalar', lambda s, t=t, tt=tt, xldt=xldt,
                               gq=(xnf_qv[-2] if len(xnf_qv) >= 2 else 0): (
                                s.wait_ge(dx.h, xldt),
                                (s.wait_ge(sv.h, gq) if gq else None),
                                s.activation(out=xnf[tt % 2][:],
                                             in_=xg[:, tt, :],
                                             func=AF.Identity,
                                             bias=st_nmr[:, t:t + 1],
                                             scale=st_rstd[:, t:t + 1]),
                                s.drain().then_inc(ss.h, 1)))
                            wait('vector', ss, se)
                            if len(xq_tp_hist) >= 2:
                                wait('vector', dtp, xq_tp_hist[-2])
                            emit_quant(xnf[tt % 2][:], xq_bf[tt % 2][:], D)
                            qv = sv.v
                            xnf_qv.append(qv)
                            on('scalar', lambda s, t=t, tt=tt, qv=qv: (
                                s.wait_ge(sv.h, qv),
                                s.dma_start(
                                    out=xqT[:, :, t * 128:(t + 1) * 128],
                                    in_=xq_bf[tt % 2][:], transpose=True)
                                .then_inc(dtp.h, 16)))
                            xq_tp_hist.append(dtp.bump(16))
                        wait('tensor', dtp, xq_tp_hist[-1])
                        if DEBUG_TAPS and b == DEBUG_TAPS_BATCH and g4 == 3:
                            wait('sync', dtp, xq_tp_hist[-1])
                            on('sync', lambda sync: sync.dma_start(
                                out=dbg["xqT"], in_=xqT[:])
                               .then_inc(dx.h, 16))
                            dx.bump(16)

                        wld = []
                        gu_cons = []
                        gslot = 0
                        wait('sync', dtp, xq_tp_hist[-1])
                        for c in range(NCH):
                            if c >= 2:
                                wait('sync', stn, wld[c - 2])
                            on('sync', lambda sync, c=c: (
                                sync.dma_start(
                                    out=wgu[c % 2][:, :, 0:256],
                                    in_=wgt_d[:, :, c * 256:(c + 1) * 256])
                                .then_inc(dx.h, 16),
                                sync.dma_start(
                                    out=wgu[c % 2][:, :, 256:512],
                                    in_=wut_d[:, :, c * 256:(c + 1) * 256])
                                .then_inc(dx.h, 16)))
                            dx.bump(32)
                            wldv = dx.v
                            lastmm = None
                            for tt in range(4):
                                t = t0 + tt
                                sg0 = 2 * (gslot % 2)
                                if gslot >= 2:
                                    wait('tensor', sv, gu_cons[gslot - 2])
                                tv = stn.bump()
                                on('tensor', lambda tn, c=c, t=t, sg0=sg0,
                                   wldv=wldv: (
                                    tn.wait_ge(dx.h, wldv),
                                    [tn.matmul(
                                        ps_gu[:, sg0, 0:256],
                                        xqT[:, d, t * 128:(t + 1) * 128],
                                        wgu[c % 2][:, d, 0:256],
                                        start=(d == 0), stop=(d == ND - 1))
                                     for d in range(ND)],
                                    [tn.matmul(
                                        ps_gu[:, sg0 + 1, 0:256],
                                        xqT[:, d, t * 128:(t + 1) * 128],
                                        wgu[c % 2][:, d, 256:512],
                                        start=(d == 0), stop=(d == ND - 1))
                                     for d in range(ND)][-1]
                                    .then_inc(stn.h, 1)))
                                lastmm = stn.v
                                se = ss.bump()
                                on('scalar', lambda s, sg0=sg0, tv=tv,
                                   gsl=gslot % 2,
                                   gg=(gu_cons[gslot - 2]
                                       if gslot >= 2 else 0): (
                                    s.wait_ge(stn.h, tv),
                                    (s.wait_ge(sv.h, gg) if gg else None),
                                    s.activation(out=silu_sb[gsl][:],
                                                 in_=ps_gu[:, sg0, 0:256],
                                                 func=AF.Silu, bias=0.0,
                                                 scale=1.0),
                                    s.drain().then_inc(ss.h, 1)))
                                vv = sv.bump()
                                on('vector', lambda v, tt=tt, sg0=sg0,
                                   se=se, gsl=gslot % 2: (
                                    v.wait_ge(ss.h, se),
                                    v.tensor_tensor(
                                        out=hterm[tt % 2][:],
                                        in0=silu_sb[gsl][:],
                                        in1=ps_gu[:, sg0 + 1, 0:256],
                                        op=ALU.mult).then_inc(sv.h, 1)))
                                gu_cons.append(vv)
                                emit_quant(hterm[tt % 2][:],
                                           hq4[:, tt, c * 256:(c + 1) * 256],
                                           256)
                                gslot += 1
                            wld.append(lastmm)
                        hq_done = sv.v

                        for tt in range(4):
                            if len(hq_tp_hist) >= 4:
                                pass
                            on('scalar', lambda s, tt=tt, hq_done=hq_done: (
                                s.wait_ge(sv.h, hq_done),
                                s.dma_start(
                                    out=hqT[:, :, tt * 128:(tt + 1) * 128],
                                    in_=hq4[:, tt, :], transpose=True)
                                .then_inc(dtp.h, 16)))
                            hq_tp_hist.append(dtp.bump(16))
                        wait('tensor', dtp, hq_tp_hist[-1])
                        if DEBUG_TAPS and b == DEBUG_TAPS_BATCH and g4 == 1:
                            wait('sync', dtp, hq_tp_hist[-1])
                            on('sync', lambda sync: sync.dma_start(
                                out=dbg["hqT"], in_=hqT[:])
                               .then_inc(dx.h, 16))
                            dx.bump(16)

                        fin_marks = []
                        wait('sync', dtp, hq_tp_hist[-1])
                        for ch in range(2):
                            wdld = []
                            for i in range(NI):
                                if i >= 2:
                                    wait('sync', stn, wdld[i - 2])
                                on('sync', lambda sync, i=i, ch=ch:
                                   sync.dma_start(
                                    out=wdt_sb[i % 2][:],
                                    in_=wdt_d[:, i, ch * 512:(ch + 1) * 512])
                                   .then_inc(dx.h, 16))
                                dx.bump(16)
                                wv = dx.v
                                if i == 0 and fin_marks:
                                    wait('tensor', sv, fin_marks[-1])
                                tv = stn.bump()
                                on('tensor', lambda tn, i=i, wv=wv: (
                                    tn.wait_ge(dx.h, wv),
                                    [tn.matmul(
                                        ps_d[:, tt2, :],
                                        hqT[:, i, tt2 * 128:(tt2 + 1) * 128],
                                        wdt_sb[i % 2][:],
                                        start=(i == 0), stop=(i == NI - 1))
                                     for tt2 in range(4)][-1]
                                    .then_inc(stn.h, 1)))
                                wdld.append(stn.v)
                            for tt in range(4):
                                t = t0 + tt
                                vf = sv.bump(2)
                                on('vector', lambda v, t=t, tt=tt, ch=ch,
                                   wl=wdld[-1], qA=qs['A']: (
                                    v.wait_ge(stn.h, wl),
                                    v.scalar_tensor_tensor(
                                        out=qA[:, 0:512],
                                        in0=ps_d[:, tt, :], scalar=0.0,
                                        in1=attn_out[:, t,
                                                     ch * 512:(ch + 1) * 512],
                                        op0=ALU.add, op1=ALU.add),
                                    v.tensor_tensor(
                                        out=out_sb[tt % 2][:],
                                        in0=qA[:, 0:512],
                                        in1=xg[:, tt, ch * 512:(ch + 1) * 512],
                                        op=ALU.add),
                                    v.drain().then_inc(sv.h, 2)))
                                fin_marks.append(vf)
                                on('sync', lambda sync, t=t, tt=tt, ch=ch,
                                   vf=vf: (
                                    sync.wait_ge(sv.h, vf),
                                    sync.dma_start(
                                        out=y_d[b, t * 128:(t + 1) * 128,
                                                ch * 512:(ch + 1) * 512],
                                        in_=out_sb[tt % 2][:])
                                    .then_inc(dx.h, 16)))
                                out_hist.append(dx.bump(16))
                                if tt >= 2 or True:
                                    wait('vector', dx, out_hist[-2]
                                         if len(out_hist) >= 2 else 0)
                        barrier()

        for b in range(B_PER_CORE):
            plan_batch(b)

        wait('sync', dx, dx.v)
        barrier()

        blk = st.enter_context(nc.Block())
        for e in ENGINES:
            def make(e):
                def body(eng):
                    for fn in prog[e]:
                        fn(eng)
                return body
            getattr(blk, e)(make(e))

    return nc


_NC_CACHE = [None]


def kernel(L0_attn_x, L0_attn_Q_w, L0_attn_Q_b, L0_attn_K_w, L0_attn_K_b,
           L0_attn_V_w, L0_attn_V_b, L0_attn_O_w, L0_attn_O_b,
           L0_ffn_w_gate, L0_ffn_w_up, L0_ffn_w_down):
    if _NC_CACHE[0] is None:
        _NC_CACHE[0] = build_program()
    nc = _NC_CACHE[0]

    x = np.ascontiguousarray(np.asarray(L0_attn_x, dtype=np.float32))
    common = {
        "qw": np.ascontiguousarray(np.asarray(L0_attn_Q_w, np.float32)),
        "kw": np.ascontiguousarray(np.asarray(L0_attn_K_w, np.float32)),
        "vw": np.ascontiguousarray(np.asarray(L0_attn_V_w, np.float32)),
        "qb": np.ascontiguousarray(np.asarray(L0_attn_Q_b, np.float32)),
        "kb": np.ascontiguousarray(np.asarray(L0_attn_K_b, np.float32)),
        "vb": np.ascontiguousarray(np.asarray(L0_attn_V_b, np.float32)),
        "ow": np.ascontiguousarray(np.asarray(L0_attn_O_w, np.float32)),
        "ob": np.ascontiguousarray(np.asarray(L0_attn_O_b, np.float32))
        .reshape(1, D),
        "wg": np.ascontiguousarray(np.asarray(L0_ffn_w_gate, np.float32)),
        "wu": np.ascontiguousarray(np.asarray(L0_ffn_w_up, np.float32)),
        "wd": np.ascontiguousarray(np.asarray(L0_ffn_w_down, np.float32)),
    }
    in_maps = []
    for c in range(NCORES):
        m = dict(common)
        m["x"] = np.ascontiguousarray(x[c * B_PER_CORE:(c + 1) * B_PER_CORE])
        in_maps.append(m)

    res = run_bass_kernel_spmd(nc, in_maps, core_ids=list(range(NCORES)))
    out = np.concatenate([np.asarray(r["y"]) for r in res.results], axis=0)
    return out.astype(np.float32)


# version 11
# speedup vs baseline: 1.0209x; 1.0209x over previous
import numpy as np
from contextlib import ExitStack

import concourse.bass as bass
import concourse.mybir as mybir
from concourse.bass_utils import run_bass_kernel_spmd

F32 = mybir.dt.float32
BF16 = mybir.dt.bfloat16
I32 = mybir.dt.int32
ALU = mybir.AluOpType
AF = mybir.ActivationFunctionType
AX = mybir.AxisListType

NCORES = 8
B_PER_CORE = 2
S = 2048
D = 1024
H = 16
DH = 64
DI = 2816
NT = S // 128
ND = D // 128
NI = DI // 128
QC = 512
NQC = S // QC
EXPB = -20.0
ENGINES = ("sync", "gpsimd", "vector", "scalar", "tensor")
DEBUG_TAPS = False
DEBUG_TAPS_BATCH = 0


class Ctr:
    def __init__(self, stack, nc, name):
        self.h = stack.enter_context(nc.semaphore(name))
        self.v = 0

    def bump(self, n=1):
        self.v += n
        return self.v


class Lanes:

    def __init__(self, stack, nc, name, k=8):
        self.lanes = [Ctr(stack, nc, f"{name}{i}") for i in range(k)]
        self.i = 0

    def next(self):
        ln = self.lanes[self.i % len(self.lanes)]
        self.i += 1
        return ln

    def totals(self):
        return [(ln, ln.v) for ln in self.lanes if ln.v > 0]


def build_program():
    nc = bass.Bass("TRN2", target_bir_lowering=False,
                   detect_race_conditions=False)

    x_d = nc.dram_tensor("x", [B_PER_CORE, S, D], F32, kind="ExternalInput").ap()
    qw_d = nc.dram_tensor("qw", [H, D, DH], F32, kind="ExternalInput").ap()
    kw_d = nc.dram_tensor("kw", [H, D, DH], F32, kind="ExternalInput").ap()
    vw_d = nc.dram_tensor("vw", [H, D, DH], F32, kind="ExternalInput").ap()
    qb_d = nc.dram_tensor("qb", [H, DH], F32, kind="ExternalInput").ap()
    kb_d = nc.dram_tensor("kb", [H, DH], F32, kind="ExternalInput").ap()
    vb_d = nc.dram_tensor("vb", [H, DH], F32, kind="ExternalInput").ap()
    ow_d = nc.dram_tensor("ow", [H, DH, D], F32, kind="ExternalInput").ap()
    ob_d = nc.dram_tensor("ob", [1, D], F32, kind="ExternalInput").ap()
    wg_d = nc.dram_tensor("wg", [DI, D], F32, kind="ExternalInput").ap()
    wu_d = nc.dram_tensor("wu", [DI, D], F32, kind="ExternalInput").ap()
    wd_d = nc.dram_tensor("wd", [D, DI], F32, kind="ExternalInput").ap()
    y_d = nc.dram_tensor("y", [B_PER_CORE, S, D], F32, kind="ExternalOutput").ap()
    wkind = "ExternalOutput" if DEBUG_TAPS else "Internal"
    wgt_d = nc.dram_tensor("wgt", [128, ND, DI], BF16, kind=wkind).ap()
    wut_d = nc.dram_tensor("wut", [128, ND, DI], BF16, kind=wkind).ap()
    wdt_d = nc.dram_tensor("wdt", [128, NI, D], BF16, kind=wkind).ap()
    dbg = {}
    if DEBUG_TAPS:
        for nm, shp, dt2 in (("xnT", [128, ND, S], BF16),
                             ("qt", [128, 8, S], BF16),
                             ("kt", [128, 8, S], BF16),
                             ("vp", [128, NT, H * 65], BF16),
                             ("ao", [128, NT, D], BF16),
                             ("pt", [128, NT, QC], BF16),
                             ("dst", [128, H * NT], F32),
                             ("drc", [128, H * NT], F32),
                             ("rstd", [128, NT], F32),
                             ("nmr", [128, NT], F32),
                             ("xqT", [128, ND, S], BF16),
                             ("hqT", [128, NI, 512], BF16)):
            dbg[nm] = nc.dram_tensor("dbg_" + nm, shp, dt2,
                                     kind="ExternalOutput").ap()

    prog = {e: [] for e in ENGINES}

    def on(eng, fn):
        prog[eng].append(fn)

    QW = 1024

    with ExitStack() as st:
        def sb(name, shape, dtype, **kw):
            return st.enter_context(nc.sbuf_tensor(name, shape, dtype, **kw))

        neg20 = sb("neg20", [128, 1], F32)
        masks = sb("masks", [128, 4, QC], BF16)
        ob_t = sb("ob_t", [128, D], F32)
        vb_t = sb("vb_t", [128, D], F32)
        qb_sb = sb("qb_sb", [128, 8], F32)
        kb_sb = sb("kb_sb", [128, 8], F32)
        st_rstd = sb("st_rstd", [128, NT], F32)
        st_nmr = sb("st_nmr", [128, NT], F32)
        ident = sb("ident", [128, 128], BF16)
        qs = {}

        def alloc_qs(scope, tag):
            qs.clear()
            for k in ("A", "B", "C", "D"):
                qs[k] = scope.enter_context(
                    nc.sbuf_tensor(f"qs_{k}_{tag}", [128, QW], F32))
            qs['E'] = scope.enter_context(
                nc.sbuf_tensor(f"qs_E_{tag}", [128, QW], I32))
            for k in ("bm", "scale", "lns", "rsh", "sc2"):
                qs[k] = scope.enter_context(
                    nc.sbuf_tensor(f"qs_{k}_{tag}", [128, QW // 16], F32))

        sv = Ctr(st, nc, "sv")
        ss = Ctr(st, nc, "ss")
        stn = Ctr(st, nc, "stn")
        sg = Ctr(st, nc, "sg")
        dxb = Lanes(st, nc, "dx", 8)
        dtpb = Lanes(st, nc, "dtp", 8)
        dcb = Lanes(st, nc, "dc", 8)
        bar = Ctr(st, nc, "bar")
        nbar = [0]

        def barrier():
            nbar[0] += 1
            k = nbar[0]
            for e in ENGINES:
                on(e, lambda eng, k=k: (eng.drain().then_inc(bar.h, 1),
                                        eng.wait_ge(bar.h, 5 * k)))

        def wait(eng, ctr, val):
            if val > 0:
                on(eng, lambda E, c=ctr, vv=val: E.wait_ge(c.h, vv))

        def wmark(eng, mark):
            if mark:
                c, vv = mark
                on(eng, lambda E, c=c, vv=vv: E.wait_ge(c.h, vv))

        def emit_quant(x_ap, out_ap, F):
            nb = F // 16
            xv = x_ap.rearrange("p (b w) -> p b w", w=16)
            A, Bb, C, Dt, E = (qs['A'][:, 0:F], qs['B'][:, 0:F],
                               qs['C'][:, 0:F], qs['D'][:, 0:F],
                               qs['E'][:, 0:F])
            bm, scale, lns, rsh, sc2 = (qs[k][:, 0:nb] for k in
                                        ("bm", "scale", "lns", "rsh", "sc2"))
            v1 = sv.bump()
            on('vector', lambda v, xv=xv, bm=bm, v1=v1: (
                v.tensor_reduce(out=bm, in_=xv, axis=AX.X, op=ALU.max,
                                apply_absolute_value=True).then_inc(sv.h, 1),
                v.wait_ge(sv.h, v1)))
            v2 = sv.bump()
            on('vector', lambda v, bm=bm, scale=scale: (
                v.tensor_scalar(
                    out=scale, in0=bm, scalar1=1.0 / 6.0, scalar2=1e-12,
                    op0=ALU.mult, op1=ALU.max),
                v.drain().then_inc(sv.h, 1)))
            s1 = ss.bump()
            on('scalar', lambda s, scale=scale, lns=lns, rsh=rsh, sc2=sc2,
               v2=v2: (
                s.wait_ge(sv.h, v2),
                s.activation(out=lns, in_=scale, func=AF.Ln, bias=0.0,
                             scale=2.0),
                s.activation(out=rsh, in_=lns, func=AF.Exp, bias=0.0,
                             scale=-1.0),
                s.mul(sc2, scale, 0.5),
                s.drain().then_inc(ss.h, 1)))
            lns2 = qs['lns'][:, 0:nb]
            scale2 = qs['scale'][:, 0:nb]
            nr1 = sv.bump(3)
            vend = sv.bump()

            bm2 = qs['bm'][:, 0:nb]

            def main(v, xv=xv, A=A, Bb=Bb, C=C, Dt=Dt, E=E, rsh=rsh,
                     sc2=sc2, out_ap=out_ap, nb=nb, s1=s1, lns2=lns2,
                     scale2=scale2, nr1=nr1, bm2=bm2):
                v.wait_ge(ss.h, s1)
                v.tensor_tensor(out=lns2, in0=scale2, in1=rsh,
                                op=ALU.mult).then_inc(sv.h, 1)
                v.wait_ge(sv.h, nr1 - 2)
                v.tensor_scalar(out=bm2, in0=lns2, scalar1=-2.0, scalar2=2.0,
                                op0=ALU.mult, op1=ALU.add).then_inc(sv.h, 1)
                v.wait_ge(sv.h, nr1 - 1)
                v.tensor_tensor(out=lns2, in0=rsh, in1=bm2,
                                op=ALU.mult).then_inc(sv.h, 1)
                v.wait_ge(sv.h, nr1)
                rsh_b = lns2.unsqueeze(-1).broadcast_to([128, nb, 16])
                sc2_b = sc2.unsqueeze(-1).broadcast_to([128, nb, 16])
                v.tensor_tensor(out=A.rearrange("p (b w) -> p b w", w=16),
                                in0=xv, in1=rsh_b, op=ALU.mult)
                v.tensor_scalar(out=Bb, in0=A, scalar1=1.0, scalar2=-1.0,
                                op0=ALU.min, op1=ALU.max)
                v.tensor_scalar(out=C, in0=A, scalar1=2.0, scalar2=-2.0,
                                op0=ALU.min, op1=ALU.max)
                v.scalar_tensor_tensor(out=Dt, in0=Bb, scalar=2.0, in1=A,
                                       op0=ALU.mult, op1=ALU.add)
                v.tensor_tensor(out=Bb, in0=Dt, in1=C, op=ALU.add)
                v.tensor_scalar(out=E, in0=Bb, scalar1=8.0, scalar2=0.0,
                                op0=ALU.add, op1=ALU.add)
                v.scalar_tensor_tensor(out=Dt, in0=E, scalar=-7.5, in1=Bb,
                                       op0=ALU.add, op1=ALU.is_lt)
                v.tensor_tensor(out=C, in0=Dt, in1=E, op=ALU.add)
                v.tensor_scalar(out=A, in0=C, scalar1=12.0, scalar2=4.0,
                                op0=ALU.min, op1=ALU.max)
                v.tensor_scalar(out=Dt, in0=C, scalar1=14.0, scalar2=2.0,
                                op0=ALU.min, op1=ALU.max)
                v.scalar_tensor_tensor(out=Bb, in0=C, scalar=2.0, in1=Dt,
                                       op0=ALU.mult, op1=ALU.subtract)
                v.scalar_tensor_tensor(out=C, in0=Bb, scalar=2.0, in1=A,
                                       op0=ALU.mult, op1=ALU.subtract)
                v.scalar_tensor_tensor(
                    out=out_ap.rearrange("p (b w) -> p b w", w=16),
                    in0=C.rearrange("p (b w) -> p b w", w=16), scalar=-8.0,
                    in1=sc2_b, op0=ALU.add, op1=ALU.mult)
                v.drain().then_inc(sv.h, 1)

            on('vector', main)
            return vend

        with ExitStack() as st0:
            iota_t = st0.enter_context(nc.sbuf_tensor("iota_t", [128, QC], F32))
            g1 = sg.bump()
            on('gpsimd', lambda g: (
                g.memset(neg20[:], EXPB),
                g.iota(iota_t[:], [[1, QC]], base=0, channel_multiplier=-1,
                       allow_small_or_imprecise_dtypes=True)
                .then_inc(sg.h, 1)))
            su_l = [dxb.next() for _ in range(4)]
            on('sync', lambda sync, su_l=su_l: (
                sync.dma_start(out=ob_t[:], in_=ob_d.broadcast_to([128, D]))
                .then_inc(su_l[0].h, 16),
                sync.dma_start(out=vb_t[:],
                               in_=vb_d.rearrange("h d -> (h d)").unsqueeze(0)
                               .broadcast_to([128, D])).then_inc(su_l[1].h, 16),
                sync.dma_start(out=qb_sb[:],
                               in_=bass.AP(qb_d.tensor, 0,
                                           [[1, 128], [128, 8]]),
                               allow_slow_non_contiguous=True)
                .then_inc(su_l[2].h, 16),
                sync.dma_start(out=kb_sb[:],
                               in_=bass.AP(kb_d.tensor, 0,
                                           [[1, 128], [128, 8]]),
                               allow_slow_non_contiguous=True)
                .then_inc(su_l[3].h, 16)))
            setup_marks = [(ln, ln.bump(16)) for ln in su_l]
            wait('vector', sg, g1)
            sv.bump()
            on('vector', lambda v: v.tensor_scalar(
                out=ident[:], in0=iota_t[:, 0:128], scalar1=0.0, scalar2=0.0,
                op0=ALU.is_equal, op1=ALU.add).then_inc(sv.h, 1))
            for m in range(4):
                sv.bump()
                on('vector', lambda v, m=m: v.tensor_scalar(
                    out=masks[:, m, :], in0=iota_t[:], scalar1=float(128 * m),
                    scalar2=0.0, op0=ALU.is_ge, op1=ALU.add)
                   .then_inc(sv.h, 1))
            masks_done = sv.v

            alloc_qs(st0, "w")
            w_in = [st0.enter_context(nc.sbuf_tensor(f"w_in{i}", [128, DI],
                                                     F32)) for i in range(2)]
            w_bf = [st0.enter_context(nc.sbuf_tensor(f"w_bf{i}", [128, DI],
                                                     BF16)) for i in range(2)]
            w_tp = [st0.enter_context(nc.sbuf_tensor(
                f"w_tp{i}", [128, NI, 128], BF16, is_transpose_dest=True))
                for i in range(2)]

            tiles = []
            for (src, rows, cols, dstT) in ((wg_d, DI, D, wgt_d),
                                            (wu_d, DI, D, wut_d),
                                            (wd_d, D, DI, wdt_d)):
                for t in range(rows // 128):
                    tiles.append((src, cols, dstT, t))

            vq_done, tp_done, stw_done, ld_done = [], [], [], []
            for k, (src, cols, dstT, t) in enumerate(tiles):
                nsub = cols // 128
                if k >= 2:
                    wait('sync', sv, vq_done[k - 2])
                lnl = dxb.next()
                on('sync', lambda sync, src=src, t=t, cols=cols, k=k, lnl=lnl:
                   sync.dma_start(out=w_in[k % 2][:, 0:cols],
                                  in_=src[t * 128:(t + 1) * 128, :])
                   .then_inc(lnl.h, 16))
                ld_done.append((lnl, lnl.bump(16)))
                wmark('vector', ld_done[k])
                if k >= 2:
                    wmark('vector', tp_done[k - 2])
                for piece in range(0, cols, QW):
                    pw = min(QW, cols - piece)
                    emit_quant(w_in[k % 2][:, piece:piece + pw],
                               w_bf[k % 2][:, piece:piece + pw], pw)
                vq_done.append(sv.v)
                wait('scalar', sv, vq_done[k])
                if k >= 2:
                    wmark('scalar', stw_done[k - 2])
                lnt = dtpb.next()
                on('scalar', lambda s, k=k, cols=cols, nsub=nsub, lnt=lnt:
                   s.dma_start(out=w_tp[k % 2][:, 0:nsub, :],
                               in_=w_bf[k % 2][:, 0:cols], transpose=True)
                   .then_inc(lnt.h, 16))
                tp_done.append((lnt, lnt.bump(16)))
                wmark('sync', tp_done[k])
                lns_ = dxb.next()
                on('sync', lambda sync, dstT=dstT, t=t, nsub=nsub, k=k,
                   lns_=lns_:
                   sync.dma_start(out=dstT[:, 0:nsub, t * 128:(t + 1) * 128],
                                  in_=w_tp[k % 2][:, 0:nsub, :])
                   .then_inc(lns_.h, 16))
                stw_done.append((lns_, lns_.bump(16)))

            for (ln, vv) in dxb.totals() + dtpb.totals():
                wait('sync', ln, vv)
            for m in setup_marks:
                wmark('sync', m)
            wait('vector', sv, masks_done)
            barrier()

        xq_tp_hist = []
        out_hist = []

        def plan_batch(b):
            with ExitStack() as stb:
                attn_out = stb.enter_context(nc.sbuf_tensor(
                    f"attn_out_{b}", [128, NT, D], BF16,
                    is_transpose_dest=True))
                xnT = attn_out[:].rearrange("p a c -> p (a c)") \
                    .rearrange("p (u w) -> p u w", u=ND)
                attnv = attn_out[:]

                with ExitStack() as sta:
                    qt_all = sta.enter_context(nc.sbuf_tensor(
                        f"qt_all_{b}", [128, 8, S], BF16))
                    kt_all = sta.enter_context(nc.sbuf_tensor(
                        f"kt_all_{b}", [128, 8, S], BF16))
                    v_perm = sta.enter_context(nc.sbuf_tensor(
                        f"v_perm_{b}", [128, NT, H * 65], BF16))
                    pt = sta.enter_context(nc.sbuf_tensor(
                        f"pt_{b}", [128, NT, QC], BF16))

                    with ExitStack() as stp:
                        x_t = [stp.enter_context(nc.sbuf_tensor(
                            f"x_t{i}_{b}", [128, D], F32)) for i in range(2)]
                        xsq = stp.enter_context(nc.sbuf_tensor(
                            f"xsq_{b}", [128, D], F32))
                        xn_bf = [stp.enter_context(nc.sbuf_tensor(
                            f"xn_bf{i}_{b}", [128, D], BF16))
                            for i in range(2)]
                        sm1 = stp.enter_context(nc.sbuf_tensor(
                            f"sm1_{b}", [128, 1], F32))
                        sm2 = stp.enter_context(nc.sbuf_tensor(
                            f"sm2_{b}", [128, 1], F32))
                        msq = stp.enter_context(nc.sbuf_tensor(
                            f"msq_{b}", [128, 1], F32))
                        var = stp.enter_context(nc.sbuf_tensor(
                            f"var_{b}", [128, 1], F32))
                        lnv = stp.enter_context(nc.sbuf_tensor(
                            f"lnv_{b}", [128, 1], F32))
                        mmean = stp.enter_context(nc.sbuf_tensor(
                            f"mmean_{b}", [128, 1], F32))
                        wbuf = [stp.enter_context(nc.sbuf_tensor(
                            f"wbuf{i}_{b}", [128, ND, 1024], BF16))
                            for i in range(2)]

                        ones_done = sv.bump()
                        on('vector', lambda v: v.memset(
                            bass.AP(v_perm, 64,
                                    [[NT * H * 65, 128], [H * 65, NT],
                                     [65, H]]), 1.0).then_inc(sv.h, 1))

                        tp_x_done = []
                        ln_s3 = []
                        for t in range(NT):
                            if t >= 2:
                                wait('sync', ss, ln_s3[t - 2])
                            lnx = dxb.next()
                            on('sync', lambda sync, t=t, lnx=lnx:
                               sync.dma_start(
                                out=x_t[t % 2][:],
                                in_=x_d[b, t * 128:(t + 1) * 128, :])
                               .then_inc(lnx.h, 16))
                            xldt = (lnx, lnx.bump(16))
                            v1 = sv.bump()
                            on('vector', lambda v, t=t, v1=v1, xldt=xldt: (
                                v.wait_ge(xldt[0].h, xldt[1]),
                                v.tensor_reduce(out=sm1[:], in_=x_t[t % 2][:],
                                                axis=AX.X, op=ALU.add)
                                .then_inc(sv.h, 1),
                                v.wait_ge(sv.h, v1)))
                            s1 = ss.bump()
                            on('scalar', lambda s, t=t, xldt=xldt: (
                                s.wait_ge(xldt[0].h, xldt[1]),
                                s.square(xsq[:], x_t[t % 2][:]),
                                s.drain().then_inc(ss.h, 1)))
                            v2 = sv.bump()
                            on('vector', lambda v, s1=s1, v2=v2: (
                                v.wait_ge(ss.h, s1),
                                v.tensor_reduce(out=sm2[:], in_=xsq[:],
                                                axis=AX.X, op=ALU.add)
                                .then_inc(sv.h, 1),
                                v.wait_ge(sv.h, v2)))
                            v3 = sv.bump(4)
                            on('vector', lambda v, v3=v3: (
                                v.tensor_scalar(out=mmean[:], in0=sm1[:],
                                                scalar1=1.0 / D, scalar2=0.0,
                                                op0=ALU.mult, op1=ALU.add)
                                .then_inc(sv.h, 1),
                                v.wait_ge(sv.h, v3 - 3),
                                v.tensor_tensor(out=msq[:], in0=mmean[:],
                                                in1=sm1[:], op=ALU.mult)
                                .then_inc(sv.h, 1),
                                v.wait_ge(sv.h, v3 - 2),
                                v.tensor_tensor(out=var[:], in0=sm2[:],
                                                in1=msq[:], op=ALU.subtract)
                                .then_inc(sv.h, 1),
                                v.wait_ge(sv.h, v3 - 1),
                                v.tensor_scalar(out=var[:], in0=var[:],
                                                scalar1=1.0 / D, scalar2=1e-5,
                                                op0=ALU.mult, op1=ALU.add)
                                .then_inc(sv.h, 1),
                                v.wait_ge(sv.h, v3)))
                            s2 = ss.bump()
                            on('scalar', lambda s, t=t, v3=v3: (
                                s.wait_ge(sv.h, v3),
                                s.activation(out=lnv[:], in_=var[:],
                                             func=AF.Ln, bias=0.0, scale=1.0),
                                s.activation(out=st_rstd[:, t:t + 1],
                                             in_=lnv[:], func=AF.Exp,
                                             bias=0.0, scale=-0.5),
                                s.drain().then_inc(ss.h, 1)))
                            v5 = sv.bump()
                            on('vector', lambda v, t=t, s2=s2: (
                                v.wait_ge(ss.h, s2),
                                v.scalar_tensor_tensor(
                                    out=st_nmr[:, t:t + 1], in0=mmean[:],
                                    scalar=-1.0, in1=st_rstd[:, t:t + 1],
                                    op0=ALU.mult, op1=ALU.mult),
                                v.drain().then_inc(sv.h, 1)))
                            s3 = ss.bump()
                            on('scalar', lambda s, t=t, v5=v5,
                               tpm=(tp_x_done[t - 2] if t >= 2 else None): (
                                s.wait_ge(sv.h, v5),
                                (s.wait_ge(tpm[0].h, tpm[1])
                                 if tpm else None),
                                s.activation(out=xn_bf[t % 2][:],
                                             in_=x_t[t % 2][:], func=AF.Identity,
                                             bias=st_nmr[:, t:t + 1],
                                             scale=st_rstd[:, t:t + 1]),
                                s.drain().then_inc(ss.h, 1)))
                            ln_s3.append(s3)
                            lnxt = dtpb.next()
                            on('scalar', lambda s, t=t, s3=s3, lnxt=lnxt: (
                                s.wait_ge(ss.h, s3),
                                s.dma_start(
                                    out=xnT[:, :, t * 128:(t + 1) * 128],
                                    in_=xn_bf[t % 2][:], transpose=True)
                                .then_inc(lnxt.h, 16)))
                            tp_x_done.append((lnxt, lnxt.bump(16)))

                        ps_qkv = stp.enter_context(nc.psum_tensor(
                            f"ps_qkv_{b}", [128, 2, 512], F32))

                        def load_qk(j, wdram):
                            marks = {}
                            for hp in range(8):
                                for hf in range(2):
                                    lnw = dcb.next()
                                    on('gpsimd', lambda g, wdram=wdram,
                                       wsb=wbuf[j % 2], hp=hp, hf=hf, lnw=lnw:
                                       g.dma_start(
                                        out=bass.AP(wsb, hp * 128 + hf * 64,
                                                    [[ND * 1024, 128],
                                                     [1024, ND], [1, 64]]),
                                        in_=bass.AP(wdram.tensor,
                                                    (2 * hp + hf) * D * DH,
                                                    [[64, 128],
                                                     [128 * 64, ND],
                                                     [1, 64]]))
                                       .then_inc(lnw.h, 16))
                                    marks[id(lnw)] = (lnw, lnw.bump(16))
                            return list(marks.values())

                        wl_q = load_qk(0, qw_d)
                        wl_k = load_qk(1, kw_d)

                        wmark('tensor', tp_x_done[-1])
                        slot = 0
                        ev_done = []
                        wuse = {}
                        for j, (bsb, dst) in enumerate(((qb_sb, qt_all),
                                                       (kb_sb, kt_all))):
                            for m in (wl_q if j == 0 else wl_k):
                                wmark('tensor', m)
                            for hp in range(8):
                                for c in range(4):
                                    sl = slot % 2
                                    if slot >= 2:
                                        e, m = ev_done[slot - 2]
                                        wait('tensor', ss if e == 'ss'
                                             else sv, m)
                                    tv = stn.bump()
                                    on('tensor', lambda tn, j=j, hp=hp, c=c,
                                       sl=sl: [
                                        tn.matmul(
                                            ps_qkv[:, sl, :],
                                            wbuf[j % 2][:, d,
                                                        hp * 128:(hp + 1) * 128],
                                            xnT[:, d, c * 512:(c + 1) * 512],
                                            start=(d == 0),
                                            stop=(d == ND - 1))
                                        for d in range(ND)][-1]
                                       .then_inc(stn.h, 1))
                                    se = ss.bump()
                                    on('scalar', lambda s, bsb=bsb, dst=dst,
                                       hp=hp, c=c, sl=sl, tv=tv: (
                                        s.wait_ge(stn.h, tv),
                                        s.activation(
                                            out=dst[:, hp,
                                                    c * 512:(c + 1) * 512],
                                            in_=ps_qkv[:, sl, :],
                                            func=AF.Identity,
                                            bias=bsb[:, hp:hp + 1],
                                            scale=1.0).then_inc(ss.h, 1)))
                                    ev_done.append(('ss', se))
                                    slot += 1
                            wuse[j] = stn.v
                        wait('gpsimd', stn, wuse[0])
                        vmarks = {}
                        for h2 in range(H):
                            lnv2 = dcb.next()
                            on('gpsimd', lambda g, h2=h2, lnv2=lnv2:
                               g.dma_start(
                                out=bass.AP(wbuf[0], h2 * 64,
                                            [[ND * 1024, 128], [1024, ND],
                                             [1, 64]]),
                                in_=bass.AP(vw_d.tensor, h2 * D * DH,
                                            [[64, 128], [128 * 64, ND],
                                             [1, 64]]))
                               .then_inc(lnv2.h, 16))
                            vmarks[id(lnv2)] = (lnv2, lnv2.bump(16))
                        for m in vmarks.values():
                            wmark('tensor', m)
                        wait('vector', sv, ones_done)
                        for t in range(NT):
                            for c in range(2):
                                sl = slot % 2
                                if slot >= 2:
                                    e, m = ev_done[slot - 2]
                                    wait('tensor', ss if e == 'ss' else sv, m)
                                tv = stn.bump()
                                on('tensor', lambda tn, t=t, c=c, sl=sl: [
                                    tn.matmul(
                                        ps_qkv[:, sl, :],
                                        xnT[:, d, t * 128:(t + 1) * 128],
                                        wbuf[0][:, d, c * 512:(c + 1) * 512],
                                        start=(d == 0), stop=(d == ND - 1))
                                    for d in range(ND)][-1]
                                   .then_inc(stn.h, 1))
                                vvl = None
                                for hh in range(8):
                                    h = c * 8 + hh
                                    vvl = sv.bump()
                                    on('vector', lambda v, t=t, sl=sl, hh=hh,
                                       h=h, tv=tv: (
                                        (v.wait_ge(stn.h, tv)
                                         if hh == 0 else None),
                                        v.scalar_tensor_tensor(
                                            out=v_perm[:, t,
                                                       h * 65:h * 65 + 64],
                                            in0=ps_qkv[:, sl,
                                                       hh * 64:(hh + 1) * 64],
                                            scalar=0.0,
                                            in1=vb_t[:, h * 64:(h + 1) * 64],
                                            op0=ALU.add, op1=ALU.add)
                                        .then_inc(sv.h, 1)))
                                ev_done.append(('sv', vvl))
                                slot += 1
                        for (e, m) in ev_done[-2:]:
                            wait('tensor', ss if e == 'ss' else sv, m)
                        barrier()
                        if DEBUG_TAPS and b == DEBUG_TAPS_BATCH:
                            dbl = [dxb.next() for _ in range(6)]
                            on('sync', lambda sync, dbl=dbl: (
                                sync.dma_start(out=dbg["xnT"],
                                               in_=attn_out[:].rearrange(
                                                   "p a c -> p (a c)")
                                               .rearrange("p (u w) -> p u w",
                                                          u=ND))
                                .then_inc(dbl[0].h, 16),
                                sync.dma_start(out=dbg["qt"], in_=qt_all[:])
                                .then_inc(dbl[1].h, 16),
                                sync.dma_start(out=dbg["kt"], in_=kt_all[:])
                                .then_inc(dbl[2].h, 16),
                                sync.dma_start(out=dbg["vp"], in_=v_perm[:])
                                .then_inc(dbl[3].h, 16),
                                sync.dma_start(out=dbg["rstd"], in_=st_rstd[:])
                                .then_inc(dbl[4].h, 16),
                                sync.dma_start(out=dbg["nmr"], in_=st_nmr[:])
                                .then_inc(dbl[5].h, 16)))
                            for ln2 in dbl:
                                wmark('sync', (ln2, ln2.bump(16)))
                            barrier()

                    with ExitStack() as sta3:
                        esc = sta3.enter_context(nc.sbuf_tensor(
                            f"esc_{b}", [128, QC], BF16))
                        dstage = sta3.enter_context(nc.sbuf_tensor(
                            f"dstage_{b}", [128, H * NT], F32))
                        drec = sta3.enter_context(nc.sbuf_tensor(
                            f"drec_{b}", [128, H * NT], F32))
                        ow_sb = sta3.enter_context(nc.sbuf_tensor(
                            f"ow_sb_{b}", [128, 8, D], BF16))
                        anorm = [sta3.enter_context(nc.sbuf_tensor(
                            f"anorm{i}_{b}", [128, D], BF16))
                            for i in range(2)]
                        avT = [sta3.enter_context(nc.sbuf_tensor(
                            f"avT{i}_{b}", [128, 8, 128], BF16,
                            is_transpose_dest=True)) for i in range(2)]
                        ps_l = sta3.enter_context(nc.psum_tensor(
                            f"ps_l_{b}", [128, 2, 512], F32))
                        ps_pv = sta3.enter_context(nc.psum_tensor(
                            f"ps_pv_{b}", [128, 2, 512], F32))
                        ps_o = sta3.enter_context(nc.psum_tensor(
                            f"ps_o_{b}", [128, 2, 512], F32))

                        ow_marks = []
                        for half in range(2):
                            lno = dcb.next()
                            on('gpsimd', lambda g, half=half, lno=lno:
                               g.dma_start(
                                out=ow_sb[half * 64:(half + 1) * 64, :, :],
                                in_=bass.AP(ow_d.tensor, half * DH * D,
                                            [[D, 64], [2 * DH * D, 8],
                                             [1, D]])).then_inc(lno.h, 16))
                            ow_marks.append((lno, lno.bump(16)))

                        lslot, pvslot = 0, 0
                        l_cons, pv_cons = [], []
                        last_pv_tv = 0
                        for h in range(H):
                            hp, po = h // 2, (h % 2) * 64
                            for c in range(NQC):
                                nkt = 4 * (c + 1)
                                exp_marks = []
                                if last_pv_tv:
                                    wait('scalar', stn, last_pv_tv)
                                    wait('vector', stn, last_pv_tv)
                                for kt in range(nkt):
                                    slq = lslot % 2
                                    if lslot >= 2:
                                        wait('tensor', ss, l_cons[lslot - 2])
                                    tv = stn.bump()
                                    on('tensor', lambda tn, hp=hp, po=po,
                                       kt=kt, c=c, slq=slq: tn.matmul(
                                        ps_l[:, slq, :],
                                        kt_all[po:po + 64, hp,
                                               kt * 128:(kt + 1) * 128],
                                        qt_all[po:po + 64, hp,
                                               c * QC:(c + 1) * QC],
                                        start=True, stop=True)
                                       .then_inc(stn.h, 1))
                                    diag = (kt >= nkt - 4)
                                    se = ss.bump()
                                    if not diag:
                                        on('scalar', lambda s, kt=kt, slq=slq,
                                           tv=tv: (
                                            s.wait_ge(stn.h, tv),
                                            s.activation(
                                                out=pt[:, kt, :],
                                                in_=ps_l[:, slq, :],
                                                func=AF.Exp, bias=neg20[:],
                                                scale=0.125)
                                            .then_inc(ss.h, 1)))
                                        exp_marks.append(('s', se))
                                    else:
                                        on('scalar', lambda s, slq=slq,
                                           tv=tv: (
                                            s.wait_ge(stn.h, tv),
                                            s.activation(
                                                out=esc[:],
                                                in_=ps_l[:, slq, :],
                                                func=AF.Exp, bias=neg20[:],
                                                scale=0.125)
                                            .then_inc(ss.h, 1)))
                                        mv = sv.bump()
                                        mi = kt - (nkt - 4)
                                        on('vector', lambda v, kt=kt, mi=mi,
                                           se=se: (
                                            v.wait_ge(ss.h, se),
                                            v.tensor_tensor(
                                                out=pt[:, kt, :],
                                                in0=esc[:],
                                                in1=masks[:, mi, :],
                                                op=ALU.mult)
                                            .then_inc(sv.h, 1)))
                                        exp_marks.append(('v', mv))
                                        wait('scalar', sv, mv)
                                    l_cons.append(se)
                                    lslot += 1
                                for qt4 in range(4):
                                    qt = c * 4 + qt4
                                    slp = pvslot % 2
                                    if pvslot >= 2:
                                        wait('tensor', sv,
                                             pv_cons[pvslot - 2])
                                    need = exp_marks[:qt + 1]
                                    svn = max([m for k2, m in need
                                               if k2 == 'v'], default=0)
                                    ssn = max([m for k2, m in need
                                               if k2 == 's'], default=0)
                                    if ssn:
                                        wait('tensor', ss, ssn)
                                    if svn:
                                        wait('tensor', sv, svn)
                                    tv = stn.bump()
                                    on('tensor', lambda tn, h=h, qt=qt,
                                       qt4=qt4, slp=slp, nk=qt + 1: [
                                        tn.matmul(
                                            ps_pv[:, slp, 0:65],
                                            pt[:, kt2,
                                               qt4 * 128:(qt4 + 1) * 128],
                                            v_perm[:, kt2,
                                                   h * 65:h * 65 + 65],
                                            start=(kt2 == 0),
                                            stop=(kt2 == nk - 1))
                                        for kt2 in range(nk)][-1]
                                       .then_inc(stn.h, 1))
                                    vv1 = sv.bump(2)
                                    on('vector', lambda v, h=h, qt=qt,
                                       slp=slp, tv=tv: (
                                        v.wait_ge(stn.h, tv),
                                        v.tensor_copy(
                                            out=attnv[:, qt,
                                                      h * 64:(h + 1) * 64],
                                            in_=ps_pv[:, slp, 0:64]),
                                        v.tensor_copy(
                                            out=dstage[:, h * NT + qt:
                                                       h * NT + qt + 1],
                                            in_=ps_pv[:, slp, 64:65])
                                        .then_inc(sv.h, 2)))
                                    pv_cons.append(vv1)
                                    pvslot += 1
                                    last_pv_tv = stn.v
                        vdr = sv.bump()
                        on('vector', lambda v: v.drain().then_inc(sv.h, 1))
                        sdr = ss.bump(2)
                        on('scalar', lambda s, vv=sv.v, sdr=sdr: (
                            s.wait_ge(sv.h, vv),
                            s.activation(out=drec[:], in_=dstage[:],
                                         func=AF.Ln, bias=0.0, scale=1.0)
                            .then_inc(ss.h, 1),
                            s.wait_ge(ss.h, sdr - 1),
                            s.activation(out=drec[:], in_=drec[:],
                                         func=AF.Exp, bias=0.0, scale=-1.0)
                            .then_inc(ss.h, 1),
                            s.wait_ge(ss.h, sdr)))

                        for m in ow_marks:
                            wmark('tensor', m)
                        tpo_done, oev_done, oproj_tv = [], [], []
                        oslot = 0
                        for qt in range(NT):
                            sm = None
                            for h in range(H):
                                sm = ss.bump()
                                on('scalar', lambda s, qt=qt, h=h, sm=sm,
                                   sdr=sdr,
                                   tpm=(tpo_done[qt - 2]
                                        if (h == 0 and qt >= 2) else None): (
                                    (s.wait_ge(ss.h, sdr)
                                     if (h == 0 and qt == 0) else None),
                                    (s.wait_ge(tpm[0].h, tpm[1])
                                     if tpm else None),
                                    s.activation(
                                        out=anorm[qt % 2][:,
                                                          h * 64:(h + 1) * 64],
                                        in_=attnv[:, qt, h * 64:(h + 1) * 64],
                                        func=AF.Identity, bias=0.0,
                                        scale=drec[:, h * NT + qt:
                                                   h * NT + qt + 1]),
                                    (s.drain() if h == H - 1 else s.nop())
                                    .then_inc(ss.h, 1)))
                            if qt >= 2:
                                wait('scalar', stn, oproj_tv[qt - 2])
                            lna = dtpb.next()
                            on('scalar', lambda s, qt=qt, sm=sm, lna=lna: (
                                s.wait_ge(ss.h, sm),
                                s.dma_start(out=avT[qt % 2][:],
                                            in_=anorm[qt % 2][:],
                                            transpose=True)
                                .then_inc(lna.h, 16)))
                            tpo_done.append((lna, lna.bump(16)))
                            for ch in range(2):
                                slo = oslot % 2
                                if oslot >= 2:
                                    wait('tensor', sv, oev_done[oslot - 2])
                                if ch == 0:
                                    wmark('tensor', tpo_done[qt])
                                tv = stn.bump()
                                on('tensor', lambda tn, qt=qt, ch=ch,
                                   slo=slo: [
                                    tn.matmul(
                                        ps_o[:, slo, :],
                                        avT[qt % 2][:, hp2, :],
                                        ow_sb[:, hp2, ch * 512:(ch + 1) * 512],
                                        start=(hp2 == 0), stop=(hp2 == 7))
                                    for hp2 in range(8)][-1]
                                   .then_inc(stn.h, 1))
                                vv = sv.bump()
                                on('vector', lambda v, qt=qt, ch=ch, slo=slo,
                                   tv=tv: (
                                    v.wait_ge(stn.h, tv),
                                    v.scalar_tensor_tensor(
                                        out=attn_out[:, qt,
                                                     ch * 512:(ch + 1) * 512],
                                        in0=ps_o[:, slo, :], scalar=0.0,
                                        in1=ob_t[:, ch * 512:(ch + 1) * 512],
                                        op0=ALU.add, op1=ALU.add)
                                    .then_inc(sv.h, 1)))
                                oev_done.append(vv)
                                oslot += 1
                            oproj_tv.append(stn.v)
                        wait('tensor', sv, oev_done[-1])
                        wait('tensor', sv, oev_done[-2])
                        if DEBUG_TAPS and b == DEBUG_TAPS_BATCH:
                            wait('sync', sv, sv.v)
                            dbl2 = [dxb.next() for _ in range(4)]
                            on('sync', lambda sync, dbl2=dbl2: (
                                sync.dma_start(out=dbg["ao"], in_=attn_out[:])
                                .then_inc(dbl2[0].h, 16),
                                sync.dma_start(out=dbg["pt"], in_=pt[:])
                                .then_inc(dbl2[1].h, 16),
                                sync.dma_start(out=dbg["dst"], in_=dstage[:])
                                .then_inc(dbl2[2].h, 16),
                                sync.dma_start(out=dbg["drc"], in_=drec[:])
                                .then_inc(dbl2[3].h, 16)))
                            for ln2 in dbl2:
                                wmark('sync', (ln2, ln2.bump(16)))
                    barrier()

                with ExitStack() as stf:
                    alloc_qs(stf, f"b{b}")
                    xqT = stf.enter_context(nc.sbuf_tensor(
                        f"xqT_{b}", [128, ND, S], BF16, is_transpose_dest=True))
                    xg = stf.enter_context(nc.sbuf_tensor(
                        f"xg_{b}", [128, 4, D], F32))
                    xnf = [stf.enter_context(nc.sbuf_tensor(
                        f"xnf{i}_{b}", [128, D], F32)) for i in range(2)]
                    xq_bf = [stf.enter_context(nc.sbuf_tensor(
                        f"xq_bf{i}_{b}", [128, D], BF16)) for i in range(2)]
                    hterm = [stf.enter_context(nc.sbuf_tensor(
                        f"hterm{i}_{b}", [128, 256], F32)) for i in range(2)]
                    silu_sb = [stf.enter_context(nc.sbuf_tensor(
                        f"silu{i}_{b}", [128, 256], F32)) for i in range(2)]
                    hq4 = stf.enter_context(nc.sbuf_tensor(
                        f"hq4_{b}", [128, 4, DI], BF16))
                    hqT = stf.enter_context(nc.sbuf_tensor(
                        f"hqT_{b}", [128, NI, 512], BF16,
                        is_transpose_dest=True))
                    wgu = [stf.enter_context(nc.sbuf_tensor(
                        f"wgu{i}_{b}", [128, ND, 512], BF16))
                        for i in range(2)]
                    wdt_sb = [stf.enter_context(nc.sbuf_tensor(
                        f"wdt_sb{i}_{b}", [128, 512], BF16))
                        for i in range(2)]
                    out_sb = [stf.enter_context(nc.sbuf_tensor(
                        f"out_sb{i}_{b}", [128, 512], F32)) for i in range(2)]
                    ps_gu = stf.enter_context(nc.psum_tensor(
                        f"ps_gu_{b}", [128, 4, 512], F32))
                    ps_d = stf.enter_context(nc.psum_tensor(
                        f"ps_d_{b}", [128, 4, 512], F32))
                    ps_tp = ps_d[:, 0, :].bitcast(BF16)

                    NCH = DI // 256
                    hq_tp_hist = []
                    xnf_qv = []
                    for g4 in range(4):
                        t0 = g4 * 4
                        for tt in range(4):
                            t = t0 + tt
                            lnb = dxb.next()
                            on('sync', lambda sync, t=t, tt=tt, lnb=lnb:
                               sync.dma_start(
                                out=xg[:, tt, :],
                                in_=x_d[b, t * 128:(t + 1) * 128, :])
                               .then_inc(lnb.h, 16))
                            xldt = (lnb, lnb.bump(16))
                            se = ss.bump()
                            on('scalar', lambda s, t=t, tt=tt, xldt=xldt,
                               gq=(xnf_qv[-2] if len(xnf_qv) >= 2 else 0): (
                                s.wait_ge(xldt[0].h, xldt[1]),
                                (s.wait_ge(sv.h, gq) if gq else None),
                                s.activation(out=xnf[tt % 2][:],
                                             in_=xg[:, tt, :],
                                             func=AF.Identity,
                                             bias=st_nmr[:, t:t + 1],
                                             scale=st_rstd[:, t:t + 1]),
                                s.drain().then_inc(ss.h, 1)))
                            wait('vector', ss, se)
                            if len(xq_tp_hist) >= 2:
                                wait('vector', sv, xq_tp_hist[-2])
                            emit_quant(xnf[tt % 2][:], xq_bf[tt % 2][:], D)
                            qv = sv.v
                            xnf_qv.append(qv)
                            if xq_tp_hist:
                                wait('tensor', sv, xq_tp_hist[-1])
                            tpv = stn.bump()
                            on('tensor', lambda tn, tt=tt, qv=qv: (
                                tn.wait_ge(sv.h, qv),
                                [tn.transpose(
                                    ps_tp[:, d * 128:(d + 1) * 128],
                                    xq_bf[tt % 2][:, d * 128:(d + 1) * 128],
                                    ident[:]) for d in range(ND)][-1]
                                .then_inc(stn.h, 1)))
                            sv.bump()
                            on('vector', lambda v, t=t, tpv=tpv: (
                                v.wait_ge(stn.h, tpv),
                                v.tensor_copy(
                                    out=bass.AP(xqT, t * 128,
                                                [[ND * S, 128], [S, ND],
                                                 [1, 128]]),
                                    in_=ps_tp[:]),
                                v.drain().then_inc(sv.h, 1)))
                            xq_tp_hist.append(sv.v)
                        wait('tensor', sv, xq_tp_hist[-1])
                        if DEBUG_TAPS and b == DEBUG_TAPS_BATCH and g4 == 3:
                            wait('sync', sv, xq_tp_hist[-1])
                            lnd = dxb.next()
                            on('sync', lambda sync, lnd=lnd: sync.dma_start(
                                out=dbg["xqT"], in_=xqT[:])
                               .then_inc(lnd.h, 16))
                            lnd.bump(16)

                        wld = []
                        gu_cons = []
                        gslot = 0
                        for c in range(NCH):
                            if c >= 2:
                                wait('sync', stn, wld[c - 2])
                            lng = dxb.next()
                            lnu = dxb.next()
                            on('sync', lambda sync, c=c, lng=lng, lnu=lnu: (
                                sync.dma_start(
                                    out=wgu[c % 2][:, :, 0:256],
                                    in_=wgt_d[:, :, c * 256:(c + 1) * 256])
                                .then_inc(lng.h, 16),
                                sync.dma_start(
                                    out=wgu[c % 2][:, :, 256:512],
                                    in_=wut_d[:, :, c * 256:(c + 1) * 256])
                                .then_inc(lnu.h, 16)))
                            wldv = ((lng, lng.bump(16)), (lnu, lnu.bump(16)))
                            lastmm = None
                            for tt in range(4):
                                t = t0 + tt
                                sg0 = 2 * (gslot % 2)
                                if gslot >= 2:
                                    wait('tensor', sv, gu_cons[gslot - 2])
                                tv = stn.bump()
                                on('tensor', lambda tn, c=c, t=t, sg0=sg0,
                                   wldv=wldv: (
                                    tn.wait_ge(wldv[0][0].h, wldv[0][1]),
                                    tn.wait_ge(wldv[1][0].h, wldv[1][1]),
                                    [tn.matmul(
                                        ps_gu[:, sg0, 0:256],
                                        xqT[:, d, t * 128:(t + 1) * 128],
                                        wgu[c % 2][:, d, 0:256],
                                        start=(d == 0), stop=(d == ND - 1))
                                     for d in range(ND)],
                                    [tn.matmul(
                                        ps_gu[:, sg0 + 1, 0:256],
                                        xqT[:, d, t * 128:(t + 1) * 128],
                                        wgu[c % 2][:, d, 256:512],
                                        start=(d == 0), stop=(d == ND - 1))
                                     for d in range(ND)][-1]
                                    .then_inc(stn.h, 1)))
                                lastmm = stn.v
                                se = ss.bump()
                                on('scalar', lambda s, sg0=sg0, tv=tv,
                                   gsl=gslot % 2,
                                   gg=(gu_cons[gslot - 2]
                                       if gslot >= 2 else 0): (
                                    s.wait_ge(stn.h, tv),
                                    (s.wait_ge(sv.h, gg) if gg else None),
                                    s.activation(out=silu_sb[gsl][:],
                                                 in_=ps_gu[:, sg0, 0:256],
                                                 func=AF.Silu, bias=0.0,
                                                 scale=1.0),
                                    s.drain().then_inc(ss.h, 1)))
                                vv = sv.bump()
                                on('vector', lambda v, tt=tt, sg0=sg0,
                                   se=se, gsl=gslot % 2: (
                                    v.wait_ge(ss.h, se),
                                    v.tensor_tensor(
                                        out=hterm[tt % 2][:],
                                        in0=silu_sb[gsl][:],
                                        in1=ps_gu[:, sg0 + 1, 0:256],
                                        op=ALU.mult).then_inc(sv.h, 1)))
                                gu_cons.append(vv)
                                emit_quant(hterm[tt % 2][:],
                                           hq4[:, tt, c * 256:(c + 1) * 256],
                                           256)
                                gslot += 1
                            wld.append(lastmm)
                        hq_done = sv.v

                        for tt in range(4):
                            for i0 in (0, 8, 16):
                                ni = min(8, NI - i0)
                                if hq_tp_hist:
                                    wait('tensor', sv, hq_tp_hist[-1])
                                tpv = stn.bump()
                                on('tensor', lambda tn, tt=tt, i0=i0, ni=ni,
                                   hq_done=hq_done: (
                                    (tn.wait_ge(sv.h, hq_done)
                                     if (tt == 0 and i0 == 0) else None),
                                    [tn.transpose(
                                        ps_tp[:, k * 128:(k + 1) * 128],
                                        hq4[:, tt,
                                            (i0 + k) * 128:(i0 + k + 1) * 128],
                                        ident[:]) for k in range(ni)][-1]
                                    .then_inc(stn.h, 1)))
                                sv.bump()
                                on('vector', lambda v, tt=tt, i0=i0, ni=ni,
                                   tpv=tpv: (
                                    v.wait_ge(stn.h, tpv),
                                    v.tensor_copy(
                                        out=bass.AP(hqT,
                                                    i0 * 512 + tt * 128,
                                                    [[NI * 512, 128],
                                                     [512, ni], [1, 128]]),
                                        in_=ps_tp[:, 0:ni * 128]),
                                    v.drain().then_inc(sv.h, 1)))
                                hq_tp_hist.append(sv.v)
                        wait('tensor', sv, hq_tp_hist[-1])
                        if DEBUG_TAPS and b == DEBUG_TAPS_BATCH and g4 == 1:
                            wait('sync', sv, hq_tp_hist[-1])
                            lnd2 = dxb.next()
                            on('sync', lambda sync, lnd2=lnd2: sync.dma_start(
                                out=dbg["hqT"], in_=hqT[:])
                               .then_inc(lnd2.h, 16))
                            lnd2.bump(16)

                        fin_marks = []
                        for ch in range(2):
                            wdld = []
                            for i in range(NI):
                                if i >= 2:
                                    wait('sync', stn, wdld[i - 2])
                                lnd3 = dxb.next()
                                on('sync', lambda sync, i=i, ch=ch, lnd3=lnd3:
                                   sync.dma_start(
                                    out=wdt_sb[i % 2][:],
                                    in_=wdt_d[:, i, ch * 512:(ch + 1) * 512])
                                   .then_inc(lnd3.h, 16))
                                wv = (lnd3, lnd3.bump(16))
                                if i == 0 and fin_marks:
                                    wait('tensor', sv, fin_marks[-1])
                                tv = stn.bump()
                                on('tensor', lambda tn, i=i, wv=wv: (
                                    tn.wait_ge(wv[0].h, wv[1]),
                                    [tn.matmul(
                                        ps_d[:, tt2, :],
                                        hqT[:, i, tt2 * 128:(tt2 + 1) * 128],
                                        wdt_sb[i % 2][:],
                                        start=(i == 0), stop=(i == NI - 1))
                                     for tt2 in range(4)][-1]
                                    .then_inc(stn.h, 1)))
                                wdld.append(stn.v)
                            for tt in range(4):
                                t = t0 + tt
                                vf = sv.bump(2)
                                on('vector', lambda v, t=t, tt=tt, ch=ch,
                                   wl=wdld[-1], qA=qs['A']: (
                                    v.wait_ge(stn.h, wl),
                                    v.scalar_tensor_tensor(
                                        out=qA[:, 0:512],
                                        in0=ps_d[:, tt, :], scalar=0.0,
                                        in1=attn_out[:, t,
                                                     ch * 512:(ch + 1) * 512],
                                        op0=ALU.add, op1=ALU.add),
                                    v.tensor_tensor(
                                        out=out_sb[tt % 2][:],
                                        in0=qA[:, 0:512],
                                        in1=xg[:, tt, ch * 512:(ch + 1) * 512],
                                        op=ALU.add),
                                    v.drain().then_inc(sv.h, 2)))
                                fin_marks.append(vf)
                                lny = dxb.next()
                                on('sync', lambda sync, t=t, tt=tt, ch=ch,
                                   vf=vf, lny=lny: (
                                    sync.wait_ge(sv.h, vf),
                                    sync.dma_start(
                                        out=y_d[b, t * 128:(t + 1) * 128,
                                                ch * 512:(ch + 1) * 512],
                                        in_=out_sb[tt % 2][:])
                                    .then_inc(lny.h, 16)))
                                out_hist.append((lny, lny.bump(16)))
                                if len(out_hist) >= 2:
                                    wmark('vector', out_hist[-2])
                        barrier()

        for b in range(B_PER_CORE):
            plan_batch(b)

        for (ln, vv) in dxb.totals() + dtpb.totals() + dcb.totals():
            wait('sync', ln, vv)
        barrier()

        blk = st.enter_context(nc.Block())
        for e in ENGINES:
            def make(e):
                def body(eng):
                    for fn in prog[e]:
                        fn(eng)
                return body
            getattr(blk, e)(make(e))

    return nc


_NC_CACHE = [None]


def kernel(L0_attn_x, L0_attn_Q_w, L0_attn_Q_b, L0_attn_K_w, L0_attn_K_b,
           L0_attn_V_w, L0_attn_V_b, L0_attn_O_w, L0_attn_O_b,
           L0_ffn_w_gate, L0_ffn_w_up, L0_ffn_w_down):
    if _NC_CACHE[0] is None:
        _NC_CACHE[0] = build_program()
    nc = _NC_CACHE[0]

    x = np.ascontiguousarray(np.asarray(L0_attn_x, dtype=np.float32))
    common = {
        "qw": np.ascontiguousarray(np.asarray(L0_attn_Q_w, np.float32)),
        "kw": np.ascontiguousarray(np.asarray(L0_attn_K_w, np.float32)),
        "vw": np.ascontiguousarray(np.asarray(L0_attn_V_w, np.float32)),
        "qb": np.ascontiguousarray(np.asarray(L0_attn_Q_b, np.float32)),
        "kb": np.ascontiguousarray(np.asarray(L0_attn_K_b, np.float32)),
        "vb": np.ascontiguousarray(np.asarray(L0_attn_V_b, np.float32)),
        "ow": np.ascontiguousarray(np.asarray(L0_attn_O_w, np.float32)),
        "ob": np.ascontiguousarray(np.asarray(L0_attn_O_b, np.float32))
        .reshape(1, D),
        "wg": np.ascontiguousarray(np.asarray(L0_ffn_w_gate, np.float32)),
        "wu": np.ascontiguousarray(np.asarray(L0_ffn_w_up, np.float32)),
        "wd": np.ascontiguousarray(np.asarray(L0_ffn_w_down, np.float32)),
    }
    in_maps = []
    for c in range(NCORES):
        m = dict(common)
        m["x"] = np.ascontiguousarray(x[c * B_PER_CORE:(c + 1) * B_PER_CORE])
        in_maps.append(m)

    res = run_bass_kernel_spmd(nc, in_maps, core_ids=list(range(NCORES)))
    out = np.concatenate([np.asarray(r["y"]) for r in res.results], axis=0)
    return out.astype(np.float32)


# version 12
# speedup vs baseline: 1.0376x; 1.0164x over previous
import numpy as np
from contextlib import ExitStack

import concourse.bass as bass
import concourse.mybir as mybir
from concourse.bass_utils import run_bass_kernel_spmd

F32 = mybir.dt.float32
BF16 = mybir.dt.bfloat16
I32 = mybir.dt.int32
ALU = mybir.AluOpType
AF = mybir.ActivationFunctionType
AX = mybir.AxisListType

NCORES = 8
B_PER_CORE = 2
S = 2048
D = 1024
H = 16
DH = 64
DI = 2816
NT = S // 128
ND = D // 128
NI = DI // 128
QC = 512
NQC = S // QC
EXPB = -20.0
ENGINES = ("sync", "gpsimd", "vector", "scalar", "tensor")
DEBUG_TAPS = False
DEBUG_TAPS_BATCH = 0


class Ctr:
    def __init__(self, stack, nc, name):
        self.h = stack.enter_context(nc.semaphore(name))
        self.v = 0

    def bump(self, n=1):
        self.v += n
        return self.v


class Lanes:

    def __init__(self, stack, nc, name, k=8):
        self.lanes = [Ctr(stack, nc, f"{name}{i}") for i in range(k)]
        self.i = 0

    def next(self):
        ln = self.lanes[self.i % len(self.lanes)]
        self.i += 1
        return ln

    def totals(self):
        return [(ln, ln.v) for ln in self.lanes if ln.v > 0]


def build_program():
    nc = bass.Bass("TRN2", target_bir_lowering=False,
                   detect_race_conditions=False)

    x_d = nc.dram_tensor("x", [B_PER_CORE, S, D], F32, kind="ExternalInput").ap()
    qw_d = nc.dram_tensor("qw", [H, D, DH], F32, kind="ExternalInput").ap()
    kw_d = nc.dram_tensor("kw", [H, D, DH], F32, kind="ExternalInput").ap()
    vw_d = nc.dram_tensor("vw", [H, D, DH], F32, kind="ExternalInput").ap()
    qb_d = nc.dram_tensor("qb", [H, DH], F32, kind="ExternalInput").ap()
    kb_d = nc.dram_tensor("kb", [H, DH], F32, kind="ExternalInput").ap()
    vb_d = nc.dram_tensor("vb", [H, DH], F32, kind="ExternalInput").ap()
    ow_d = nc.dram_tensor("ow", [H, DH, D], F32, kind="ExternalInput").ap()
    ob_d = nc.dram_tensor("ob", [1, D], F32, kind="ExternalInput").ap()
    wg_d = nc.dram_tensor("wg", [DI, D], F32, kind="ExternalInput").ap()
    wu_d = nc.dram_tensor("wu", [DI, D], F32, kind="ExternalInput").ap()
    wd_d = nc.dram_tensor("wd", [D, DI], F32, kind="ExternalInput").ap()
    y_d = nc.dram_tensor("y", [B_PER_CORE, S, D], F32, kind="ExternalOutput").ap()
    wkind = "ExternalOutput" if DEBUG_TAPS else "Internal"
    wgt_d = nc.dram_tensor("wgt", [128, ND, DI], BF16, kind=wkind).ap()
    wut_d = nc.dram_tensor("wut", [128, ND, DI], BF16, kind=wkind).ap()
    wdt_d = nc.dram_tensor("wdt", [128, NI, D], BF16, kind=wkind).ap()
    dbg = {}
    if DEBUG_TAPS:
        for nm, shp, dt2 in (("xnT", [128, ND, S], BF16),
                             ("qt", [128, 8, S], BF16),
                             ("kt", [128, 8, S], BF16),
                             ("vp", [128, NT, H * 65], BF16),
                             ("ao", [128, NT, D], BF16),
                             ("pt", [128, NT, QC], BF16),
                             ("dst", [128, H * NT], F32),
                             ("drc", [128, H * NT], F32),
                             ("rstd", [128, NT], F32),
                             ("nmr", [128, NT], F32),
                             ("xqT", [128, ND, S], BF16),
                             ("hqT", [128, NI, 512], BF16)):
            dbg[nm] = nc.dram_tensor("dbg_" + nm, shp, dt2,
                                     kind="ExternalOutput").ap()

    prog = {e: [] for e in ENGINES}

    def on(eng, fn):
        prog[eng].append(fn)

    QW = 1024

    with ExitStack() as st:
        def sb(name, shape, dtype, **kw):
            return st.enter_context(nc.sbuf_tensor(name, shape, dtype, **kw))

        neg20 = sb("neg20", [128, 1], F32)
        masks = sb("masks", [128, 4, QC], BF16)
        ob_t = sb("ob_t", [128, D], F32)
        vb_t = sb("vb_t", [128, D], F32)
        qb_sb = sb("qb_sb", [128, 8], F32)
        kb_sb = sb("kb_sb", [128, 8], F32)
        st_rstd = sb("st_rstd", [128, NT], F32)
        st_nmr = sb("st_nmr", [128, NT], F32)
        ident = sb("ident", [128, 128], BF16)
        qs = {}

        def alloc_qs(scope, tag):
            qs.clear()
            for k in ("A", "B", "C", "D"):
                qs[k] = scope.enter_context(
                    nc.sbuf_tensor(f"qs_{k}_{tag}", [128, QW], F32))
            qs['E'] = scope.enter_context(
                nc.sbuf_tensor(f"qs_E_{tag}", [128, QW], I32))
            for k in ("bm", "scale", "lns", "rsh", "sc2"):
                qs[k] = scope.enter_context(
                    nc.sbuf_tensor(f"qs_{k}_{tag}", [128, QW // 16], F32))

        sv = Ctr(st, nc, "sv")
        ss = Ctr(st, nc, "ss")
        stn = Ctr(st, nc, "stn")
        sg = Ctr(st, nc, "sg")
        dxb = Lanes(st, nc, "dx", 8)
        dtpb = Lanes(st, nc, "dtp", 8)
        dcb = Lanes(st, nc, "dc", 8)
        bar = Ctr(st, nc, "bar")
        nbar = [0]

        def barrier():
            nbar[0] += 1
            k = nbar[0]
            for e in ENGINES:
                on(e, lambda eng, k=k: (eng.drain().then_inc(bar.h, 1),
                                        eng.wait_ge(bar.h, 5 * k)))

        def wait(eng, ctr, val):
            if val > 0:
                on(eng, lambda E, c=ctr, vv=val: E.wait_ge(c.h, vv))

        def wmark(eng, mark):
            if mark:
                c, vv = mark
                on(eng, lambda E, c=c, vv=vv: E.wait_ge(c.h, vv))

        def emit_quant(x_ap, out_ap, F, drain_out=False):
            nb = F // 16
            xv = x_ap.rearrange("p (b w) -> p b w", w=16)
            A, Bb, C, Dt, E = (qs['A'][:, 0:F], qs['B'][:, 0:F],
                               qs['C'][:, 0:F], qs['D'][:, 0:F],
                               qs['E'][:, 0:F])
            bm, scale, lns, rsh, sc2 = (qs[k][:, 0:nb] for k in
                                        ("bm", "scale", "lns", "rsh", "sc2"))
            v1 = sv.bump()
            on('vector', lambda v, xv=xv, bm=bm, v1=v1: (
                v.tensor_reduce(out=bm, in_=xv, axis=AX.X, op=ALU.max,
                                apply_absolute_value=True).then_inc(sv.h, 1),
                v.wait_ge(sv.h, v1)))
            v2 = sv.bump()
            on('vector', lambda v, bm=bm, scale=scale: (
                v.tensor_scalar(
                    out=scale, in0=bm, scalar1=1.0 / 6.0, scalar2=1e-12,
                    op0=ALU.mult, op1=ALU.max).then_inc(sv.h, 1)))
            s1 = ss.bump()
            on('scalar', lambda s, scale=scale, lns=lns, rsh=rsh, sc2=sc2,
               v2=v2: (
                s.wait_ge(sv.h, v2),
                s.activation(out=lns, in_=scale, func=AF.Ln, bias=0.0,
                             scale=2.0),
                s.activation(out=rsh, in_=lns, func=AF.Exp, bias=0.0,
                             scale=-1.0),
                s.mul(sc2, scale, 0.5).then_inc(ss.h, 1)))
            lns2 = qs['lns'][:, 0:nb]
            scale2 = qs['scale'][:, 0:nb]
            nr1 = sv.bump(3)
            vend = sv.bump()

            bm2 = qs['bm'][:, 0:nb]

            def main(v, xv=xv, A=A, Bb=Bb, C=C, Dt=Dt, E=E, rsh=rsh,
                     sc2=sc2, out_ap=out_ap, nb=nb, s1=s1, lns2=lns2,
                     scale2=scale2, nr1=nr1, bm2=bm2, drain_out=drain_out):
                v.wait_ge(ss.h, s1)
                v.tensor_tensor(out=lns2, in0=scale2, in1=rsh,
                                op=ALU.mult).then_inc(sv.h, 1)
                v.wait_ge(sv.h, nr1 - 2)
                v.tensor_scalar(out=bm2, in0=lns2, scalar1=-2.0, scalar2=2.0,
                                op0=ALU.mult, op1=ALU.add).then_inc(sv.h, 1)
                v.wait_ge(sv.h, nr1 - 1)
                v.tensor_tensor(out=lns2, in0=rsh, in1=bm2,
                                op=ALU.mult).then_inc(sv.h, 1)
                v.wait_ge(sv.h, nr1)
                rsh_b = lns2.unsqueeze(-1).broadcast_to([128, nb, 16])
                sc2_b = sc2.unsqueeze(-1).broadcast_to([128, nb, 16])
                v.tensor_tensor(out=A.rearrange("p (b w) -> p b w", w=16),
                                in0=xv, in1=rsh_b, op=ALU.mult)
                v.tensor_scalar(out=Bb, in0=A, scalar1=1.0, scalar2=-1.0,
                                op0=ALU.min, op1=ALU.max)
                v.tensor_scalar(out=C, in0=A, scalar1=2.0, scalar2=-2.0,
                                op0=ALU.min, op1=ALU.max)
                v.scalar_tensor_tensor(out=Dt, in0=Bb, scalar=2.0, in1=A,
                                       op0=ALU.mult, op1=ALU.add)
                v.tensor_tensor(out=Bb, in0=Dt, in1=C, op=ALU.add)
                v.tensor_scalar(out=E, in0=Bb, scalar1=8.0, scalar2=0.0,
                                op0=ALU.add, op1=ALU.add)
                v.scalar_tensor_tensor(out=Dt, in0=E, scalar=-7.5, in1=Bb,
                                       op0=ALU.add, op1=ALU.is_lt)
                v.tensor_tensor(out=C, in0=Dt, in1=E, op=ALU.add)
                v.tensor_scalar(out=A, in0=C, scalar1=12.0, scalar2=4.0,
                                op0=ALU.min, op1=ALU.max)
                v.tensor_scalar(out=Dt, in0=C, scalar1=14.0, scalar2=2.0,
                                op0=ALU.min, op1=ALU.max)
                v.scalar_tensor_tensor(out=Bb, in0=C, scalar=2.0, in1=Dt,
                                       op0=ALU.mult, op1=ALU.subtract)
                v.scalar_tensor_tensor(out=C, in0=Bb, scalar=2.0, in1=A,
                                       op0=ALU.mult, op1=ALU.subtract)
                fin = v.scalar_tensor_tensor(
                    out=out_ap.rearrange("p (b w) -> p b w", w=16),
                    in0=C.rearrange("p (b w) -> p b w", w=16), scalar=-8.0,
                    in1=sc2_b, op0=ALU.add, op1=ALU.mult)
                if drain_out:
                    v.drain().then_inc(sv.h, 1)
                else:
                    fin.then_inc(sv.h, 1)

            on('vector', main)
            return vend

        with ExitStack() as st0:
            iota_t = st0.enter_context(nc.sbuf_tensor("iota_t", [128, QC], F32))
            g1 = sg.bump()
            on('gpsimd', lambda g: (
                g.memset(neg20[:], EXPB),
                g.iota(iota_t[:], [[1, QC]], base=0, channel_multiplier=-1,
                       allow_small_or_imprecise_dtypes=True)
                .then_inc(sg.h, 1)))
            su_l = [dxb.next() for _ in range(4)]
            on('sync', lambda sync, su_l=su_l: (
                sync.dma_start(out=ob_t[:], in_=ob_d.broadcast_to([128, D]))
                .then_inc(su_l[0].h, 16),
                sync.dma_start(out=vb_t[:],
                               in_=vb_d.rearrange("h d -> (h d)").unsqueeze(0)
                               .broadcast_to([128, D])).then_inc(su_l[1].h, 16),
                sync.dma_start(out=qb_sb[:],
                               in_=bass.AP(qb_d.tensor, 0,
                                           [[1, 128], [128, 8]]),
                               allow_slow_non_contiguous=True)
                .then_inc(su_l[2].h, 16),
                sync.dma_start(out=kb_sb[:],
                               in_=bass.AP(kb_d.tensor, 0,
                                           [[1, 128], [128, 8]]),
                               allow_slow_non_contiguous=True)
                .then_inc(su_l[3].h, 16)))
            setup_marks = [(ln, ln.bump(16)) for ln in su_l]
            wait('vector', sg, g1)
            sv.bump()
            on('vector', lambda v: v.tensor_scalar(
                out=ident[:], in0=iota_t[:, 0:128], scalar1=0.0, scalar2=0.0,
                op0=ALU.is_equal, op1=ALU.add).then_inc(sv.h, 1))
            for m in range(4):
                sv.bump()
                on('vector', lambda v, m=m: v.tensor_scalar(
                    out=masks[:, m, :], in0=iota_t[:], scalar1=float(128 * m),
                    scalar2=0.0, op0=ALU.is_ge, op1=ALU.add)
                   .then_inc(sv.h, 1))
            masks_done = sv.v

            alloc_qs(st0, "w")
            w_in = [st0.enter_context(nc.sbuf_tensor(f"w_in{i}", [128, DI],
                                                     F32)) for i in range(2)]
            w_bf = [st0.enter_context(nc.sbuf_tensor(f"w_bf{i}", [128, DI],
                                                     BF16)) for i in range(2)]
            w_tp = [st0.enter_context(nc.sbuf_tensor(
                f"w_tp{i}", [128, NI, 128], BF16, is_transpose_dest=True))
                for i in range(2)]

            tiles = []
            for (src, rows, cols, dstT) in ((wg_d, DI, D, wgt_d),
                                            (wu_d, DI, D, wut_d),
                                            (wd_d, D, DI, wdt_d)):
                for t in range(rows // 128):
                    tiles.append((src, cols, dstT, t))

            vq_done, tp_done, stw_done, ld_done = [], [], [], []
            for k, (src, cols, dstT, t) in enumerate(tiles):
                nsub = cols // 128
                if k >= 2:
                    wait('sync', sv, vq_done[k - 2])
                lnl = dxb.next()
                on('sync', lambda sync, src=src, t=t, cols=cols, k=k, lnl=lnl:
                   sync.dma_start(out=w_in[k % 2][:, 0:cols],
                                  in_=src[t * 128:(t + 1) * 128, :])
                   .then_inc(lnl.h, 16))
                ld_done.append((lnl, lnl.bump(16)))
                wmark('vector', ld_done[k])
                if k >= 2:
                    wmark('vector', tp_done[k - 2])
                for piece in range(0, cols, QW):
                    pw = min(QW, cols - piece)
                    emit_quant(w_in[k % 2][:, piece:piece + pw],
                               w_bf[k % 2][:, piece:piece + pw], pw,
                               drain_out=True)
                vq_done.append(sv.v)
                wait('scalar', sv, vq_done[k])
                if k >= 2:
                    wmark('scalar', stw_done[k - 2])
                lnt = dtpb.next()
                on('scalar', lambda s, k=k, cols=cols, nsub=nsub, lnt=lnt:
                   s.dma_start(out=w_tp[k % 2][:, 0:nsub, :],
                               in_=w_bf[k % 2][:, 0:cols], transpose=True)
                   .then_inc(lnt.h, 16))
                tp_done.append((lnt, lnt.bump(16)))
                wmark('sync', tp_done[k])
                lns_ = dxb.next()
                on('sync', lambda sync, dstT=dstT, t=t, nsub=nsub, k=k,
                   lns_=lns_:
                   sync.dma_start(out=dstT[:, 0:nsub, t * 128:(t + 1) * 128],
                                  in_=w_tp[k % 2][:, 0:nsub, :])
                   .then_inc(lns_.h, 16))
                stw_done.append((lns_, lns_.bump(16)))

            for (ln, vv) in dxb.totals() + dtpb.totals():
                wait('sync', ln, vv)
            for m in setup_marks:
                wmark('sync', m)
            wait('vector', sv, masks_done)
            barrier()

        xq_tp_hist = []
        out_hist = []

        def plan_batch(b):
            with ExitStack() as stb:
                attn_out = stb.enter_context(nc.sbuf_tensor(
                    f"attn_out_{b}", [128, NT, D], BF16,
                    is_transpose_dest=True))
                xnT = attn_out[:].rearrange("p a c -> p (a c)") \
                    .rearrange("p (u w) -> p u w", u=ND)
                attnv = attn_out[:]

                with ExitStack() as sta:
                    qt_all = sta.enter_context(nc.sbuf_tensor(
                        f"qt_all_{b}", [128, 8, S], BF16))
                    kt_all = sta.enter_context(nc.sbuf_tensor(
                        f"kt_all_{b}", [128, 8, S], BF16))
                    v_perm = sta.enter_context(nc.sbuf_tensor(
                        f"v_perm_{b}", [128, NT, H * 65], BF16))
                    pt = sta.enter_context(nc.sbuf_tensor(
                        f"pt_{b}", [128, NT, QC], BF16))

                    with ExitStack() as stp:
                        x_t = [stp.enter_context(nc.sbuf_tensor(
                            f"x_t{i}_{b}", [128, D], F32)) for i in range(2)]
                        xsq = stp.enter_context(nc.sbuf_tensor(
                            f"xsq_{b}", [128, D], F32))
                        xn_bf = [stp.enter_context(nc.sbuf_tensor(
                            f"xn_bf{i}_{b}", [128, D], BF16))
                            for i in range(2)]
                        sm1 = stp.enter_context(nc.sbuf_tensor(
                            f"sm1_{b}", [128, 1], F32))
                        sm2 = stp.enter_context(nc.sbuf_tensor(
                            f"sm2_{b}", [128, 1], F32))
                        msq = stp.enter_context(nc.sbuf_tensor(
                            f"msq_{b}", [128, 1], F32))
                        var = stp.enter_context(nc.sbuf_tensor(
                            f"var_{b}", [128, 1], F32))
                        lnv = stp.enter_context(nc.sbuf_tensor(
                            f"lnv_{b}", [128, 1], F32))
                        mmean = stp.enter_context(nc.sbuf_tensor(
                            f"mmean_{b}", [128, 1], F32))
                        wbuf = [stp.enter_context(nc.sbuf_tensor(
                            f"wbuf{i}_{b}", [128, ND, 1024], BF16))
                            for i in range(2)]

                        ones_done = sv.bump()
                        on('vector', lambda v: v.memset(
                            bass.AP(v_perm, 64,
                                    [[NT * H * 65, 128], [H * 65, NT],
                                     [65, H]]), 1.0).then_inc(sv.h, 1))

                        tp_x_done = []
                        ln_s3 = []
                        for t in range(NT):
                            if t >= 2:
                                wait('sync', ss, ln_s3[t - 2])
                            lnx = dxb.next()
                            on('sync', lambda sync, t=t, lnx=lnx:
                               sync.dma_start(
                                out=x_t[t % 2][:],
                                in_=x_d[b, t * 128:(t + 1) * 128, :])
                               .then_inc(lnx.h, 16))
                            xldt = (lnx, lnx.bump(16))
                            v1 = sv.bump()
                            on('vector', lambda v, t=t, v1=v1, xldt=xldt: (
                                v.wait_ge(xldt[0].h, xldt[1]),
                                v.tensor_reduce(out=sm1[:], in_=x_t[t % 2][:],
                                                axis=AX.X, op=ALU.add)
                                .then_inc(sv.h, 1),
                                v.wait_ge(sv.h, v1)))
                            s1 = ss.bump()
                            on('scalar', lambda s, t=t, xldt=xldt: (
                                s.wait_ge(xldt[0].h, xldt[1]),
                                s.square(xsq[:], x_t[t % 2][:])
                                .then_inc(ss.h, 1)))
                            v2 = sv.bump()
                            on('vector', lambda v, s1=s1, v2=v2: (
                                v.wait_ge(ss.h, s1),
                                v.tensor_reduce(out=sm2[:], in_=xsq[:],
                                                axis=AX.X, op=ALU.add)
                                .then_inc(sv.h, 1),
                                v.wait_ge(sv.h, v2)))
                            v3 = sv.bump(4)
                            on('vector', lambda v, v3=v3: (
                                v.tensor_scalar(out=mmean[:], in0=sm1[:],
                                                scalar1=1.0 / D, scalar2=0.0,
                                                op0=ALU.mult, op1=ALU.add)
                                .then_inc(sv.h, 1),
                                v.wait_ge(sv.h, v3 - 3),
                                v.tensor_tensor(out=msq[:], in0=mmean[:],
                                                in1=sm1[:], op=ALU.mult)
                                .then_inc(sv.h, 1),
                                v.wait_ge(sv.h, v3 - 2),
                                v.tensor_tensor(out=var[:], in0=sm2[:],
                                                in1=msq[:], op=ALU.subtract)
                                .then_inc(sv.h, 1),
                                v.wait_ge(sv.h, v3 - 1),
                                v.tensor_scalar(out=var[:], in0=var[:],
                                                scalar1=1.0 / D, scalar2=1e-5,
                                                op0=ALU.mult, op1=ALU.add)
                                .then_inc(sv.h, 1),
                                v.wait_ge(sv.h, v3)))
                            s2 = ss.bump()
                            on('scalar', lambda s, t=t, v3=v3: (
                                s.wait_ge(sv.h, v3),
                                s.activation(out=lnv[:], in_=var[:],
                                             func=AF.Ln, bias=0.0, scale=1.0),
                                s.activation(out=st_rstd[:, t:t + 1],
                                             in_=lnv[:], func=AF.Exp,
                                             bias=0.0, scale=-0.5)
                                .then_inc(ss.h, 1)))
                            v5 = sv.bump()
                            on('vector', lambda v, t=t, s2=s2: (
                                v.wait_ge(ss.h, s2),
                                v.scalar_tensor_tensor(
                                    out=st_nmr[:, t:t + 1], in0=mmean[:],
                                    scalar=-1.0, in1=st_rstd[:, t:t + 1],
                                    op0=ALU.mult, op1=ALU.mult)
                                .then_inc(sv.h, 1)))
                            s3 = ss.bump()
                            on('scalar', lambda s, t=t, v5=v5,
                               tpm=(tp_x_done[t - 2] if t >= 2 else None): (
                                s.wait_ge(sv.h, v5),
                                (s.wait_ge(tpm[0].h, tpm[1])
                                 if tpm else None),
                                s.activation(out=xn_bf[t % 2][:],
                                             in_=x_t[t % 2][:], func=AF.Identity,
                                             bias=st_nmr[:, t:t + 1],
                                             scale=st_rstd[:, t:t + 1]),
                                s.drain().then_inc(ss.h, 1)))
                            ln_s3.append(s3)
                            lnxt = dtpb.next()
                            on('scalar', lambda s, t=t, s3=s3, lnxt=lnxt: (
                                s.wait_ge(ss.h, s3),
                                s.dma_start(
                                    out=xnT[:, :, t * 128:(t + 1) * 128],
                                    in_=xn_bf[t % 2][:], transpose=True)
                                .then_inc(lnxt.h, 16)))
                            tp_x_done.append((lnxt, lnxt.bump(16)))

                        ps_qkv = stp.enter_context(nc.psum_tensor(
                            f"ps_qkv_{b}", [128, 2, 512], F32))

                        def load_qk(j, wdram):
                            marks = {}
                            for hp in range(8):
                                for hf in range(2):
                                    lnw = dcb.next()
                                    on('gpsimd', lambda g, wdram=wdram,
                                       wsb=wbuf[j % 2], hp=hp, hf=hf, lnw=lnw:
                                       g.dma_start(
                                        out=bass.AP(wsb, hp * 128 + hf * 64,
                                                    [[ND * 1024, 128],
                                                     [1024, ND], [1, 64]]),
                                        in_=bass.AP(wdram.tensor,
                                                    (2 * hp + hf) * D * DH,
                                                    [[64, 128],
                                                     [128 * 64, ND],
                                                     [1, 64]]))
                                       .then_inc(lnw.h, 16))
                                    marks[id(lnw)] = (lnw, lnw.bump(16))
                            return list(marks.values())

                        wl_q = load_qk(0, qw_d)
                        wl_k = load_qk(1, kw_d)

                        wmark('tensor', tp_x_done[-1])
                        slot = 0
                        ev_done = []
                        wuse = {}
                        for j, (bsb, dst) in enumerate(((qb_sb, qt_all),
                                                       (kb_sb, kt_all))):
                            for m in (wl_q if j == 0 else wl_k):
                                wmark('tensor', m)
                            for hp in range(8):
                                for c in range(4):
                                    sl = slot % 2
                                    if slot >= 2:
                                        e, m = ev_done[slot - 2]
                                        wait('tensor', ss if e == 'ss'
                                             else sv, m)
                                    tv = stn.bump()
                                    on('tensor', lambda tn, j=j, hp=hp, c=c,
                                       sl=sl: [
                                        tn.matmul(
                                            ps_qkv[:, sl, :],
                                            wbuf[j % 2][:, d,
                                                        hp * 128:(hp + 1) * 128],
                                            xnT[:, d, c * 512:(c + 1) * 512],
                                            start=(d == 0),
                                            stop=(d == ND - 1))
                                        for d in range(ND)][-1]
                                       .then_inc(stn.h, 1))
                                    se = ss.bump()
                                    on('scalar', lambda s, bsb=bsb, dst=dst,
                                       hp=hp, c=c, sl=sl, tv=tv: (
                                        s.wait_ge(stn.h, tv),
                                        s.activation(
                                            out=dst[:, hp,
                                                    c * 512:(c + 1) * 512],
                                            in_=ps_qkv[:, sl, :],
                                            func=AF.Identity,
                                            bias=bsb[:, hp:hp + 1],
                                            scale=1.0).then_inc(ss.h, 1)))
                                    ev_done.append(('ss', se))
                                    slot += 1
                            wuse[j] = stn.v
                        wait('gpsimd', stn, wuse[0])
                        vmarks = {}
                        for h2 in range(H):
                            lnv2 = dcb.next()
                            on('gpsimd', lambda g, h2=h2, lnv2=lnv2:
                               g.dma_start(
                                out=bass.AP(wbuf[0], h2 * 64,
                                            [[ND * 1024, 128], [1024, ND],
                                             [1, 64]]),
                                in_=bass.AP(vw_d.tensor, h2 * D * DH,
                                            [[64, 128], [128 * 64, ND],
                                             [1, 64]]))
                               .then_inc(lnv2.h, 16))
                            vmarks[id(lnv2)] = (lnv2, lnv2.bump(16))
                        for m in vmarks.values():
                            wmark('tensor', m)
                        wait('vector', sv, ones_done)
                        for t in range(NT):
                            for c in range(2):
                                sl = slot % 2
                                if slot >= 2:
                                    e, m = ev_done[slot - 2]
                                    wait('tensor', ss if e == 'ss' else sv, m)
                                tv = stn.bump()
                                on('tensor', lambda tn, t=t, c=c, sl=sl: [
                                    tn.matmul(
                                        ps_qkv[:, sl, :],
                                        xnT[:, d, t * 128:(t + 1) * 128],
                                        wbuf[0][:, d, c * 512:(c + 1) * 512],
                                        start=(d == 0), stop=(d == ND - 1))
                                    for d in range(ND)][-1]
                                   .then_inc(stn.h, 1))
                                vvl = None
                                for hh in range(8):
                                    h = c * 8 + hh
                                    vvl = sv.bump()
                                    on('vector', lambda v, t=t, sl=sl, hh=hh,
                                       h=h, tv=tv: (
                                        (v.wait_ge(stn.h, tv)
                                         if hh == 0 else None),
                                        v.scalar_tensor_tensor(
                                            out=v_perm[:, t,
                                                       h * 65:h * 65 + 64],
                                            in0=ps_qkv[:, sl,
                                                       hh * 64:(hh + 1) * 64],
                                            scalar=0.0,
                                            in1=vb_t[:, h * 64:(h + 1) * 64],
                                            op0=ALU.add, op1=ALU.add)
                                        .then_inc(sv.h, 1)))
                                ev_done.append(('sv', vvl))
                                slot += 1
                        for (e, m) in ev_done[-2:]:
                            wait('tensor', ss if e == 'ss' else sv, m)
                        barrier()
                        if DEBUG_TAPS and b == DEBUG_TAPS_BATCH:
                            dbl = [dxb.next() for _ in range(6)]
                            on('sync', lambda sync, dbl=dbl: (
                                sync.dma_start(out=dbg["xnT"],
                                               in_=attn_out[:].rearrange(
                                                   "p a c -> p (a c)")
                                               .rearrange("p (u w) -> p u w",
                                                          u=ND))
                                .then_inc(dbl[0].h, 16),
                                sync.dma_start(out=dbg["qt"], in_=qt_all[:])
                                .then_inc(dbl[1].h, 16),
                                sync.dma_start(out=dbg["kt"], in_=kt_all[:])
                                .then_inc(dbl[2].h, 16),
                                sync.dma_start(out=dbg["vp"], in_=v_perm[:])
                                .then_inc(dbl[3].h, 16),
                                sync.dma_start(out=dbg["rstd"], in_=st_rstd[:])
                                .then_inc(dbl[4].h, 16),
                                sync.dma_start(out=dbg["nmr"], in_=st_nmr[:])
                                .then_inc(dbl[5].h, 16)))
                            for ln2 in dbl:
                                wmark('sync', (ln2, ln2.bump(16)))
                            barrier()

                    with ExitStack() as sta3:
                        esc = sta3.enter_context(nc.sbuf_tensor(
                            f"esc_{b}", [128, QC], BF16))
                        dstage = sta3.enter_context(nc.sbuf_tensor(
                            f"dstage_{b}", [128, H * NT], F32))
                        drec = sta3.enter_context(nc.sbuf_tensor(
                            f"drec_{b}", [128, H * NT], F32))
                        ow_sb = sta3.enter_context(nc.sbuf_tensor(
                            f"ow_sb_{b}", [128, 8, D], BF16))
                        anorm = [sta3.enter_context(nc.sbuf_tensor(
                            f"anorm{i}_{b}", [128, D], BF16))
                            for i in range(2)]
                        avT = [sta3.enter_context(nc.sbuf_tensor(
                            f"avT{i}_{b}", [128, 8, 128], BF16,
                            is_transpose_dest=True)) for i in range(2)]
                        ps_l = sta3.enter_context(nc.psum_tensor(
                            f"ps_l_{b}", [128, 2, 512], F32))
                        ps_pv = sta3.enter_context(nc.psum_tensor(
                            f"ps_pv_{b}", [128, 2, 512], F32))
                        ps_o = sta3.enter_context(nc.psum_tensor(
                            f"ps_o_{b}", [128, 2, 512], F32))

                        ow_marks = []
                        for half in range(2):
                            lno = dcb.next()
                            on('gpsimd', lambda g, half=half, lno=lno:
                               g.dma_start(
                                out=ow_sb[half * 64:(half + 1) * 64, :, :],
                                in_=bass.AP(ow_d.tensor, half * DH * D,
                                            [[D, 64], [2 * DH * D, 8],
                                             [1, D]])).then_inc(lno.h, 16))
                            ow_marks.append((lno, lno.bump(16)))

                        lslot, pvslot = 0, 0
                        l_cons, pv_cons = [], []
                        last_pv_tv = 0
                        for h in range(H):
                            hp, po = h // 2, (h % 2) * 64
                            for c in range(NQC):
                                nkt = 4 * (c + 1)
                                exp_marks = []
                                if last_pv_tv:
                                    wait('scalar', stn, last_pv_tv)
                                    wait('vector', stn, last_pv_tv)
                                for kt in range(nkt):
                                    slq = lslot % 2
                                    if lslot >= 2:
                                        wait('tensor', ss, l_cons[lslot - 2])
                                    tv = stn.bump()
                                    on('tensor', lambda tn, hp=hp, po=po,
                                       kt=kt, c=c, slq=slq: tn.matmul(
                                        ps_l[:, slq, :],
                                        kt_all[po:po + 64, hp,
                                               kt * 128:(kt + 1) * 128],
                                        qt_all[po:po + 64, hp,
                                               c * QC:(c + 1) * QC],
                                        start=True, stop=True)
                                       .then_inc(stn.h, 1))
                                    diag = (kt >= nkt - 4)
                                    se = ss.bump()
                                    if not diag:
                                        on('scalar', lambda s, kt=kt, slq=slq,
                                           tv=tv: (
                                            s.wait_ge(stn.h, tv),
                                            s.activation(
                                                out=pt[:, kt, :],
                                                in_=ps_l[:, slq, :],
                                                func=AF.Exp, bias=neg20[:],
                                                scale=0.125)
                                            .then_inc(ss.h, 1)))
                                        exp_marks.append(('s', se))
                                    else:
                                        on('scalar', lambda s, slq=slq,
                                           tv=tv: (
                                            s.wait_ge(stn.h, tv),
                                            s.activation(
                                                out=esc[:],
                                                in_=ps_l[:, slq, :],
                                                func=AF.Exp, bias=neg20[:],
                                                scale=0.125)
                                            .then_inc(ss.h, 1)))
                                        mv = sv.bump()
                                        mi = kt - (nkt - 4)
                                        on('vector', lambda v, kt=kt, mi=mi,
                                           se=se: (
                                            v.wait_ge(ss.h, se),
                                            v.tensor_tensor(
                                                out=pt[:, kt, :],
                                                in0=esc[:],
                                                in1=masks[:, mi, :],
                                                op=ALU.mult)
                                            .then_inc(sv.h, 1)))
                                        exp_marks.append(('v', mv))
                                        wait('scalar', sv, mv)
                                    l_cons.append(se)
                                    lslot += 1
                                for qt4 in range(4):
                                    qt = c * 4 + qt4
                                    slp = pvslot % 2
                                    if pvslot >= 2:
                                        wait('tensor', sv,
                                             pv_cons[pvslot - 2])
                                    need = exp_marks[:qt + 1]
                                    svn = max([m for k2, m in need
                                               if k2 == 'v'], default=0)
                                    ssn = max([m for k2, m in need
                                               if k2 == 's'], default=0)
                                    if ssn:
                                        wait('tensor', ss, ssn)
                                    if svn:
                                        wait('tensor', sv, svn)
                                    tv = stn.bump()
                                    on('tensor', lambda tn, h=h, qt=qt,
                                       qt4=qt4, slp=slp, nk=qt + 1: [
                                        tn.matmul(
                                            ps_pv[:, slp, 0:65],
                                            pt[:, kt2,
                                               qt4 * 128:(qt4 + 1) * 128],
                                            v_perm[:, kt2,
                                                   h * 65:h * 65 + 65],
                                            start=(kt2 == 0),
                                            stop=(kt2 == nk - 1))
                                        for kt2 in range(nk)][-1]
                                       .then_inc(stn.h, 1))
                                    vv1 = sv.bump(2)
                                    on('vector', lambda v, h=h, qt=qt,
                                       slp=slp, tv=tv: (
                                        v.wait_ge(stn.h, tv),
                                        v.tensor_copy(
                                            out=attnv[:, qt,
                                                      h * 64:(h + 1) * 64],
                                            in_=ps_pv[:, slp, 0:64]),
                                        v.tensor_copy(
                                            out=dstage[:, h * NT + qt:
                                                       h * NT + qt + 1],
                                            in_=ps_pv[:, slp, 64:65])
                                        .then_inc(sv.h, 2)))
                                    pv_cons.append(vv1)
                                    pvslot += 1
                                    last_pv_tv = stn.v
                        sdr = ss.bump(2)
                        on('scalar', lambda s, vv=sv.v, sdr=sdr: (
                            s.wait_ge(sv.h, vv),
                            s.activation(out=drec[:], in_=dstage[:],
                                         func=AF.Ln, bias=0.0, scale=1.0)
                            .then_inc(ss.h, 1),
                            s.wait_ge(ss.h, sdr - 1),
                            s.activation(out=drec[:], in_=drec[:],
                                         func=AF.Exp, bias=0.0, scale=-1.0)
                            .then_inc(ss.h, 1),
                            s.wait_ge(ss.h, sdr)))

                        for m in ow_marks:
                            wmark('tensor', m)
                        tpo_done, oev_done, oproj_tv = [], [], []
                        oslot = 0
                        for qt in range(NT):
                            sm = None
                            for h in range(H):
                                sm = ss.bump()
                                on('scalar', lambda s, qt=qt, h=h, sm=sm,
                                   sdr=sdr,
                                   tpm=(tpo_done[qt - 2]
                                        if (h == 0 and qt >= 2) else None): (
                                    (s.wait_ge(ss.h, sdr)
                                     if (h == 0 and qt == 0) else None),
                                    (s.wait_ge(tpm[0].h, tpm[1])
                                     if tpm else None),
                                    s.activation(
                                        out=anorm[qt % 2][:,
                                                          h * 64:(h + 1) * 64],
                                        in_=attnv[:, qt, h * 64:(h + 1) * 64],
                                        func=AF.Identity, bias=0.0,
                                        scale=drec[:, h * NT + qt:
                                                   h * NT + qt + 1]),
                                    (s.drain() if h == H - 1 else s.nop())
                                    .then_inc(ss.h, 1)))
                            if qt >= 2:
                                wait('scalar', stn, oproj_tv[qt - 2])
                            lna = dtpb.next()
                            on('scalar', lambda s, qt=qt, sm=sm, lna=lna: (
                                s.wait_ge(ss.h, sm),
                                s.dma_start(out=avT[qt % 2][:],
                                            in_=anorm[qt % 2][:],
                                            transpose=True)
                                .then_inc(lna.h, 16)))
                            tpo_done.append((lna, lna.bump(16)))
                            for ch in range(2):
                                slo = oslot % 2
                                if oslot >= 2:
                                    wait('tensor', sv, oev_done[oslot - 2])
                                if ch == 0:
                                    wmark('tensor', tpo_done[qt])
                                tv = stn.bump()
                                on('tensor', lambda tn, qt=qt, ch=ch,
                                   slo=slo: [
                                    tn.matmul(
                                        ps_o[:, slo, :],
                                        avT[qt % 2][:, hp2, :],
                                        ow_sb[:, hp2, ch * 512:(ch + 1) * 512],
                                        start=(hp2 == 0), stop=(hp2 == 7))
                                    for hp2 in range(8)][-1]
                                   .then_inc(stn.h, 1))
                                vv = sv.bump()
                                on('vector', lambda v, qt=qt, ch=ch, slo=slo,
                                   tv=tv: (
                                    v.wait_ge(stn.h, tv),
                                    v.scalar_tensor_tensor(
                                        out=attn_out[:, qt,
                                                     ch * 512:(ch + 1) * 512],
                                        in0=ps_o[:, slo, :], scalar=0.0,
                                        in1=ob_t[:, ch * 512:(ch + 1) * 512],
                                        op0=ALU.add, op1=ALU.add)
                                    .then_inc(sv.h, 1)))
                                oev_done.append(vv)
                                oslot += 1
                            oproj_tv.append(stn.v)
                        wait('tensor', sv, oev_done[-1])
                        wait('tensor', sv, oev_done[-2])
                        if DEBUG_TAPS and b == DEBUG_TAPS_BATCH:
                            wait('sync', sv, sv.v)
                            dbl2 = [dxb.next() for _ in range(4)]
                            on('sync', lambda sync, dbl2=dbl2: (
                                sync.dma_start(out=dbg["ao"], in_=attn_out[:])
                                .then_inc(dbl2[0].h, 16),
                                sync.dma_start(out=dbg["pt"], in_=pt[:])
                                .then_inc(dbl2[1].h, 16),
                                sync.dma_start(out=dbg["dst"], in_=dstage[:])
                                .then_inc(dbl2[2].h, 16),
                                sync.dma_start(out=dbg["drc"], in_=drec[:])
                                .then_inc(dbl2[3].h, 16)))
                            for ln2 in dbl2:
                                wmark('sync', (ln2, ln2.bump(16)))
                    barrier()

                with ExitStack() as stf:
                    alloc_qs(stf, f"b{b}")
                    xqT = stf.enter_context(nc.sbuf_tensor(
                        f"xqT_{b}", [128, ND, S], BF16, is_transpose_dest=True))
                    xg = stf.enter_context(nc.sbuf_tensor(
                        f"xg_{b}", [128, 4, D], F32))
                    xnf = [stf.enter_context(nc.sbuf_tensor(
                        f"xnf{i}_{b}", [128, D], F32)) for i in range(2)]
                    xq_bf = [stf.enter_context(nc.sbuf_tensor(
                        f"xq_bf{i}_{b}", [128, D], BF16)) for i in range(2)]
                    hterm = [stf.enter_context(nc.sbuf_tensor(
                        f"hterm{i}_{b}", [128, 256], F32)) for i in range(2)]
                    silu_sb = [stf.enter_context(nc.sbuf_tensor(
                        f"silu{i}_{b}", [128, 256], F32)) for i in range(2)]
                    hq4 = stf.enter_context(nc.sbuf_tensor(
                        f"hq4_{b}", [128, 4, DI], BF16))
                    hqT = stf.enter_context(nc.sbuf_tensor(
                        f"hqT_{b}", [128, NI, 512], BF16,
                        is_transpose_dest=True))
                    wgu = [stf.enter_context(nc.sbuf_tensor(
                        f"wgu{i}_{b}", [128, ND, 512], BF16))
                        for i in range(2)]
                    wdt_sb = [stf.enter_context(nc.sbuf_tensor(
                        f"wdt_sb{i}_{b}", [128, 512], BF16))
                        for i in range(2)]
                    out_sb = [stf.enter_context(nc.sbuf_tensor(
                        f"out_sb{i}_{b}", [128, 512], F32)) for i in range(2)]
                    ps_gu = stf.enter_context(nc.psum_tensor(
                        f"ps_gu_{b}", [128, 4, 512], F32))
                    ps_d = stf.enter_context(nc.psum_tensor(
                        f"ps_d_{b}", [128, 4, 512], F32))
                    ps_tp = ps_d[:, 0, :].bitcast(BF16)

                    NCH = DI // 256
                    hq_tp_hist = []
                    xnf_qv = []
                    for g4 in range(4):
                        t0 = g4 * 4
                        for tt in range(4):
                            t = t0 + tt
                            lnb = dxb.next()
                            on('sync', lambda sync, t=t, tt=tt, lnb=lnb:
                               sync.dma_start(
                                out=xg[:, tt, :],
                                in_=x_d[b, t * 128:(t + 1) * 128, :])
                               .then_inc(lnb.h, 16))
                            xldt = (lnb, lnb.bump(16))
                            se = ss.bump()
                            on('scalar', lambda s, t=t, tt=tt, xldt=xldt,
                               gq=(xnf_qv[-2] if len(xnf_qv) >= 2 else 0): (
                                s.wait_ge(xldt[0].h, xldt[1]),
                                (s.wait_ge(sv.h, gq) if gq else None),
                                s.activation(out=xnf[tt % 2][:],
                                             in_=xg[:, tt, :],
                                             func=AF.Identity,
                                             bias=st_nmr[:, t:t + 1],
                                             scale=st_rstd[:, t:t + 1])
                                .then_inc(ss.h, 1)))
                            wait('vector', ss, se)
                            if len(xq_tp_hist) >= 2:
                                wait('vector', sv, xq_tp_hist[-2])
                            emit_quant(xnf[tt % 2][:], xq_bf[tt % 2][:], D)
                            qv = sv.v
                            xnf_qv.append(qv)
                            if xq_tp_hist:
                                wait('tensor', sv, xq_tp_hist[-1])
                            tpv = stn.bump()
                            on('tensor', lambda tn, tt=tt, qv=qv: (
                                tn.wait_ge(sv.h, qv),
                                [tn.transpose(
                                    ps_tp[:, d * 128:(d + 1) * 128],
                                    xq_bf[tt % 2][:, d * 128:(d + 1) * 128],
                                    ident[:]) for d in range(ND)][-1]
                                .then_inc(stn.h, 1)))
                            sv.bump()
                            on('vector', lambda v, t=t, tpv=tpv: (
                                v.wait_ge(stn.h, tpv),
                                v.tensor_copy(
                                    out=bass.AP(xqT, t * 128,
                                                [[ND * S, 128], [S, ND],
                                                 [1, 128]]),
                                    in_=ps_tp[:]).then_inc(sv.h, 1)))
                            xq_tp_hist.append(sv.v)
                        wait('tensor', sv, xq_tp_hist[-1])
                        if DEBUG_TAPS and b == DEBUG_TAPS_BATCH and g4 == 3:
                            wait('sync', sv, xq_tp_hist[-1])
                            lnd = dxb.next()
                            on('sync', lambda sync, lnd=lnd: sync.dma_start(
                                out=dbg["xqT"], in_=xqT[:])
                               .then_inc(lnd.h, 16))
                            lnd.bump(16)

                        wld = []
                        gu_cons = []
                        gslot = 0
                        for c in range(NCH):
                            if c >= 2:
                                wait('sync', stn, wld[c - 2])
                            lng = dxb.next()
                            lnu = dxb.next()
                            on('sync', lambda sync, c=c, lng=lng, lnu=lnu: (
                                sync.dma_start(
                                    out=wgu[c % 2][:, :, 0:256],
                                    in_=wgt_d[:, :, c * 256:(c + 1) * 256])
                                .then_inc(lng.h, 16),
                                sync.dma_start(
                                    out=wgu[c % 2][:, :, 256:512],
                                    in_=wut_d[:, :, c * 256:(c + 1) * 256])
                                .then_inc(lnu.h, 16)))
                            wldv = ((lng, lng.bump(16)), (lnu, lnu.bump(16)))
                            lastmm = None
                            for tt in range(4):
                                t = t0 + tt
                                sg0 = 2 * (gslot % 2)
                                if gslot >= 2:
                                    wait('tensor', sv, gu_cons[gslot - 2])
                                tv = stn.bump()
                                on('tensor', lambda tn, c=c, t=t, sg0=sg0,
                                   wldv=wldv: (
                                    tn.wait_ge(wldv[0][0].h, wldv[0][1]),
                                    tn.wait_ge(wldv[1][0].h, wldv[1][1]),
                                    [tn.matmul(
                                        ps_gu[:, sg0, 0:256],
                                        xqT[:, d, t * 128:(t + 1) * 128],
                                        wgu[c % 2][:, d, 0:256],
                                        start=(d == 0), stop=(d == ND - 1))
                                     for d in range(ND)],
                                    [tn.matmul(
                                        ps_gu[:, sg0 + 1, 0:256],
                                        xqT[:, d, t * 128:(t + 1) * 128],
                                        wgu[c % 2][:, d, 256:512],
                                        start=(d == 0), stop=(d == ND - 1))
                                     for d in range(ND)][-1]
                                    .then_inc(stn.h, 1)))
                                lastmm = stn.v
                                se = ss.bump()
                                on('scalar', lambda s, sg0=sg0, tv=tv,
                                   gsl=gslot % 2,
                                   gg=(gu_cons[gslot - 2]
                                       if gslot >= 2 else 0): (
                                    s.wait_ge(stn.h, tv),
                                    (s.wait_ge(sv.h, gg) if gg else None),
                                    s.activation(out=silu_sb[gsl][:],
                                                 in_=ps_gu[:, sg0, 0:256],
                                                 func=AF.Silu, bias=0.0,
                                                 scale=1.0)
                                    .then_inc(ss.h, 1)))
                                vv = sv.bump()
                                on('vector', lambda v, tt=tt, sg0=sg0,
                                   se=se, gsl=gslot % 2: (
                                    v.wait_ge(ss.h, se),
                                    v.tensor_tensor(
                                        out=hterm[tt % 2][:],
                                        in0=silu_sb[gsl][:],
                                        in1=ps_gu[:, sg0 + 1, 0:256],
                                        op=ALU.mult).then_inc(sv.h, 1)))
                                gu_cons.append(vv)
                                emit_quant(hterm[tt % 2][:],
                                           hq4[:, tt, c * 256:(c + 1) * 256],
                                           256)
                                gslot += 1
                            wld.append(lastmm)
                        hq_done = sv.v

                        for tt in range(4):
                            for i0 in (0, 8, 16):
                                ni = min(8, NI - i0)
                                if hq_tp_hist:
                                    wait('tensor', sv, hq_tp_hist[-1])
                                tpv = stn.bump()
                                on('tensor', lambda tn, tt=tt, i0=i0, ni=ni,
                                   hq_done=hq_done: (
                                    (tn.wait_ge(sv.h, hq_done)
                                     if (tt == 0 and i0 == 0) else None),
                                    [tn.transpose(
                                        ps_tp[:, k * 128:(k + 1) * 128],
                                        hq4[:, tt,
                                            (i0 + k) * 128:(i0 + k + 1) * 128],
                                        ident[:]) for k in range(ni)][-1]
                                    .then_inc(stn.h, 1)))
                                sv.bump()
                                on('vector', lambda v, tt=tt, i0=i0, ni=ni,
                                   tpv=tpv: (
                                    v.wait_ge(stn.h, tpv),
                                    v.tensor_copy(
                                        out=bass.AP(hqT,
                                                    i0 * 512 + tt * 128,
                                                    [[NI * 512, 128],
                                                     [512, ni], [1, 128]]),
                                        in_=ps_tp[:, 0:ni * 128])
                                    .then_inc(sv.h, 1)))
                                hq_tp_hist.append(sv.v)
                        wait('tensor', sv, hq_tp_hist[-1])
                        if DEBUG_TAPS and b == DEBUG_TAPS_BATCH and g4 == 1:
                            wait('sync', sv, hq_tp_hist[-1])
                            lnd2 = dxb.next()
                            on('sync', lambda sync, lnd2=lnd2: sync.dma_start(
                                out=dbg["hqT"], in_=hqT[:])
                               .then_inc(lnd2.h, 16))
                            lnd2.bump(16)

                        fin_marks = []
                        for ch in range(2):
                            wdld = []
                            for i in range(NI):
                                if i >= 2:
                                    wait('sync', stn, wdld[i - 2])
                                lnd3 = dxb.next()
                                on('sync', lambda sync, i=i, ch=ch, lnd3=lnd3:
                                   sync.dma_start(
                                    out=wdt_sb[i % 2][:],
                                    in_=wdt_d[:, i, ch * 512:(ch + 1) * 512])
                                   .then_inc(lnd3.h, 16))
                                wv = (lnd3, lnd3.bump(16))
                                if i == 0 and fin_marks:
                                    wait('tensor', sv, fin_marks[-1])
                                tv = stn.bump()
                                on('tensor', lambda tn, i=i, wv=wv: (
                                    tn.wait_ge(wv[0].h, wv[1]),
                                    [tn.matmul(
                                        ps_d[:, tt2, :],
                                        hqT[:, i, tt2 * 128:(tt2 + 1) * 128],
                                        wdt_sb[i % 2][:],
                                        start=(i == 0), stop=(i == NI - 1))
                                     for tt2 in range(4)][-1]
                                    .then_inc(stn.h, 1)))
                                wdld.append(stn.v)
                            for tt in range(4):
                                t = t0 + tt
                                vf = sv.bump(2)
                                on('vector', lambda v, t=t, tt=tt, ch=ch,
                                   wl=wdld[-1], qA=qs['A']: (
                                    v.wait_ge(stn.h, wl),
                                    v.scalar_tensor_tensor(
                                        out=qA[:, 0:512],
                                        in0=ps_d[:, tt, :], scalar=0.0,
                                        in1=attn_out[:, t,
                                                     ch * 512:(ch + 1) * 512],
                                        op0=ALU.add, op1=ALU.add),
                                    v.tensor_tensor(
                                        out=out_sb[tt % 2][:],
                                        in0=qA[:, 0:512],
                                        in1=xg[:, tt, ch * 512:(ch + 1) * 512],
                                        op=ALU.add),
                                    v.drain().then_inc(sv.h, 2)))
                                fin_marks.append(vf)
                                lny = dxb.next()
                                on('sync', lambda sync, t=t, tt=tt, ch=ch,
                                   vf=vf, lny=lny: (
                                    sync.wait_ge(sv.h, vf),
                                    sync.dma_start(
                                        out=y_d[b, t * 128:(t + 1) * 128,
                                                ch * 512:(ch + 1) * 512],
                                        in_=out_sb[tt % 2][:])
                                    .then_inc(lny.h, 16)))
                                out_hist.append((lny, lny.bump(16)))
                                if len(out_hist) >= 2:
                                    wmark('vector', out_hist[-2])
                        barrier()

        for b in range(B_PER_CORE):
            plan_batch(b)

        for (ln, vv) in dxb.totals() + dtpb.totals() + dcb.totals():
            wait('sync', ln, vv)
        barrier()

        blk = st.enter_context(nc.Block())
        for e in ENGINES:
            def make(e):
                def body(eng):
                    for fn in prog[e]:
                        fn(eng)
                return body
            getattr(blk, e)(make(e))

    return nc


_NC_CACHE = [None]


def kernel(L0_attn_x, L0_attn_Q_w, L0_attn_Q_b, L0_attn_K_w, L0_attn_K_b,
           L0_attn_V_w, L0_attn_V_b, L0_attn_O_w, L0_attn_O_b,
           L0_ffn_w_gate, L0_ffn_w_up, L0_ffn_w_down):
    if _NC_CACHE[0] is None:
        _NC_CACHE[0] = build_program()
    nc = _NC_CACHE[0]

    x = np.ascontiguousarray(np.asarray(L0_attn_x, dtype=np.float32))
    common = {
        "qw": np.ascontiguousarray(np.asarray(L0_attn_Q_w, np.float32)),
        "kw": np.ascontiguousarray(np.asarray(L0_attn_K_w, np.float32)),
        "vw": np.ascontiguousarray(np.asarray(L0_attn_V_w, np.float32)),
        "qb": np.ascontiguousarray(np.asarray(L0_attn_Q_b, np.float32)),
        "kb": np.ascontiguousarray(np.asarray(L0_attn_K_b, np.float32)),
        "vb": np.ascontiguousarray(np.asarray(L0_attn_V_b, np.float32)),
        "ow": np.ascontiguousarray(np.asarray(L0_attn_O_w, np.float32)),
        "ob": np.ascontiguousarray(np.asarray(L0_attn_O_b, np.float32))
        .reshape(1, D),
        "wg": np.ascontiguousarray(np.asarray(L0_ffn_w_gate, np.float32)),
        "wu": np.ascontiguousarray(np.asarray(L0_ffn_w_up, np.float32)),
        "wd": np.ascontiguousarray(np.asarray(L0_ffn_w_down, np.float32)),
    }
    in_maps = []
    for c in range(NCORES):
        m = dict(common)
        m["x"] = np.ascontiguousarray(x[c * B_PER_CORE:(c + 1) * B_PER_CORE])
        in_maps.append(m)

    res = run_bass_kernel_spmd(nc, in_maps, core_ids=list(range(NCORES)))
    out = np.concatenate([np.asarray(r["y"]) for r in res.results], axis=0)
    return out.astype(np.float32)


# version 13
# speedup vs baseline: 1.1309x; 1.0899x over previous
import numpy as np
from contextlib import ExitStack

import concourse.bass as bass
import concourse.mybir as mybir
from concourse.bass_utils import run_bass_kernel_spmd

F32 = mybir.dt.float32
BF16 = mybir.dt.bfloat16
I32 = mybir.dt.int32
ALU = mybir.AluOpType
AF = mybir.ActivationFunctionType
AX = mybir.AxisListType

NCORES = 8
B_PER_CORE = 2
S = 2048
D = 1024
H = 16
DH = 64
DI = 2816
NT = S // 128
ND = D // 128
NI = DI // 128
QC = 512
NQC = S // QC
EXPB = -20.0
ENGINES = ("sync", "gpsimd", "vector", "scalar", "tensor")
DEBUG_TAPS = False
DEBUG_TAPS_BATCH = 0


class Ctr:
    def __init__(self, stack, nc, name):
        self.h = stack.enter_context(nc.semaphore(name))
        self.v = 0

    def bump(self, n=1):
        self.v += n
        return self.v


class Lanes:

    def __init__(self, stack, nc, name, k=8):
        self.lanes = [Ctr(stack, nc, f"{name}{i}") for i in range(k)]
        self.i = 0

    def next(self):
        ln = self.lanes[self.i % len(self.lanes)]
        self.i += 1
        return ln

    def totals(self):
        return [(ln, ln.v) for ln in self.lanes if ln.v > 0]


def build_program():
    nc = bass.Bass("TRN2", target_bir_lowering=False,
                   detect_race_conditions=False)

    x_d = nc.dram_tensor("x", [B_PER_CORE, S, D], F32, kind="ExternalInput").ap()
    qw_d = nc.dram_tensor("qw", [H, D, DH], F32, kind="ExternalInput").ap()
    kw_d = nc.dram_tensor("kw", [H, D, DH], F32, kind="ExternalInput").ap()
    vw_d = nc.dram_tensor("vw", [H, D, DH], F32, kind="ExternalInput").ap()
    qb_d = nc.dram_tensor("qb", [H, DH], F32, kind="ExternalInput").ap()
    kb_d = nc.dram_tensor("kb", [H, DH], F32, kind="ExternalInput").ap()
    vb_d = nc.dram_tensor("vb", [H, DH], F32, kind="ExternalInput").ap()
    ow_d = nc.dram_tensor("ow", [H, DH, D], F32, kind="ExternalInput").ap()
    ob_d = nc.dram_tensor("ob", [1, D], F32, kind="ExternalInput").ap()
    wg_d = nc.dram_tensor("wg", [DI, D], F32, kind="ExternalInput").ap()
    wu_d = nc.dram_tensor("wu", [DI, D], F32, kind="ExternalInput").ap()
    wd_d = nc.dram_tensor("wd", [D, DI], F32, kind="ExternalInput").ap()
    y_d = nc.dram_tensor("y", [B_PER_CORE, S, D], F32, kind="ExternalOutput").ap()
    wkind = "ExternalOutput" if DEBUG_TAPS else "Internal"
    wgt_d = nc.dram_tensor("wgt", [128, ND, DI], BF16, kind=wkind).ap()
    wut_d = nc.dram_tensor("wut", [128, ND, DI], BF16, kind=wkind).ap()
    wdt_d = nc.dram_tensor("wdt", [128, NI, D], BF16, kind=wkind).ap()
    dbg = {}
    if DEBUG_TAPS:
        for nm, shp, dt2 in (("xnT", [128, ND, S], BF16),
                             ("qt", [128, 8, S], BF16),
                             ("kt", [128, 8, S], BF16),
                             ("vp", [128, NT, H * 65], BF16),
                             ("ao", [128, NT, D], BF16),
                             ("pt", [128, NT, QC], BF16),
                             ("dst", [128, H * NT], F32),
                             ("drc", [128, H * NT], F32),
                             ("rstd", [128, NT], F32),
                             ("nmr", [128, NT], F32),
                             ("xqT", [128, ND, S], BF16),
                             ("hqT", [128, NI, 512], BF16)):
            dbg[nm] = nc.dram_tensor("dbg_" + nm, shp, dt2,
                                     kind="ExternalOutput").ap()

    prog = {e: [] for e in ENGINES}

    def on(eng, fn):
        prog[eng].append(fn)

    QW = 1024

    with ExitStack() as st:
        def sb(name, shape, dtype, **kw):
            return st.enter_context(nc.sbuf_tensor(name, shape, dtype, **kw))

        neg20 = sb("neg20", [128, 1], F32)
        masks = sb("masks", [128, 4, QC], BF16)
        qb_sb = sb("qb_sb", [128, 8], F32)
        kb_sb = sb("kb_sb", [128, 8], F32)
        st_rstd = sb("st_rstd", [128, NT], F32)
        st_nmr = sb("st_nmr", [128, NT], F32)
        ident = sb("ident", [128, 128], BF16)
        qs = {}

        def alloc_qs(scope, tag):
            qs.clear()
            for k in ("A", "B", "C", "D"):
                qs[k] = scope.enter_context(
                    nc.sbuf_tensor(f"qs_{k}_{tag}", [128, QW], F32))
            qs['E'] = scope.enter_context(
                nc.sbuf_tensor(f"qs_E_{tag}", [128, QW], I32))
            for k in ("bm", "scale", "lns", "rsh", "sc2"):
                qs[k] = scope.enter_context(
                    nc.sbuf_tensor(f"qs_{k}_{tag}", [128, QW // 16], F32))

        sv = Ctr(st, nc, "sv")
        ss = Ctr(st, nc, "ss")
        stn = Ctr(st, nc, "stn")
        sg = Ctr(st, nc, "sg")
        dxb = Lanes(st, nc, "dx", 8)
        dtpb = Lanes(st, nc, "dtp", 8)
        dcb = Lanes(st, nc, "dc", 8)
        bar = Ctr(st, nc, "bar")
        nbar = [0]

        def barrier():
            nbar[0] += 1
            k = nbar[0]
            for e in ENGINES:
                on(e, lambda eng, k=k: (eng.drain().then_inc(bar.h, 1),
                                        eng.wait_ge(bar.h, 5 * k)))

        def wait(eng, ctr, val):
            if val > 0:
                on(eng, lambda E, c=ctr, vv=val: E.wait_ge(c.h, vv))

        def wmark(eng, mark):
            if mark:
                c, vv = mark
                on(eng, lambda E, c=c, vv=vv: E.wait_ge(c.h, vv))

        def emit_quant(x_ap, out_ap, F, drain_out=False):
            nb = F // 16
            xv = x_ap.rearrange("p (b w) -> p b w", w=16)
            A, Bb, C, Dt, E = (qs['A'][:, 0:F], qs['B'][:, 0:F],
                               qs['C'][:, 0:F], qs['D'][:, 0:F],
                               qs['E'][:, 0:F])
            bm, scale, lns, rsh, sc2 = (qs[k][:, 0:nb] for k in
                                        ("bm", "scale", "lns", "rsh", "sc2"))
            v1 = sv.bump()
            on('vector', lambda v, xv=xv, bm=bm, v1=v1: (
                v.tensor_reduce(out=bm, in_=xv, axis=AX.X, op=ALU.max,
                                apply_absolute_value=True).then_inc(sv.h, 1),
                v.wait_ge(sv.h, v1)))
            v2 = sv.bump()
            on('vector', lambda v, bm=bm, scale=scale: (
                v.tensor_scalar(
                    out=scale, in0=bm, scalar1=1.0 / 6.0, scalar2=1e-12,
                    op0=ALU.mult, op1=ALU.max).then_inc(sv.h, 1)))
            s1 = ss.bump()
            on('scalar', lambda s, scale=scale, lns=lns, rsh=rsh, sc2=sc2,
               v2=v2: (
                s.wait_ge(sv.h, v2),
                s.activation(out=lns, in_=scale, func=AF.Ln, bias=0.0,
                             scale=2.0),
                s.activation(out=rsh, in_=lns, func=AF.Exp, bias=0.0,
                             scale=-1.0),
                s.mul(sc2, scale, 0.5).then_inc(ss.h, 1)))
            lns2 = qs['lns'][:, 0:nb]
            scale2 = qs['scale'][:, 0:nb]
            nr1 = sv.bump(3)
            vend = sv.bump()

            bm2 = qs['bm'][:, 0:nb]

            def main(v, xv=xv, A=A, Bb=Bb, C=C, Dt=Dt, E=E, rsh=rsh,
                     sc2=sc2, out_ap=out_ap, nb=nb, s1=s1, lns2=lns2,
                     scale2=scale2, nr1=nr1, bm2=bm2, drain_out=drain_out):
                v.wait_ge(ss.h, s1)
                v.tensor_tensor(out=lns2, in0=scale2, in1=rsh,
                                op=ALU.mult).then_inc(sv.h, 1)
                v.wait_ge(sv.h, nr1 - 2)
                v.tensor_scalar(out=bm2, in0=lns2, scalar1=-2.0, scalar2=2.0,
                                op0=ALU.mult, op1=ALU.add).then_inc(sv.h, 1)
                v.wait_ge(sv.h, nr1 - 1)
                v.tensor_tensor(out=lns2, in0=rsh, in1=bm2,
                                op=ALU.mult).then_inc(sv.h, 1)
                v.wait_ge(sv.h, nr1)
                rsh_b = lns2.unsqueeze(-1).broadcast_to([128, nb, 16])
                sc2_b = sc2.unsqueeze(-1).broadcast_to([128, nb, 16])
                v.tensor_tensor(out=A.rearrange("p (b w) -> p b w", w=16),
                                in0=xv, in1=rsh_b, op=ALU.mult)
                v.tensor_scalar(out=Bb, in0=A, scalar1=1.0, scalar2=-1.0,
                                op0=ALU.min, op1=ALU.max)
                v.tensor_scalar(out=C, in0=A, scalar1=2.0, scalar2=-2.0,
                                op0=ALU.min, op1=ALU.max)
                v.scalar_tensor_tensor(out=Dt, in0=Bb, scalar=2.0, in1=A,
                                       op0=ALU.mult, op1=ALU.add)
                v.tensor_tensor(out=Bb, in0=Dt, in1=C, op=ALU.add)
                v.tensor_scalar(out=E, in0=Bb, scalar1=8.0, scalar2=0.0,
                                op0=ALU.add, op1=ALU.add)
                v.scalar_tensor_tensor(out=Dt, in0=E, scalar=-7.5, in1=Bb,
                                       op0=ALU.add, op1=ALU.is_lt)
                v.tensor_tensor(out=C, in0=Dt, in1=E, op=ALU.add)
                v.tensor_scalar(out=A, in0=C, scalar1=12.0, scalar2=4.0,
                                op0=ALU.min, op1=ALU.max)
                v.tensor_scalar(out=Dt, in0=C, scalar1=14.0, scalar2=2.0,
                                op0=ALU.min, op1=ALU.max)
                v.scalar_tensor_tensor(out=Bb, in0=C, scalar=2.0, in1=Dt,
                                       op0=ALU.mult, op1=ALU.subtract)
                v.scalar_tensor_tensor(out=C, in0=Bb, scalar=2.0, in1=A,
                                       op0=ALU.mult, op1=ALU.subtract)
                fin = v.scalar_tensor_tensor(
                    out=out_ap.rearrange("p (b w) -> p b w", w=16),
                    in0=C.rearrange("p (b w) -> p b w", w=16), scalar=-8.0,
                    in1=sc2_b, op0=ALU.add, op1=ALU.mult)
                if drain_out:
                    v.drain().then_inc(sv.h, 1)
                else:
                    fin.then_inc(sv.h, 1)

            on('vector', main)
            return vend

        with ExitStack() as st0:
            iota_t = st0.enter_context(nc.sbuf_tensor("iota_t", [128, QC], F32))
            g1 = sg.bump()
            on('gpsimd', lambda g: (
                g.memset(neg20[:], EXPB),
                g.iota(iota_t[:], [[1, QC]], base=0, channel_multiplier=-1,
                       allow_small_or_imprecise_dtypes=True)
                .then_inc(sg.h, 1)))
            su_l = [dxb.next() for _ in range(2)]
            on('sync', lambda sync, su_l=su_l: (
                sync.dma_start(out=qb_sb[:],
                               in_=bass.AP(qb_d.tensor, 0,
                                           [[1, 128], [128, 8]]),
                               allow_slow_non_contiguous=True)
                .then_inc(su_l[0].h, 16),
                sync.dma_start(out=kb_sb[:],
                               in_=bass.AP(kb_d.tensor, 0,
                                           [[1, 128], [128, 8]]),
                               allow_slow_non_contiguous=True)
                .then_inc(su_l[1].h, 16)))
            setup_marks = [(ln, ln.bump(16)) for ln in su_l]
            wait('vector', sg, g1)
            sv.bump()
            on('vector', lambda v: v.tensor_scalar(
                out=ident[:], in0=iota_t[:, 0:128], scalar1=0.0, scalar2=0.0,
                op0=ALU.is_equal, op1=ALU.add).then_inc(sv.h, 1))
            for m in range(4):
                sv.bump()
                on('vector', lambda v, m=m: v.tensor_scalar(
                    out=masks[:, m, :], in0=iota_t[:], scalar1=float(128 * m),
                    scalar2=0.0, op0=ALU.is_ge, op1=ALU.add)
                   .then_inc(sv.h, 1))
            masks_done = sv.v

            alloc_qs(st0, "w")
            w_in = [st0.enter_context(nc.sbuf_tensor(f"w_in{i}", [128, DI],
                                                     F32)) for i in range(2)]
            w_bf = [st0.enter_context(nc.sbuf_tensor(f"w_bf{i}", [128, DI],
                                                     BF16)) for i in range(2)]
            w_tp = [st0.enter_context(nc.sbuf_tensor(
                f"w_tp{i}", [128, NI, 128], BF16, is_transpose_dest=True))
                for i in range(2)]

            tiles = []
            for (src, rows, cols, dstT) in ((wg_d, DI, D, wgt_d),
                                            (wu_d, DI, D, wut_d),
                                            (wd_d, D, DI, wdt_d)):
                for t in range(rows // 128):
                    tiles.append((src, cols, dstT, t))

            vq_done, tp_done, stw_done, ld_done = [], [], [], []
            for k, (src, cols, dstT, t) in enumerate(tiles):
                nsub = cols // 128
                if k >= 2:
                    wait('sync', sv, vq_done[k - 2])
                lnl = dxb.next()
                on('sync', lambda sync, src=src, t=t, cols=cols, k=k, lnl=lnl:
                   sync.dma_start(out=w_in[k % 2][:, 0:cols],
                                  in_=src[t * 128:(t + 1) * 128, :])
                   .then_inc(lnl.h, 16))
                ld_done.append((lnl, lnl.bump(16)))
                wmark('vector', ld_done[k])
                if k >= 2:
                    wmark('vector', tp_done[k - 2])
                for piece in range(0, cols, QW):
                    pw = min(QW, cols - piece)
                    emit_quant(w_in[k % 2][:, piece:piece + pw],
                               w_bf[k % 2][:, piece:piece + pw], pw,
                               drain_out=True)
                vq_done.append(sv.v)
                wait('scalar', sv, vq_done[k])
                if k >= 2:
                    wmark('scalar', stw_done[k - 2])
                lnt = dtpb.next()
                on('scalar', lambda s, k=k, cols=cols, nsub=nsub, lnt=lnt:
                   s.dma_start(out=w_tp[k % 2][:, 0:nsub, :],
                               in_=w_bf[k % 2][:, 0:cols], transpose=True)
                   .then_inc(lnt.h, 16))
                tp_done.append((lnt, lnt.bump(16)))
                wmark('sync', tp_done[k])
                lns_ = dxb.next()
                on('sync', lambda sync, dstT=dstT, t=t, nsub=nsub, k=k,
                   lns_=lns_:
                   sync.dma_start(out=dstT[:, 0:nsub, t * 128:(t + 1) * 128],
                                  in_=w_tp[k % 2][:, 0:nsub, :])
                   .then_inc(lns_.h, 16))
                stw_done.append((lns_, lns_.bump(16)))

            for (ln, vv) in dxb.totals() + dtpb.totals():
                wait('sync', ln, vv)
            for m in setup_marks:
                wmark('sync', m)
            wait('vector', sv, masks_done)
            barrier()

        xq_tp_hist = []
        out_hist = []

        def plan_batch(b):
            with ExitStack() as stb:
                attn_out = stb.enter_context(nc.sbuf_tensor(
                    f"attn_out_{b}", [128, NT, D], BF16,
                    is_transpose_dest=True))
                xnT = attn_out[:].rearrange("p a c -> p (a c)") \
                    .rearrange("p (u w) -> p u w", u=ND)
                attnv = attn_out[:]

                with ExitStack() as sta:
                    qt_all = sta.enter_context(nc.sbuf_tensor(
                        f"qt_all_{b}", [128, 8, S], BF16))
                    kt_all = sta.enter_context(nc.sbuf_tensor(
                        f"kt_all_{b}", [128, 8, S], BF16))
                    v_perm = sta.enter_context(nc.sbuf_tensor(
                        f"v_perm_{b}", [128, NT, H * 65], BF16))
                    pt = sta.enter_context(nc.sbuf_tensor(
                        f"pt_{b}", [128, NT, QC], BF16))
                    ob_t = sta.enter_context(nc.sbuf_tensor(
                        f"ob_t_{b}", [128, D], F32))
                    vb_t = sta.enter_context(nc.sbuf_tensor(
                        f"vb_t_{b}", [128, D], F32))
                    bt_l = [dxb.next() for _ in range(2)]
                    on('sync', lambda sync, bt_l=bt_l: (
                        sync.dma_start(out=ob_t[:],
                                       in_=ob_d.broadcast_to([128, D]))
                        .then_inc(bt_l[0].h, 16),
                        sync.dma_start(
                            out=vb_t[:],
                            in_=vb_d.rearrange("h d -> (h d)").unsqueeze(0)
                            .broadcast_to([128, D])).then_inc(bt_l[1].h, 16)))
                    bias_marks = [(ln, ln.bump(16)) for ln in bt_l]

                    with ExitStack() as stp:
                        x_t = [stp.enter_context(nc.sbuf_tensor(
                            f"x_t{i}_{b}", [128, D], F32)) for i in range(2)]
                        xsq = stp.enter_context(nc.sbuf_tensor(
                            f"xsq_{b}", [128, D], F32))
                        xn_bf = [stp.enter_context(nc.sbuf_tensor(
                            f"xn_bf{i}_{b}", [128, D], BF16))
                            for i in range(2)]
                        sm1 = stp.enter_context(nc.sbuf_tensor(
                            f"sm1_{b}", [128, 1], F32))
                        sm2 = stp.enter_context(nc.sbuf_tensor(
                            f"sm2_{b}", [128, 1], F32))
                        msq = stp.enter_context(nc.sbuf_tensor(
                            f"msq_{b}", [128, 1], F32))
                        var = stp.enter_context(nc.sbuf_tensor(
                            f"var_{b}", [128, 1], F32))
                        lnv = stp.enter_context(nc.sbuf_tensor(
                            f"lnv_{b}", [128, 1], F32))
                        mmean = stp.enter_context(nc.sbuf_tensor(
                            f"mmean_{b}", [128, 1], F32))
                        wbuf = [stp.enter_context(nc.sbuf_tensor(
                            f"wbuf{i}_{b}", [128, ND, 1024], BF16))
                            for i in range(2)]

                        ones_done = sv.bump()
                        on('vector', lambda v: v.memset(
                            bass.AP(v_perm, 64,
                                    [[NT * H * 65, 128], [H * 65, NT],
                                     [65, H]]), 1.0).then_inc(sv.h, 1))

                        tp_x_done = []
                        ln_s3 = []
                        for t in range(NT):
                            if t >= 2:
                                wait('sync', ss, ln_s3[t - 2])
                            lnx = dxb.next()
                            on('sync', lambda sync, t=t, lnx=lnx:
                               sync.dma_start(
                                out=x_t[t % 2][:],
                                in_=x_d[b, t * 128:(t + 1) * 128, :])
                               .then_inc(lnx.h, 16))
                            xldt = (lnx, lnx.bump(16))
                            v1 = sv.bump()
                            on('vector', lambda v, t=t, v1=v1, xldt=xldt: (
                                v.wait_ge(xldt[0].h, xldt[1]),
                                v.tensor_reduce(out=sm1[:], in_=x_t[t % 2][:],
                                                axis=AX.X, op=ALU.add)
                                .then_inc(sv.h, 1),
                                v.wait_ge(sv.h, v1)))
                            s1 = ss.bump()
                            on('scalar', lambda s, t=t, xldt=xldt: (
                                s.wait_ge(xldt[0].h, xldt[1]),
                                s.square(xsq[:], x_t[t % 2][:])
                                .then_inc(ss.h, 1)))
                            v2 = sv.bump()
                            on('vector', lambda v, s1=s1, v2=v2: (
                                v.wait_ge(ss.h, s1),
                                v.tensor_reduce(out=sm2[:], in_=xsq[:],
                                                axis=AX.X, op=ALU.add)
                                .then_inc(sv.h, 1),
                                v.wait_ge(sv.h, v2)))
                            v3 = sv.bump(4)
                            on('vector', lambda v, v3=v3: (
                                v.tensor_scalar(out=mmean[:], in0=sm1[:],
                                                scalar1=1.0 / D, scalar2=0.0,
                                                op0=ALU.mult, op1=ALU.add)
                                .then_inc(sv.h, 1),
                                v.wait_ge(sv.h, v3 - 3),
                                v.tensor_tensor(out=msq[:], in0=mmean[:],
                                                in1=sm1[:], op=ALU.mult)
                                .then_inc(sv.h, 1),
                                v.wait_ge(sv.h, v3 - 2),
                                v.tensor_tensor(out=var[:], in0=sm2[:],
                                                in1=msq[:], op=ALU.subtract)
                                .then_inc(sv.h, 1),
                                v.wait_ge(sv.h, v3 - 1),
                                v.tensor_scalar(out=var[:], in0=var[:],
                                                scalar1=1.0 / D, scalar2=1e-5,
                                                op0=ALU.mult, op1=ALU.add)
                                .then_inc(sv.h, 1),
                                v.wait_ge(sv.h, v3)))
                            s2 = ss.bump()
                            on('scalar', lambda s, t=t, v3=v3: (
                                s.wait_ge(sv.h, v3),
                                s.activation(out=lnv[:], in_=var[:],
                                             func=AF.Ln, bias=0.0, scale=1.0),
                                s.activation(out=st_rstd[:, t:t + 1],
                                             in_=lnv[:], func=AF.Exp,
                                             bias=0.0, scale=-0.5)
                                .then_inc(ss.h, 1)))
                            v5 = sv.bump()
                            on('vector', lambda v, t=t, s2=s2: (
                                v.wait_ge(ss.h, s2),
                                v.scalar_tensor_tensor(
                                    out=st_nmr[:, t:t + 1], in0=mmean[:],
                                    scalar=-1.0, in1=st_rstd[:, t:t + 1],
                                    op0=ALU.mult, op1=ALU.mult)
                                .then_inc(sv.h, 1)))
                            s3 = ss.bump()
                            on('scalar', lambda s, t=t, v5=v5,
                               tpm=(tp_x_done[t - 2] if t >= 2 else None): (
                                s.wait_ge(sv.h, v5),
                                (s.wait_ge(tpm[0].h, tpm[1])
                                 if tpm else None),
                                s.activation(out=xn_bf[t % 2][:],
                                             in_=x_t[t % 2][:], func=AF.Identity,
                                             bias=st_nmr[:, t:t + 1],
                                             scale=st_rstd[:, t:t + 1]),
                                s.drain().then_inc(ss.h, 1)))
                            ln_s3.append(s3)
                            lnxt = dtpb.next()
                            on('scalar', lambda s, t=t, s3=s3, lnxt=lnxt: (
                                s.wait_ge(ss.h, s3),
                                s.dma_start(
                                    out=xnT[:, :, t * 128:(t + 1) * 128],
                                    in_=xn_bf[t % 2][:], transpose=True)
                                .then_inc(lnxt.h, 16)))
                            tp_x_done.append((lnxt, lnxt.bump(16)))

                        ps_qkv = stp.enter_context(nc.psum_tensor(
                            f"ps_qkv_{b}", [128, 2, 512], F32))

                        def load_qk(j, wdram):
                            marks = {}
                            for hp in range(8):
                                for hf in range(2):
                                    lnw = dcb.next()
                                    on('gpsimd', lambda g, wdram=wdram,
                                       wsb=wbuf[j % 2], hp=hp, hf=hf, lnw=lnw:
                                       g.dma_start(
                                        out=bass.AP(wsb, hp * 128 + hf * 64,
                                                    [[ND * 1024, 128],
                                                     [1024, ND], [1, 64]]),
                                        in_=bass.AP(wdram.tensor,
                                                    (2 * hp + hf) * D * DH,
                                                    [[64, 128],
                                                     [128 * 64, ND],
                                                     [1, 64]]))
                                       .then_inc(lnw.h, 16))
                                    marks[id(lnw)] = (lnw, lnw.bump(16))
                            return list(marks.values())

                        wl_q = load_qk(0, qw_d)
                        wl_k = load_qk(1, kw_d)

                        wmark('tensor', tp_x_done[-1])
                        slot = 0
                        ev_done = []
                        wuse = {}
                        for j, (bsb, dst) in enumerate(((qb_sb, qt_all),
                                                       (kb_sb, kt_all))):
                            for m in (wl_q if j == 0 else wl_k):
                                wmark('tensor', m)
                            for hp in range(8):
                                for c in range(4):
                                    sl = slot % 2
                                    if slot >= 2:
                                        e, m = ev_done[slot - 2]
                                        wait('tensor', ss if e == 'ss'
                                             else sv, m)
                                    tv = stn.bump()
                                    on('tensor', lambda tn, j=j, hp=hp, c=c,
                                       sl=sl: [
                                        tn.matmul(
                                            ps_qkv[:, sl, :],
                                            wbuf[j % 2][:, d,
                                                        hp * 128:(hp + 1) * 128],
                                            xnT[:, d, c * 512:(c + 1) * 512],
                                            start=(d == 0),
                                            stop=(d == ND - 1))
                                        for d in range(ND)][-1]
                                       .then_inc(stn.h, 1))
                                    se = ss.bump()
                                    on('scalar', lambda s, bsb=bsb, dst=dst,
                                       hp=hp, c=c, sl=sl, tv=tv: (
                                        s.wait_ge(stn.h, tv),
                                        s.activation(
                                            out=dst[:, hp,
                                                    c * 512:(c + 1) * 512],
                                            in_=ps_qkv[:, sl, :],
                                            func=AF.Identity,
                                            bias=bsb[:, hp:hp + 1],
                                            scale=1.0).then_inc(ss.h, 1)))
                                    ev_done.append(('ss', se))
                                    slot += 1
                            wuse[j] = stn.v
                        wait('gpsimd', stn, wuse[0])
                        vmarks = {}
                        for h2 in range(H):
                            lnv2 = dcb.next()
                            on('gpsimd', lambda g, h2=h2, lnv2=lnv2:
                               g.dma_start(
                                out=bass.AP(wbuf[0], h2 * 64,
                                            [[ND * 1024, 128], [1024, ND],
                                             [1, 64]]),
                                in_=bass.AP(vw_d.tensor, h2 * D * DH,
                                            [[64, 128], [128 * 64, ND],
                                             [1, 64]]))
                               .then_inc(lnv2.h, 16))
                            vmarks[id(lnv2)] = (lnv2, lnv2.bump(16))
                        for m in vmarks.values():
                            wmark('tensor', m)
                        wmark('vector', bias_marks[1])
                        wait('vector', sv, ones_done)
                        for t in range(NT):
                            for c in range(2):
                                sl = slot % 2
                                if slot >= 2:
                                    e, m = ev_done[slot - 2]
                                    wait('tensor', ss if e == 'ss' else sv, m)
                                tv = stn.bump()
                                on('tensor', lambda tn, t=t, c=c, sl=sl: [
                                    tn.matmul(
                                        ps_qkv[:, sl, :],
                                        xnT[:, d, t * 128:(t + 1) * 128],
                                        wbuf[0][:, d, c * 512:(c + 1) * 512],
                                        start=(d == 0), stop=(d == ND - 1))
                                    for d in range(ND)][-1]
                                   .then_inc(stn.h, 1))
                                vvl = None
                                for hh in range(8):
                                    h = c * 8 + hh
                                    vvl = sv.bump()
                                    on('vector', lambda v, t=t, sl=sl, hh=hh,
                                       h=h, tv=tv: (
                                        (v.wait_ge(stn.h, tv)
                                         if hh == 0 else None),
                                        v.scalar_tensor_tensor(
                                            out=v_perm[:, t,
                                                       h * 65:h * 65 + 64],
                                            in0=ps_qkv[:, sl,
                                                       hh * 64:(hh + 1) * 64],
                                            scalar=0.0,
                                            in1=vb_t[:, h * 64:(h + 1) * 64],
                                            op0=ALU.add, op1=ALU.add)
                                        .then_inc(sv.h, 1)))
                                ev_done.append(('sv', vvl))
                                slot += 1
                        for (e, m) in ev_done[-2:]:
                            wait('tensor', ss if e == 'ss' else sv, m)
                        barrier()
                        if DEBUG_TAPS and b == DEBUG_TAPS_BATCH:
                            dbl = [dxb.next() for _ in range(6)]
                            on('sync', lambda sync, dbl=dbl: (
                                sync.dma_start(out=dbg["xnT"],
                                               in_=attn_out[:].rearrange(
                                                   "p a c -> p (a c)")
                                               .rearrange("p (u w) -> p u w",
                                                          u=ND))
                                .then_inc(dbl[0].h, 16),
                                sync.dma_start(out=dbg["qt"], in_=qt_all[:])
                                .then_inc(dbl[1].h, 16),
                                sync.dma_start(out=dbg["kt"], in_=kt_all[:])
                                .then_inc(dbl[2].h, 16),
                                sync.dma_start(out=dbg["vp"], in_=v_perm[:])
                                .then_inc(dbl[3].h, 16),
                                sync.dma_start(out=dbg["rstd"], in_=st_rstd[:])
                                .then_inc(dbl[4].h, 16),
                                sync.dma_start(out=dbg["nmr"], in_=st_nmr[:])
                                .then_inc(dbl[5].h, 16)))
                            for ln2 in dbl:
                                wmark('sync', (ln2, ln2.bump(16)))
                            barrier()

                    with ExitStack() as sta3:
                        esc = sta3.enter_context(nc.sbuf_tensor(
                            f"esc_{b}", [128, QC], BF16))
                        dstage = sta3.enter_context(nc.sbuf_tensor(
                            f"dstage_{b}", [128, H * NT], F32))
                        drec = sta3.enter_context(nc.sbuf_tensor(
                            f"drec_{b}", [128, H * NT], F32))
                        ow_sb = sta3.enter_context(nc.sbuf_tensor(
                            f"ow_sb_{b}", [128, 8, D], BF16))
                        anorm = [sta3.enter_context(nc.sbuf_tensor(
                            f"anorm{i}_{b}", [128, D], BF16))
                            for i in range(2)]
                        avT = [sta3.enter_context(nc.sbuf_tensor(
                            f"avT{i}_{b}", [128, 8, 128], BF16,
                            is_transpose_dest=True)) for i in range(2)]
                        ps_l = sta3.enter_context(nc.psum_tensor(
                            f"ps_l_{b}", [128, 2, 512], F32))
                        ps_pv = sta3.enter_context(nc.psum_tensor(
                            f"ps_pv_{b}", [128, 2, 512], F32))
                        ps_o = sta3.enter_context(nc.psum_tensor(
                            f"ps_o_{b}", [128, 2, 512], F32))

                        ow_marks = []
                        for half in range(2):
                            lno = dcb.next()
                            on('gpsimd', lambda g, half=half, lno=lno:
                               g.dma_start(
                                out=ow_sb[half * 64:(half + 1) * 64, :, :],
                                in_=bass.AP(ow_d.tensor, half * DH * D,
                                            [[D, 64], [2 * DH * D, 8],
                                             [1, D]])).then_inc(lno.h, 16))
                            ow_marks.append((lno, lno.bump(16)))

                        lslot, pvslot = 0, 0
                        l_cons, pv_cons = [], []
                        last_pv_tv = 0
                        for h in range(H):
                            hp, po = h // 2, (h % 2) * 64
                            for c in range(NQC):
                                nkt = 4 * (c + 1)
                                exp_marks = []
                                if last_pv_tv:
                                    wait('scalar', stn, last_pv_tv)
                                    wait('vector', stn, last_pv_tv)
                                for kt in range(nkt):
                                    slq = lslot % 2
                                    if lslot >= 2:
                                        wait('tensor', ss, l_cons[lslot - 2])
                                    tv = stn.bump()
                                    on('tensor', lambda tn, hp=hp, po=po,
                                       kt=kt, c=c, slq=slq: tn.matmul(
                                        ps_l[:, slq, :],
                                        kt_all[po:po + 64, hp,
                                               kt * 128:(kt + 1) * 128],
                                        qt_all[po:po + 64, hp,
                                               c * QC:(c + 1) * QC],
                                        start=True, stop=True)
                                       .then_inc(stn.h, 1))
                                    diag = (kt >= nkt - 4)
                                    se = ss.bump()
                                    if not diag:
                                        on('scalar', lambda s, kt=kt, slq=slq,
                                           tv=tv: (
                                            s.wait_ge(stn.h, tv),
                                            s.activation(
                                                out=pt[:, kt, :],
                                                in_=ps_l[:, slq, :],
                                                func=AF.Exp, bias=neg20[:],
                                                scale=0.125)
                                            .then_inc(ss.h, 1)))
                                        exp_marks.append(('s', se))
                                    else:
                                        on('scalar', lambda s, slq=slq,
                                           tv=tv: (
                                            s.wait_ge(stn.h, tv),
                                            s.activation(
                                                out=esc[:],
                                                in_=ps_l[:, slq, :],
                                                func=AF.Exp, bias=neg20[:],
                                                scale=0.125)
                                            .then_inc(ss.h, 1)))
                                        mv = sv.bump()
                                        mi = kt - (nkt - 4)
                                        on('vector', lambda v, kt=kt, mi=mi,
                                           se=se: (
                                            v.wait_ge(ss.h, se),
                                            v.tensor_tensor(
                                                out=pt[:, kt, :],
                                                in0=esc[:],
                                                in1=masks[:, mi, :],
                                                op=ALU.mult)
                                            .then_inc(sv.h, 1)))
                                        exp_marks.append(('v', mv))
                                        wait('scalar', sv, mv)
                                    l_cons.append(se)
                                    lslot += 1
                                for qt4 in range(4):
                                    qt = c * 4 + qt4
                                    slp = pvslot % 2
                                    if pvslot >= 2:
                                        wait('tensor', sv,
                                             pv_cons[pvslot - 2])
                                    need = exp_marks[:qt + 1]
                                    svn = max([m for k2, m in need
                                               if k2 == 'v'], default=0)
                                    ssn = max([m for k2, m in need
                                               if k2 == 's'], default=0)
                                    if ssn:
                                        wait('tensor', ss, ssn)
                                    if svn:
                                        wait('tensor', sv, svn)
                                    tv = stn.bump()
                                    on('tensor', lambda tn, h=h, qt=qt,
                                       qt4=qt4, slp=slp, nk=qt + 1: [
                                        tn.matmul(
                                            ps_pv[:, slp, 0:65],
                                            pt[:, kt2,
                                               qt4 * 128:(qt4 + 1) * 128],
                                            v_perm[:, kt2,
                                                   h * 65:h * 65 + 65],
                                            start=(kt2 == 0),
                                            stop=(kt2 == nk - 1))
                                        for kt2 in range(nk)][-1]
                                       .then_inc(stn.h, 1))
                                    vv1 = sv.bump(2)
                                    on('vector', lambda v, h=h, qt=qt,
                                       slp=slp, tv=tv: (
                                        v.wait_ge(stn.h, tv),
                                        v.tensor_copy(
                                            out=attnv[:, qt,
                                                      h * 64:(h + 1) * 64],
                                            in_=ps_pv[:, slp, 0:64]),
                                        v.tensor_copy(
                                            out=dstage[:, h * NT + qt:
                                                       h * NT + qt + 1],
                                            in_=ps_pv[:, slp, 64:65])
                                        .then_inc(sv.h, 2)))
                                    pv_cons.append(vv1)
                                    pvslot += 1
                                    last_pv_tv = stn.v
                        sdr = ss.bump(2)
                        on('scalar', lambda s, vv=sv.v, sdr=sdr: (
                            s.wait_ge(sv.h, vv),
                            s.activation(out=drec[:], in_=dstage[:],
                                         func=AF.Ln, bias=0.0, scale=1.0)
                            .then_inc(ss.h, 1),
                            s.wait_ge(ss.h, sdr - 1),
                            s.activation(out=drec[:], in_=drec[:],
                                         func=AF.Exp, bias=0.0, scale=-1.0)
                            .then_inc(ss.h, 1),
                            s.wait_ge(ss.h, sdr)))

                        for m in ow_marks:
                            wmark('tensor', m)
                        wmark('vector', bias_marks[0])
                        tpo_done, oev_done, oproj_tv = [], [], []
                        oslot = 0
                        for qt in range(NT):
                            sm = None
                            for h in range(H):
                                sm = ss.bump()
                                on('scalar', lambda s, qt=qt, h=h, sm=sm,
                                   sdr=sdr,
                                   tpm=(tpo_done[qt - 2]
                                        if (h == 0 and qt >= 2) else None): (
                                    (s.wait_ge(ss.h, sdr)
                                     if (h == 0 and qt == 0) else None),
                                    (s.wait_ge(tpm[0].h, tpm[1])
                                     if tpm else None),
                                    s.activation(
                                        out=anorm[qt % 2][:,
                                                          h * 64:(h + 1) * 64],
                                        in_=attnv[:, qt, h * 64:(h + 1) * 64],
                                        func=AF.Identity, bias=0.0,
                                        scale=drec[:, h * NT + qt:
                                                   h * NT + qt + 1]),
                                    (s.drain() if h == H - 1 else s.nop())
                                    .then_inc(ss.h, 1)))
                            if qt >= 2:
                                wait('scalar', stn, oproj_tv[qt - 2])
                            lna = dtpb.next()
                            on('scalar', lambda s, qt=qt, sm=sm, lna=lna: (
                                s.wait_ge(ss.h, sm),
                                s.dma_start(out=avT[qt % 2][:],
                                            in_=anorm[qt % 2][:],
                                            transpose=True)
                                .then_inc(lna.h, 16)))
                            tpo_done.append((lna, lna.bump(16)))
                            for ch in range(2):
                                slo = oslot % 2
                                if oslot >= 2:
                                    wait('tensor', sv, oev_done[oslot - 2])
                                if ch == 0:
                                    wmark('tensor', tpo_done[qt])
                                tv = stn.bump()
                                on('tensor', lambda tn, qt=qt, ch=ch,
                                   slo=slo: [
                                    tn.matmul(
                                        ps_o[:, slo, :],
                                        avT[qt % 2][:, hp2, :],
                                        ow_sb[:, hp2, ch * 512:(ch + 1) * 512],
                                        start=(hp2 == 0), stop=(hp2 == 7))
                                    for hp2 in range(8)][-1]
                                   .then_inc(stn.h, 1))
                                vv = sv.bump()
                                on('vector', lambda v, qt=qt, ch=ch, slo=slo,
                                   tv=tv: (
                                    v.wait_ge(stn.h, tv),
                                    v.scalar_tensor_tensor(
                                        out=attn_out[:, qt,
                                                     ch * 512:(ch + 1) * 512],
                                        in0=ps_o[:, slo, :], scalar=0.0,
                                        in1=ob_t[:, ch * 512:(ch + 1) * 512],
                                        op0=ALU.add, op1=ALU.add)
                                    .then_inc(sv.h, 1)))
                                oev_done.append(vv)
                                oslot += 1
                            oproj_tv.append(stn.v)
                        wait('tensor', sv, oev_done[-1])
                        wait('tensor', sv, oev_done[-2])
                        if DEBUG_TAPS and b == DEBUG_TAPS_BATCH:
                            wait('sync', sv, sv.v)
                            dbl2 = [dxb.next() for _ in range(4)]
                            on('sync', lambda sync, dbl2=dbl2: (
                                sync.dma_start(out=dbg["ao"], in_=attn_out[:])
                                .then_inc(dbl2[0].h, 16),
                                sync.dma_start(out=dbg["pt"], in_=pt[:])
                                .then_inc(dbl2[1].h, 16),
                                sync.dma_start(out=dbg["dst"], in_=dstage[:])
                                .then_inc(dbl2[2].h, 16),
                                sync.dma_start(out=dbg["drc"], in_=drec[:])
                                .then_inc(dbl2[3].h, 16)))
                            for ln2 in dbl2:
                                wmark('sync', (ln2, ln2.bump(16)))
                    barrier()

                with ExitStack() as stf:
                    alloc_qs(stf, f"b{b}")
                    xqT = stf.enter_context(nc.sbuf_tensor(
                        f"xqT_{b}", [128, ND, S], BF16, is_transpose_dest=True))
                    xg = stf.enter_context(nc.sbuf_tensor(
                        f"xg_{b}", [128, 4, D], F32))
                    xnf = [stf.enter_context(nc.sbuf_tensor(
                        f"xnf{i}_{b}", [128, D], F32)) for i in range(2)]
                    xq_bf = [stf.enter_context(nc.sbuf_tensor(
                        f"xq_bf{i}_{b}", [128, D], BF16)) for i in range(2)]
                    hterm = [stf.enter_context(nc.sbuf_tensor(
                        f"hterm{i}_{b}", [128, 512], F32)) for i in range(2)]
                    silu_sb = [stf.enter_context(nc.sbuf_tensor(
                        f"silu{i}_{b}", [128, 512], F32)) for i in range(2)]
                    hq4 = stf.enter_context(nc.sbuf_tensor(
                        f"hq4_{b}", [128, 4, DI], BF16))
                    hqT = stf.enter_context(nc.sbuf_tensor(
                        f"hqT_{b}", [128, NI, 512], BF16,
                        is_transpose_dest=True))
                    wgu = [stf.enter_context(nc.sbuf_tensor(
                        f"wgu{i}_{b}", [128, ND, 1024], BF16))
                        for i in range(2)]
                    wdt_sb = [stf.enter_context(nc.sbuf_tensor(
                        f"wdt_sb{i}_{b}", [128, 512], BF16))
                        for i in range(2)]
                    out_sb = [stf.enter_context(nc.sbuf_tensor(
                        f"out_sb{i}_{b}", [128, 512], F32)) for i in range(2)]
                    ps_gu = stf.enter_context(nc.psum_tensor(
                        f"ps_gu_{b}", [128, 4, 512], F32))
                    ps_d = stf.enter_context(nc.psum_tensor(
                        f"ps_d_{b}", [128, 4, 512], F32))
                    ps_tp = ps_d[:, 0, :].bitcast(BF16)

                    NCH = DI // 256
                    hq_tp_hist = []
                    xnf_qv = []
                    for g4 in range(4):
                        t0 = g4 * 4
                        for tt in range(4):
                            t = t0 + tt
                            lnb = dxb.next()
                            on('sync', lambda sync, t=t, tt=tt, lnb=lnb:
                               sync.dma_start(
                                out=xg[:, tt, :],
                                in_=x_d[b, t * 128:(t + 1) * 128, :])
                               .then_inc(lnb.h, 16))
                            xldt = (lnb, lnb.bump(16))
                            se = ss.bump()
                            on('scalar', lambda s, t=t, tt=tt, xldt=xldt,
                               gq=(xnf_qv[-2] if len(xnf_qv) >= 2 else 0): (
                                s.wait_ge(xldt[0].h, xldt[1]),
                                (s.wait_ge(sv.h, gq) if gq else None),
                                s.activation(out=xnf[tt % 2][:],
                                             in_=xg[:, tt, :],
                                             func=AF.Identity,
                                             bias=st_nmr[:, t:t + 1],
                                             scale=st_rstd[:, t:t + 1])
                                .then_inc(ss.h, 1)))
                            wait('vector', ss, se)
                            if len(xq_tp_hist) >= 2:
                                wait('vector', sv, xq_tp_hist[-2])
                            emit_quant(xnf[tt % 2][:], xq_bf[tt % 2][:], D)
                            qv = sv.v
                            xnf_qv.append(qv)
                            if xq_tp_hist:
                                wait('tensor', sv, xq_tp_hist[-1])
                            tpv = stn.bump()
                            on('tensor', lambda tn, tt=tt, qv=qv: (
                                tn.wait_ge(sv.h, qv),
                                [tn.transpose(
                                    ps_tp[:, d * 128:(d + 1) * 128],
                                    xq_bf[tt % 2][:, d * 128:(d + 1) * 128],
                                    ident[:]) for d in range(ND)][-1]
                                .then_inc(stn.h, 1)))
                            sv.bump()
                            on('vector', lambda v, t=t, tpv=tpv: (
                                v.wait_ge(stn.h, tpv),
                                v.tensor_copy(
                                    out=bass.AP(xqT, t * 128,
                                                [[ND * S, 128], [S, ND],
                                                 [1, 128]]),
                                    in_=ps_tp[:]).then_inc(sv.h, 1)))
                            xq_tp_hist.append(sv.v)
                        wait('tensor', sv, xq_tp_hist[-1])
                        if DEBUG_TAPS and b == DEBUG_TAPS_BATCH and g4 == 3:
                            wait('sync', sv, xq_tp_hist[-1])
                            lnd = dxb.next()
                            on('sync', lambda sync, lnd=lnd: sync.dma_start(
                                out=dbg["xqT"], in_=xqT[:])
                               .then_inc(lnd.h, 16))
                            lnd.bump(16)

                        wld = []
                        gu_cons = []
                        gslot = 0
                        CHUNKS = [(o, min(512, DI - o))
                                  for o in range(0, DI, 512)]
                        for c, (coff, cw) in enumerate(CHUNKS):
                            if c >= 2:
                                wait('sync', stn, wld[c - 2])
                            lng = dxb.next()
                            lnu = dxb.next()
                            on('sync', lambda sync, coff=coff, cw=cw, c=c,
                               lng=lng, lnu=lnu: (
                                sync.dma_start(
                                    out=wgu[c % 2][:, :, 0:cw],
                                    in_=wgt_d[:, :, coff:coff + cw])
                                .then_inc(lng.h, 16),
                                sync.dma_start(
                                    out=wgu[c % 2][:, :, cw:2 * cw],
                                    in_=wut_d[:, :, coff:coff + cw])
                                .then_inc(lnu.h, 16)))
                            wldv = ((lng, lng.bump(16)), (lnu, lnu.bump(16)))
                            lastmm = None
                            for tt in range(4):
                                t = t0 + tt
                                sg0 = 2 * (gslot % 2)
                                if gslot >= 2:
                                    wait('tensor', sv, gu_cons[gslot - 2])
                                tv = stn.bump()
                                on('tensor', lambda tn, c=c, t=t, sg0=sg0,
                                   cw=cw, wldv=wldv: (
                                    tn.wait_ge(wldv[0][0].h, wldv[0][1]),
                                    tn.wait_ge(wldv[1][0].h, wldv[1][1]),
                                    [tn.matmul(
                                        ps_gu[:, sg0, 0:cw],
                                        xqT[:, d, t * 128:(t + 1) * 128],
                                        wgu[c % 2][:, d, 0:cw],
                                        start=(d == 0), stop=(d == ND - 1))
                                     for d in range(ND)],
                                    [tn.matmul(
                                        ps_gu[:, sg0 + 1, 0:cw],
                                        xqT[:, d, t * 128:(t + 1) * 128],
                                        wgu[c % 2][:, d, cw:2 * cw],
                                        start=(d == 0), stop=(d == ND - 1))
                                     for d in range(ND)][-1]
                                    .then_inc(stn.h, 1)))
                                lastmm = stn.v
                                se = ss.bump()
                                on('scalar', lambda s, sg0=sg0, tv=tv,
                                   cw=cw, gsl=gslot % 2,
                                   gg=(gu_cons[gslot - 2]
                                       if gslot >= 2 else 0): (
                                    s.wait_ge(stn.h, tv),
                                    (s.wait_ge(sv.h, gg) if gg else None),
                                    s.activation(out=silu_sb[gsl][:, 0:cw],
                                                 in_=ps_gu[:, sg0, 0:cw],
                                                 func=AF.Silu, bias=0.0,
                                                 scale=1.0)
                                    .then_inc(ss.h, 1)))
                                vv = sv.bump()
                                on('vector', lambda v, tt=tt, sg0=sg0,
                                   cw=cw, se=se, gsl=gslot % 2: (
                                    v.wait_ge(ss.h, se),
                                    v.tensor_tensor(
                                        out=hterm[tt % 2][:, 0:cw],
                                        in0=silu_sb[gsl][:, 0:cw],
                                        in1=ps_gu[:, sg0 + 1, 0:cw],
                                        op=ALU.mult).then_inc(sv.h, 1)))
                                gu_cons.append(vv)
                                emit_quant(hterm[tt % 2][:, 0:cw],
                                           hq4[:, tt, coff:coff + cw],
                                           cw)
                                gslot += 1
                            wld.append(lastmm)
                        hq_done = sv.v

                        for tt in range(4):
                            for i0 in (0, 8, 16):
                                ni = min(8, NI - i0)
                                if hq_tp_hist:
                                    wait('tensor', sv, hq_tp_hist[-1])
                                tpv = stn.bump()
                                on('tensor', lambda tn, tt=tt, i0=i0, ni=ni,
                                   hq_done=hq_done: (
                                    (tn.wait_ge(sv.h, hq_done)
                                     if (tt == 0 and i0 == 0) else None),
                                    [tn.transpose(
                                        ps_tp[:, k * 128:(k + 1) * 128],
                                        hq4[:, tt,
                                            (i0 + k) * 128:(i0 + k + 1) * 128],
                                        ident[:]) for k in range(ni)][-1]
                                    .then_inc(stn.h, 1)))
                                sv.bump()
                                on('vector', lambda v, tt=tt, i0=i0, ni=ni,
                                   tpv=tpv: (
                                    v.wait_ge(stn.h, tpv),
                                    v.tensor_copy(
                                        out=bass.AP(hqT,
                                                    i0 * 512 + tt * 128,
                                                    [[NI * 512, 128],
                                                     [512, ni], [1, 128]]),
                                        in_=ps_tp[:, 0:ni * 128])
                                    .then_inc(sv.h, 1)))
                                hq_tp_hist.append(sv.v)
                        wait('tensor', sv, hq_tp_hist[-1])
                        if DEBUG_TAPS and b == DEBUG_TAPS_BATCH and g4 == 1:
                            wait('sync', sv, hq_tp_hist[-1])
                            lnd2 = dxb.next()
                            on('sync', lambda sync, lnd2=lnd2: sync.dma_start(
                                out=dbg["hqT"], in_=hqT[:])
                               .then_inc(lnd2.h, 16))
                            lnd2.bump(16)

                        fin_marks = []
                        for ch in range(2):
                            wdld = []
                            for i in range(NI):
                                if i >= 2:
                                    wait('sync', stn, wdld[i - 2])
                                lnd3 = dxb.next()
                                on('sync', lambda sync, i=i, ch=ch, lnd3=lnd3:
                                   sync.dma_start(
                                    out=wdt_sb[i % 2][:],
                                    in_=wdt_d[:, i, ch * 512:(ch + 1) * 512])
                                   .then_inc(lnd3.h, 16))
                                wv = (lnd3, lnd3.bump(16))
                                if i == 0 and fin_marks:
                                    wait('tensor', sv, fin_marks[-1])
                                tv = stn.bump()
                                on('tensor', lambda tn, i=i, wv=wv: (
                                    tn.wait_ge(wv[0].h, wv[1]),
                                    [tn.matmul(
                                        ps_d[:, tt2, :],
                                        hqT[:, i, tt2 * 128:(tt2 + 1) * 128],
                                        wdt_sb[i % 2][:],
                                        start=(i == 0), stop=(i == NI - 1))
                                     for tt2 in range(4)][-1]
                                    .then_inc(stn.h, 1)))
                                wdld.append(stn.v)
                            for tt in range(4):
                                t = t0 + tt
                                vf = sv.bump(2)
                                on('vector', lambda v, t=t, tt=tt, ch=ch,
                                   wl=wdld[-1], qA=qs['A']: (
                                    v.wait_ge(stn.h, wl),
                                    v.scalar_tensor_tensor(
                                        out=qA[:, 0:512],
                                        in0=ps_d[:, tt, :], scalar=0.0,
                                        in1=attn_out[:, t,
                                                     ch * 512:(ch + 1) * 512],
                                        op0=ALU.add, op1=ALU.add),
                                    v.tensor_tensor(
                                        out=out_sb[tt % 2][:],
                                        in0=qA[:, 0:512],
                                        in1=xg[:, tt, ch * 512:(ch + 1) * 512],
                                        op=ALU.add),
                                    v.drain().then_inc(sv.h, 2)))
                                fin_marks.append(vf)
                                lny = dxb.next()
                                on('sync', lambda sync, t=t, tt=tt, ch=ch,
                                   vf=vf, lny=lny: (
                                    sync.wait_ge(sv.h, vf),
                                    sync.dma_start(
                                        out=y_d[b, t * 128:(t + 1) * 128,
                                                ch * 512:(ch + 1) * 512],
                                        in_=out_sb[tt % 2][:])
                                    .then_inc(lny.h, 16)))
                                out_hist.append((lny, lny.bump(16)))
                                if len(out_hist) >= 2:
                                    wmark('vector', out_hist[-2])
                        barrier()

        for b in range(B_PER_CORE):
            plan_batch(b)

        for (ln, vv) in dxb.totals() + dtpb.totals() + dcb.totals():
            wait('sync', ln, vv)
        barrier()

        blk = st.enter_context(nc.Block())
        for e in ENGINES:
            def make(e):
                def body(eng):
                    for fn in prog[e]:
                        fn(eng)
                return body
            getattr(blk, e)(make(e))

    return nc


_NC_CACHE = [None]


def kernel(L0_attn_x, L0_attn_Q_w, L0_attn_Q_b, L0_attn_K_w, L0_attn_K_b,
           L0_attn_V_w, L0_attn_V_b, L0_attn_O_w, L0_attn_O_b,
           L0_ffn_w_gate, L0_ffn_w_up, L0_ffn_w_down):
    if _NC_CACHE[0] is None:
        _NC_CACHE[0] = build_program()
    nc = _NC_CACHE[0]

    x = np.ascontiguousarray(np.asarray(L0_attn_x, dtype=np.float32))
    common = {
        "qw": np.ascontiguousarray(np.asarray(L0_attn_Q_w, np.float32)),
        "kw": np.ascontiguousarray(np.asarray(L0_attn_K_w, np.float32)),
        "vw": np.ascontiguousarray(np.asarray(L0_attn_V_w, np.float32)),
        "qb": np.ascontiguousarray(np.asarray(L0_attn_Q_b, np.float32)),
        "kb": np.ascontiguousarray(np.asarray(L0_attn_K_b, np.float32)),
        "vb": np.ascontiguousarray(np.asarray(L0_attn_V_b, np.float32)),
        "ow": np.ascontiguousarray(np.asarray(L0_attn_O_w, np.float32)),
        "ob": np.ascontiguousarray(np.asarray(L0_attn_O_b, np.float32))
        .reshape(1, D),
        "wg": np.ascontiguousarray(np.asarray(L0_ffn_w_gate, np.float32)),
        "wu": np.ascontiguousarray(np.asarray(L0_ffn_w_up, np.float32)),
        "wd": np.ascontiguousarray(np.asarray(L0_ffn_w_down, np.float32)),
    }
    in_maps = []
    for c in range(NCORES):
        m = dict(common)
        m["x"] = np.ascontiguousarray(x[c * B_PER_CORE:(c + 1) * B_PER_CORE])
        in_maps.append(m)

    res = run_bass_kernel_spmd(nc, in_maps, core_ids=list(range(NCORES)))
    out = np.concatenate([np.asarray(r["y"]) for r in res.results], axis=0)
    return out.astype(np.float32)


# version 14 (final)
# speedup vs baseline: 1.1931x; 1.0550x over previous
import numpy as np
from contextlib import ExitStack

import concourse.bass as bass
import concourse.mybir as mybir
from concourse.bass_utils import run_bass_kernel_spmd

F32 = mybir.dt.float32
BF16 = mybir.dt.bfloat16
I32 = mybir.dt.int32
ALU = mybir.AluOpType
AF = mybir.ActivationFunctionType
AX = mybir.AxisListType

NCORES = 8
B_PER_CORE = 2
S = 2048
D = 1024
H = 16
DH = 64
DI = 2816
NT = S // 128
ND = D // 128
NI = DI // 128
QC = 512
NQC = S // QC
EXPB = -20.0
ENGINES = ("sync", "gpsimd", "vector", "scalar", "tensor")
DEBUG_TAPS = False
DEBUG_TAPS_BATCH = 0


class Ctr:
    def __init__(self, stack, nc, name):
        self.h = stack.enter_context(nc.semaphore(name))
        self.v = 0

    def bump(self, n=1):
        self.v += n
        return self.v


class Lanes:

    def __init__(self, stack, nc, name, k=8):
        self.lanes = [Ctr(stack, nc, f"{name}{i}") for i in range(k)]
        self.i = 0

    def next(self):
        ln = self.lanes[self.i % len(self.lanes)]
        self.i += 1
        return ln

    def totals(self):
        return [(ln, ln.v) for ln in self.lanes if ln.v > 0]


def build_program():
    nc = bass.Bass("TRN2", target_bir_lowering=False,
                   detect_race_conditions=False)

    x_d = nc.dram_tensor("x", [B_PER_CORE, S, D], F32, kind="ExternalInput").ap()
    qw_d = nc.dram_tensor("qw", [H, D, DH], F32, kind="ExternalInput").ap()
    kw_d = nc.dram_tensor("kw", [H, D, DH], F32, kind="ExternalInput").ap()
    vw_d = nc.dram_tensor("vw", [H, D, DH], F32, kind="ExternalInput").ap()
    qb_d = nc.dram_tensor("qb", [H, DH], F32, kind="ExternalInput").ap()
    kb_d = nc.dram_tensor("kb", [H, DH], F32, kind="ExternalInput").ap()
    vb_d = nc.dram_tensor("vb", [H, DH], F32, kind="ExternalInput").ap()
    ow_d = nc.dram_tensor("ow", [H, DH, D], F32, kind="ExternalInput").ap()
    ob_d = nc.dram_tensor("ob", [1, D], F32, kind="ExternalInput").ap()
    wg_d = nc.dram_tensor("wg", [DI, D], F32, kind="ExternalInput").ap()
    wu_d = nc.dram_tensor("wu", [DI, D], F32, kind="ExternalInput").ap()
    wd_d = nc.dram_tensor("wd", [D, DI], F32, kind="ExternalInput").ap()
    y_d = nc.dram_tensor("y", [B_PER_CORE, S, D], F32, kind="ExternalOutput").ap()
    wkind = "ExternalOutput" if DEBUG_TAPS else "Internal"
    wgt_d = nc.dram_tensor("wgt", [128, ND, DI], BF16, kind=wkind).ap()
    wut_d = nc.dram_tensor("wut", [128, ND, DI], BF16, kind=wkind).ap()
    wdt_d = nc.dram_tensor("wdt", [128, NI, D], BF16, kind=wkind).ap()
    dbg = {}
    if DEBUG_TAPS:
        for nm, shp, dt2 in (("xnT", [128, ND, S], BF16),
                             ("qt", [128, 8, S], BF16),
                             ("kt", [128, 8, S], BF16),
                             ("vp", [128, NT, H * 65], BF16),
                             ("ao", [128, NT, D], BF16),
                             ("pt", [128, NT, QC], BF16),
                             ("dst", [128, H * NT], F32),
                             ("drc", [128, H * NT], F32),
                             ("rstd", [128, NT], F32),
                             ("nmr", [128, NT], F32),
                             ("xqT", [128, ND, S], BF16),
                             ("hqT", [128, NI, 512], BF16)):
            dbg[nm] = nc.dram_tensor("dbg_" + nm, shp, dt2,
                                     kind="ExternalOutput").ap()

    prog = {e: [] for e in ENGINES}

    def on(eng, fn):
        prog[eng].append(fn)

    QW = 1024

    with ExitStack() as st:
        def sb(name, shape, dtype, **kw):
            return st.enter_context(nc.sbuf_tensor(name, shape, dtype, **kw))

        neg20 = sb("neg20", [128, 1], F32)
        masks = sb("masks", [128, 4, QC], BF16)
        qb_sb = sb("qb_sb", [128, 8], F32)
        kb_sb = sb("kb_sb", [128, 8], F32)
        st_rstd = sb("st_rstd", [128, NT], F32)
        st_nmr = sb("st_nmr", [128, NT], F32)
        ident = sb("ident", [128, 128], BF16)
        qs = {}

        def alloc_qs(scope, tag):
            qs.clear()
            for k in ("A", "B", "C", "D"):
                qs[k] = scope.enter_context(
                    nc.sbuf_tensor(f"qs_{k}_{tag}", [128, QW], F32))
            qs['E'] = scope.enter_context(
                nc.sbuf_tensor(f"qs_E_{tag}", [128, QW], I32))
            for k in ("bm", "scale", "lns", "rsh", "sc2"):
                qs[k] = scope.enter_context(
                    nc.sbuf_tensor(f"qs_{k}_{tag}", [128, QW // 16], F32))

        sv = Ctr(st, nc, "sv")
        ss = Ctr(st, nc, "ss")
        stn = Ctr(st, nc, "stn")
        sg = Ctr(st, nc, "sg")
        dxb = Lanes(st, nc, "dx", 8)
        dtpb = Lanes(st, nc, "dtp", 8)
        dcb = Lanes(st, nc, "dc", 8)
        bar = Ctr(st, nc, "bar")
        nbar = [0]

        def barrier():
            nbar[0] += 1
            k = nbar[0]
            for e in ENGINES:
                on(e, lambda eng, k=k: (eng.drain().then_inc(bar.h, 1),
                                        eng.wait_ge(bar.h, 5 * k)))

        def wait(eng, ctr, val):
            if val > 0:
                on(eng, lambda E, c=ctr, vv=val: E.wait_ge(c.h, vv))

        def wmark(eng, mark):
            if mark:
                c, vv = mark
                on(eng, lambda E, c=c, vv=vv: E.wait_ge(c.h, vv))

        def emit_quant(x_ap, out_ap, F, drain_out=False):
            nb = F // 16
            xv = x_ap.rearrange("p (b w) -> p b w", w=16)
            A, Bb, C, Dt, E = (qs['A'][:, 0:F], qs['B'][:, 0:F],
                               qs['C'][:, 0:F], qs['D'][:, 0:F],
                               qs['E'][:, 0:F])
            bm, scale, lns, rsh, sc2 = (qs[k][:, 0:nb] for k in
                                        ("bm", "scale", "lns", "rsh", "sc2"))
            v1 = sv.bump()
            on('vector', lambda v, xv=xv, bm=bm, v1=v1: (
                v.tensor_reduce(out=bm, in_=xv, axis=AX.X, op=ALU.max,
                                apply_absolute_value=True).then_inc(sv.h, 1),
                v.wait_ge(sv.h, v1)))
            v2 = sv.bump()
            on('vector', lambda v, bm=bm, scale=scale: (
                v.tensor_scalar(
                    out=scale, in0=bm, scalar1=1.0 / 6.0, scalar2=1e-12,
                    op0=ALU.mult, op1=ALU.max).then_inc(sv.h, 1)))
            s1 = ss.bump()
            on('scalar', lambda s, scale=scale, lns=lns, rsh=rsh, sc2=sc2,
               v2=v2: (
                s.wait_ge(sv.h, v2),
                s.activation(out=lns, in_=scale, func=AF.Ln, bias=0.0,
                             scale=2.0),
                s.activation(out=rsh, in_=lns, func=AF.Exp, bias=0.0,
                             scale=-1.0),
                s.mul(sc2, scale, 0.5).then_inc(ss.h, 1)))
            lns2 = qs['lns'][:, 0:nb]
            scale2 = qs['scale'][:, 0:nb]
            nr1 = sv.bump(3)
            vend = sv.bump()

            bm2 = qs['bm'][:, 0:nb]

            def main(v, xv=xv, A=A, Bb=Bb, C=C, Dt=Dt, E=E, rsh=rsh,
                     sc2=sc2, out_ap=out_ap, nb=nb, s1=s1, lns2=lns2,
                     scale2=scale2, nr1=nr1, bm2=bm2, drain_out=drain_out):
                v.wait_ge(ss.h, s1)
                v.tensor_tensor(out=lns2, in0=scale2, in1=rsh,
                                op=ALU.mult).then_inc(sv.h, 1)
                v.wait_ge(sv.h, nr1 - 2)
                v.tensor_scalar(out=bm2, in0=lns2, scalar1=-2.0, scalar2=2.0,
                                op0=ALU.mult, op1=ALU.add).then_inc(sv.h, 1)
                v.wait_ge(sv.h, nr1 - 1)
                v.tensor_tensor(out=lns2, in0=rsh, in1=bm2,
                                op=ALU.mult).then_inc(sv.h, 1)
                v.wait_ge(sv.h, nr1)
                rsh_b = lns2.unsqueeze(-1).broadcast_to([128, nb, 16])
                sc2_b = sc2.unsqueeze(-1).broadcast_to([128, nb, 16])
                v.tensor_tensor(out=A.rearrange("p (b w) -> p b w", w=16),
                                in0=xv, in1=rsh_b, op=ALU.mult)
                v.tensor_scalar(out=Bb, in0=A, scalar1=1.0, scalar2=-1.0,
                                op0=ALU.min, op1=ALU.max)
                v.tensor_scalar(out=C, in0=A, scalar1=2.0, scalar2=-2.0,
                                op0=ALU.min, op1=ALU.max)
                v.scalar_tensor_tensor(out=Dt, in0=Bb, scalar=2.0, in1=A,
                                       op0=ALU.mult, op1=ALU.add)
                v.tensor_tensor(out=Bb, in0=Dt, in1=C, op=ALU.add)
                v.tensor_scalar(out=E, in0=Bb, scalar1=8.0, scalar2=0.0,
                                op0=ALU.add, op1=ALU.add)
                v.scalar_tensor_tensor(out=Dt, in0=E, scalar=-7.5, in1=Bb,
                                       op0=ALU.add, op1=ALU.is_lt)
                v.tensor_tensor(out=C, in0=Dt, in1=E, op=ALU.add)
                v.tensor_scalar(out=A, in0=C, scalar1=12.0, scalar2=4.0,
                                op0=ALU.min, op1=ALU.max)
                v.tensor_scalar(out=Dt, in0=C, scalar1=14.0, scalar2=2.0,
                                op0=ALU.min, op1=ALU.max)
                v.scalar_tensor_tensor(out=Bb, in0=C, scalar=2.0, in1=Dt,
                                       op0=ALU.mult, op1=ALU.subtract)
                v.scalar_tensor_tensor(out=C, in0=Bb, scalar=2.0, in1=A,
                                       op0=ALU.mult, op1=ALU.subtract)
                fin = v.scalar_tensor_tensor(
                    out=out_ap.rearrange("p (b w) -> p b w", w=16),
                    in0=C.rearrange("p (b w) -> p b w", w=16), scalar=-8.0,
                    in1=sc2_b, op0=ALU.add, op1=ALU.mult)
                if drain_out:
                    v.drain().then_inc(sv.h, 1)
                else:
                    fin.then_inc(sv.h, 1)

            on('vector', main)
            return vend

        with ExitStack() as st0:
            iota_t = st0.enter_context(nc.sbuf_tensor("iota_t", [128, QC], F32))
            g1 = sg.bump()
            on('gpsimd', lambda g: (
                g.memset(neg20[:], EXPB),
                g.iota(iota_t[:], [[1, QC]], base=0, channel_multiplier=-1,
                       allow_small_or_imprecise_dtypes=True)
                .then_inc(sg.h, 1)))
            su_l = [dxb.next() for _ in range(2)]
            on('sync', lambda sync, su_l=su_l: (
                sync.dma_start(out=qb_sb[:],
                               in_=bass.AP(qb_d.tensor, 0,
                                           [[1, 128], [128, 8]]),
                               allow_slow_non_contiguous=True)
                .then_inc(su_l[0].h, 16),
                sync.dma_start(out=kb_sb[:],
                               in_=bass.AP(kb_d.tensor, 0,
                                           [[1, 128], [128, 8]]),
                               allow_slow_non_contiguous=True)
                .then_inc(su_l[1].h, 16)))
            setup_marks = [(ln, ln.bump(16)) for ln in su_l]
            wait('vector', sg, g1)
            sv.bump()
            on('vector', lambda v: v.tensor_scalar(
                out=ident[:], in0=iota_t[:, 0:128], scalar1=0.0, scalar2=0.0,
                op0=ALU.is_equal, op1=ALU.add).then_inc(sv.h, 1))
            for m in range(4):
                sv.bump()
                on('vector', lambda v, m=m: v.tensor_scalar(
                    out=masks[:, m, :], in0=iota_t[:], scalar1=float(128 * m),
                    scalar2=0.0, op0=ALU.is_ge, op1=ALU.add)
                   .then_inc(sv.h, 1))
            masks_done = sv.v

        w_tiles = []
        for (src, rows, cols, dstT) in ((wg_d, DI, D, wgt_d),
                                        (wu_d, DI, D, wut_d),
                                        (wd_d, D, DI, wdt_d)):
            for t in range(rows // 128):
                w_tiles.append((src, cols, dstT, t))
        W = {'s1': 0, 's2': 0, 'vq': [], 'tp': [], 'stw': [],
             'pidx': 0, 'vqp': [], 'bufs': None}

        def plan_w_open(scope):
            alloc_qs(scope, "w")
            W['bufs'] = (
                [scope.enter_context(nc.sbuf_tensor(
                    f"w_in{i}", [128, QW], F32)) for i in range(2)],
                [scope.enter_context(nc.sbuf_tensor(
                    f"w_bf{i}", [128, DI], BF16)) for i in range(2)],
                [scope.enter_context(nc.sbuf_tensor(
                    "w_tp0", [128, NI, 128], BF16,
                    is_transpose_dest=True))])

        def plan_w_stage1(k):
            w_in, w_bf, _ = W['bufs']
            src, cols, dstT, t = w_tiles[k]
            if k >= 2:
                wmark('vector', W['tp'][k - 2])
            for piece in range(0, cols, QW):
                pw = min(QW, cols - piece)
                p = W['pidx']
                W['pidx'] += 1
                if p >= 2:
                    wait('sync', sv, W['vqp'][p - 2])
                lnl = dxb.next()
                on('sync', lambda sync, src=src, t=t, piece=piece, pw=pw,
                   p=p, lnl=lnl:
                   sync.dma_start(out=w_in[p % 2][:, 0:pw],
                                  in_=src[t * 128:(t + 1) * 128,
                                          piece:piece + pw])
                   .then_inc(lnl.h, 16))
                wmark('vector', (lnl, lnl.bump(16)))
                emit_quant(w_in[p % 2][:, 0:pw],
                           w_bf[k % 2][:, piece:piece + pw], pw,
                           drain_out=True)
                W['vqp'].append(sv.v)
            W['vq'].append(sv.v)

        def plan_w_stage2(k):
            _, w_bf, w_tp = W['bufs']
            src, cols, dstT, t = w_tiles[k]
            nsub = cols // 128
            wait('scalar', sv, W['vq'][k])
            if k >= 1:
                wmark('scalar', W['stw'][k - 1])
            lnt = dtpb.next()
            on('scalar', lambda s, k=k, cols=cols, nsub=nsub, lnt=lnt:
               s.dma_start(out=w_tp[0][:, 0:nsub, :],
                           in_=w_bf[k % 2][:, 0:cols], transpose=True)
               .then_inc(lnt.h, 16))
            W['tp'].append((lnt, lnt.bump(16)))
            wmark('sync', W['tp'][k])
            lns_ = dxb.next()
            on('sync', lambda sync, dstT=dstT, t=t, nsub=nsub, k=k,
               lns_=lns_:
               sync.dma_start(out=dstT[:, 0:nsub, t * 128:(t + 1) * 128],
                              in_=w_tp[0][:, 0:nsub, :])
               .then_inc(lns_.h, 16))
            W['stw'].append((lns_, lns_.bump(16)))

        def w_step():
            if W['s2'] < W['s1'] - 1 or (W['s1'] >= len(w_tiles)
                                         and W['s2'] < len(w_tiles)):
                plan_w_stage2(W['s2'])
                W['s2'] += 1
            if W['s1'] < len(w_tiles):
                plan_w_stage1(W['s1'])
                W['s1'] += 1

        def w_flush():
            while W['s1'] < len(w_tiles):
                plan_w_stage1(W['s1'])
                W['s1'] += 1
            while W['s2'] < len(w_tiles):
                plan_w_stage2(W['s2'])
                W['s2'] += 1
            for (ln, vv) in dxb.totals() + dtpb.totals():
                wait('sync', ln, vv)

        xq_tp_hist = []
        out_hist = []

        def plan_batch(b, with_w=False):
            with ExitStack() as stb:
                attn_out = stb.enter_context(nc.sbuf_tensor(
                    f"attn_out_{b}", [128, NT, D], BF16,
                    is_transpose_dest=True))
                xnT = attn_out[:].rearrange("p a c -> p (a c)") \
                    .rearrange("p (u w) -> p u w", u=ND)
                attnv = attn_out[:]

                with ExitStack() as sta:
                    qt_all = sta.enter_context(nc.sbuf_tensor(
                        f"qt_all_{b}", [128, 8, S], BF16))
                    kt_all = sta.enter_context(nc.sbuf_tensor(
                        f"kt_all_{b}", [128, 8, S], BF16))
                    v_perm = sta.enter_context(nc.sbuf_tensor(
                        f"v_perm_{b}", [128, NT, H * 65], BF16))
                    pt = sta.enter_context(nc.sbuf_tensor(
                        f"pt_{b}", [128, NT, QC], BF16))
                    ob_t = sta.enter_context(nc.sbuf_tensor(
                        f"ob_t_{b}", [128, D], F32))
                    vb_t = sta.enter_context(nc.sbuf_tensor(
                        f"vb_t_{b}", [128, D], F32))
                    bt_l = [dxb.next() for _ in range(2)]
                    on('sync', lambda sync, bt_l=bt_l: (
                        sync.dma_start(out=ob_t[:],
                                       in_=ob_d.broadcast_to([128, D]))
                        .then_inc(bt_l[0].h, 16),
                        sync.dma_start(
                            out=vb_t[:],
                            in_=vb_d.rearrange("h d -> (h d)").unsqueeze(0)
                            .broadcast_to([128, D])).then_inc(bt_l[1].h, 16)))
                    bias_marks = [(ln, ln.bump(16)) for ln in bt_l]

                    with ExitStack() as stp:
                        x_t = [stp.enter_context(nc.sbuf_tensor(
                            f"x_t{i}_{b}", [128, D], F32)) for i in range(2)]
                        xsq = stp.enter_context(nc.sbuf_tensor(
                            f"xsq_{b}", [128, D], F32))
                        xn_bf = [stp.enter_context(nc.sbuf_tensor(
                            f"xn_bf{i}_{b}", [128, D], BF16))
                            for i in range(2)]
                        sm1 = stp.enter_context(nc.sbuf_tensor(
                            f"sm1_{b}", [128, 1], F32))
                        sm2 = stp.enter_context(nc.sbuf_tensor(
                            f"sm2_{b}", [128, 1], F32))
                        msq = stp.enter_context(nc.sbuf_tensor(
                            f"msq_{b}", [128, 1], F32))
                        var = stp.enter_context(nc.sbuf_tensor(
                            f"var_{b}", [128, 1], F32))
                        lnv = stp.enter_context(nc.sbuf_tensor(
                            f"lnv_{b}", [128, 1], F32))
                        mmean = stp.enter_context(nc.sbuf_tensor(
                            f"mmean_{b}", [128, 1], F32))
                        wbuf = [stp.enter_context(nc.sbuf_tensor(
                            f"wbuf{i}_{b}", [128, ND, 1024], BF16))
                            for i in range(2)]

                        ones_done = sv.bump()
                        on('vector', lambda v: v.memset(
                            bass.AP(v_perm, 64,
                                    [[NT * H * 65, 128], [H * 65, NT],
                                     [65, H]]), 1.0).then_inc(sv.h, 1))

                        tp_x_done = []
                        ln_s3 = []
                        for t in range(NT):
                            if t >= 2:
                                wait('sync', ss, ln_s3[t - 2])
                            lnx = dxb.next()
                            on('sync', lambda sync, t=t, lnx=lnx:
                               sync.dma_start(
                                out=x_t[t % 2][:],
                                in_=x_d[b, t * 128:(t + 1) * 128, :])
                               .then_inc(lnx.h, 16))
                            xldt = (lnx, lnx.bump(16))
                            v1 = sv.bump()
                            on('vector', lambda v, t=t, v1=v1, xldt=xldt: (
                                v.wait_ge(xldt[0].h, xldt[1]),
                                v.tensor_reduce(out=sm1[:], in_=x_t[t % 2][:],
                                                axis=AX.X, op=ALU.add)
                                .then_inc(sv.h, 1),
                                v.wait_ge(sv.h, v1)))
                            s1 = ss.bump()
                            on('scalar', lambda s, t=t, xldt=xldt: (
                                s.wait_ge(xldt[0].h, xldt[1]),
                                s.square(xsq[:], x_t[t % 2][:])
                                .then_inc(ss.h, 1)))
                            v2 = sv.bump()
                            on('vector', lambda v, s1=s1, v2=v2: (
                                v.wait_ge(ss.h, s1),
                                v.tensor_reduce(out=sm2[:], in_=xsq[:],
                                                axis=AX.X, op=ALU.add)
                                .then_inc(sv.h, 1),
                                v.wait_ge(sv.h, v2)))
                            v3 = sv.bump(4)
                            on('vector', lambda v, v3=v3: (
                                v.tensor_scalar(out=mmean[:], in0=sm1[:],
                                                scalar1=1.0 / D, scalar2=0.0,
                                                op0=ALU.mult, op1=ALU.add)
                                .then_inc(sv.h, 1),
                                v.wait_ge(sv.h, v3 - 3),
                                v.tensor_tensor(out=msq[:], in0=mmean[:],
                                                in1=sm1[:], op=ALU.mult)
                                .then_inc(sv.h, 1),
                                v.wait_ge(sv.h, v3 - 2),
                                v.tensor_tensor(out=var[:], in0=sm2[:],
                                                in1=msq[:], op=ALU.subtract)
                                .then_inc(sv.h, 1),
                                v.wait_ge(sv.h, v3 - 1),
                                v.tensor_scalar(out=var[:], in0=var[:],
                                                scalar1=1.0 / D, scalar2=1e-5,
                                                op0=ALU.mult, op1=ALU.add)
                                .then_inc(sv.h, 1),
                                v.wait_ge(sv.h, v3)))
                            s2 = ss.bump()
                            on('scalar', lambda s, t=t, v3=v3: (
                                s.wait_ge(sv.h, v3),
                                s.activation(out=lnv[:], in_=var[:],
                                             func=AF.Ln, bias=0.0, scale=1.0),
                                s.activation(out=st_rstd[:, t:t + 1],
                                             in_=lnv[:], func=AF.Exp,
                                             bias=0.0, scale=-0.5)
                                .then_inc(ss.h, 1)))
                            v5 = sv.bump()
                            on('vector', lambda v, t=t, s2=s2: (
                                v.wait_ge(ss.h, s2),
                                v.scalar_tensor_tensor(
                                    out=st_nmr[:, t:t + 1], in0=mmean[:],
                                    scalar=-1.0, in1=st_rstd[:, t:t + 1],
                                    op0=ALU.mult, op1=ALU.mult)
                                .then_inc(sv.h, 1)))
                            s3 = ss.bump()
                            on('scalar', lambda s, t=t, v5=v5,
                               tpm=(tp_x_done[t - 2] if t >= 2 else None): (
                                s.wait_ge(sv.h, v5),
                                (s.wait_ge(tpm[0].h, tpm[1])
                                 if tpm else None),
                                s.activation(out=xn_bf[t % 2][:],
                                             in_=x_t[t % 2][:], func=AF.Identity,
                                             bias=st_nmr[:, t:t + 1],
                                             scale=st_rstd[:, t:t + 1]),
                                s.drain().then_inc(ss.h, 1)))
                            ln_s3.append(s3)
                            lnxt = dtpb.next()
                            on('scalar', lambda s, t=t, s3=s3, lnxt=lnxt: (
                                s.wait_ge(ss.h, s3),
                                s.dma_start(
                                    out=xnT[:, :, t * 128:(t + 1) * 128],
                                    in_=xn_bf[t % 2][:], transpose=True)
                                .then_inc(lnxt.h, 16)))
                            tp_x_done.append((lnxt, lnxt.bump(16)))

                        ps_qkv = stp.enter_context(nc.psum_tensor(
                            f"ps_qkv_{b}", [128, 2, 512], F32))

                        def load_qk(j, wdram):
                            marks = {}
                            for hp in range(8):
                                for hf in range(2):
                                    lnw = dcb.next()
                                    on('gpsimd', lambda g, wdram=wdram,
                                       wsb=wbuf[j % 2], hp=hp, hf=hf, lnw=lnw:
                                       g.dma_start(
                                        out=bass.AP(wsb, hp * 128 + hf * 64,
                                                    [[ND * 1024, 128],
                                                     [1024, ND], [1, 64]]),
                                        in_=bass.AP(wdram.tensor,
                                                    (2 * hp + hf) * D * DH,
                                                    [[64, 128],
                                                     [128 * 64, ND],
                                                     [1, 64]]))
                                       .then_inc(lnw.h, 16))
                                    marks[id(lnw)] = (lnw, lnw.bump(16))
                            return list(marks.values())

                        wl_q = load_qk(0, qw_d)
                        wl_k = load_qk(1, kw_d)

                        wmark('tensor', tp_x_done[-1])
                        slot = 0
                        ev_done = []
                        wuse = {}
                        if b == 0:
                            for m in setup_marks:
                                wmark('scalar', m)
                        for j, (bsb, dst) in enumerate(((qb_sb, qt_all),
                                                       (kb_sb, kt_all))):
                            for m in (wl_q if j == 0 else wl_k):
                                wmark('tensor', m)
                            for hp in range(8):
                                for c in range(4):
                                    sl = slot % 2
                                    if slot >= 2:
                                        e, m = ev_done[slot - 2]
                                        wait('tensor', ss if e == 'ss'
                                             else sv, m)
                                    tv = stn.bump()
                                    on('tensor', lambda tn, j=j, hp=hp, c=c,
                                       sl=sl: [
                                        tn.matmul(
                                            ps_qkv[:, sl, :],
                                            wbuf[j % 2][:, d,
                                                        hp * 128:(hp + 1) * 128],
                                            xnT[:, d, c * 512:(c + 1) * 512],
                                            start=(d == 0),
                                            stop=(d == ND - 1))
                                        for d in range(ND)][-1]
                                       .then_inc(stn.h, 1))
                                    se = ss.bump()
                                    on('scalar', lambda s, bsb=bsb, dst=dst,
                                       hp=hp, c=c, sl=sl, tv=tv: (
                                        s.wait_ge(stn.h, tv),
                                        s.activation(
                                            out=dst[:, hp,
                                                    c * 512:(c + 1) * 512],
                                            in_=ps_qkv[:, sl, :],
                                            func=AF.Identity,
                                            bias=bsb[:, hp:hp + 1],
                                            scale=1.0).then_inc(ss.h, 1)))
                                    ev_done.append(('ss', se))
                                    slot += 1
                            wuse[j] = stn.v
                        wait('gpsimd', stn, wuse[0])
                        vmarks = {}
                        for h2 in range(H):
                            lnv2 = dcb.next()
                            on('gpsimd', lambda g, h2=h2, lnv2=lnv2:
                               g.dma_start(
                                out=bass.AP(wbuf[0], h2 * 64,
                                            [[ND * 1024, 128], [1024, ND],
                                             [1, 64]]),
                                in_=bass.AP(vw_d.tensor, h2 * D * DH,
                                            [[64, 128], [128 * 64, ND],
                                             [1, 64]]))
                               .then_inc(lnv2.h, 16))
                            vmarks[id(lnv2)] = (lnv2, lnv2.bump(16))
                        for m in vmarks.values():
                            wmark('tensor', m)
                        wmark('vector', bias_marks[1])
                        wait('vector', sv, ones_done)
                        for t in range(NT):
                            for c in range(2):
                                sl = slot % 2
                                if slot >= 2:
                                    e, m = ev_done[slot - 2]
                                    wait('tensor', ss if e == 'ss' else sv, m)
                                tv = stn.bump()
                                on('tensor', lambda tn, t=t, c=c, sl=sl: [
                                    tn.matmul(
                                        ps_qkv[:, sl, :],
                                        xnT[:, d, t * 128:(t + 1) * 128],
                                        wbuf[0][:, d, c * 512:(c + 1) * 512],
                                        start=(d == 0), stop=(d == ND - 1))
                                    for d in range(ND)][-1]
                                   .then_inc(stn.h, 1))
                                vvl = None
                                for hh in range(8):
                                    h = c * 8 + hh
                                    vvl = sv.bump()
                                    on('vector', lambda v, t=t, sl=sl, hh=hh,
                                       h=h, tv=tv: (
                                        (v.wait_ge(stn.h, tv)
                                         if hh == 0 else None),
                                        v.scalar_tensor_tensor(
                                            out=v_perm[:, t,
                                                       h * 65:h * 65 + 64],
                                            in0=ps_qkv[:, sl,
                                                       hh * 64:(hh + 1) * 64],
                                            scalar=0.0,
                                            in1=vb_t[:, h * 64:(h + 1) * 64],
                                            op0=ALU.add, op1=ALU.add)
                                        .then_inc(sv.h, 1)))
                                ev_done.append(('sv', vvl))
                                slot += 1
                        for (e, m) in ev_done[-2:]:
                            wait('tensor', ss if e == 'ss' else sv, m)
                        barrier()
                        if DEBUG_TAPS and b == DEBUG_TAPS_BATCH:
                            dbl = [dxb.next() for _ in range(6)]
                            on('sync', lambda sync, dbl=dbl: (
                                sync.dma_start(out=dbg["xnT"],
                                               in_=attn_out[:].rearrange(
                                                   "p a c -> p (a c)")
                                               .rearrange("p (u w) -> p u w",
                                                          u=ND))
                                .then_inc(dbl[0].h, 16),
                                sync.dma_start(out=dbg["qt"], in_=qt_all[:])
                                .then_inc(dbl[1].h, 16),
                                sync.dma_start(out=dbg["kt"], in_=kt_all[:])
                                .then_inc(dbl[2].h, 16),
                                sync.dma_start(out=dbg["vp"], in_=v_perm[:])
                                .then_inc(dbl[3].h, 16),
                                sync.dma_start(out=dbg["rstd"], in_=st_rstd[:])
                                .then_inc(dbl[4].h, 16),
                                sync.dma_start(out=dbg["nmr"], in_=st_nmr[:])
                                .then_inc(dbl[5].h, 16)))
                            for ln2 in dbl:
                                wmark('sync', (ln2, ln2.bump(16)))
                            barrier()

                    with ExitStack() as sta3:
                        esc = sta3.enter_context(nc.sbuf_tensor(
                            f"esc_{b}", [128, QC], BF16))
                        dstage = sta3.enter_context(nc.sbuf_tensor(
                            f"dstage_{b}", [128, H * NT], F32))
                        drec = sta3.enter_context(nc.sbuf_tensor(
                            f"drec_{b}", [128, H * NT], F32))
                        ps_l = sta3.enter_context(nc.psum_tensor(
                            f"ps_l_{b}", [128, 2, 512], F32))
                        ps_pv = sta3.enter_context(nc.psum_tensor(
                            f"ps_pv_{b}", [128, 2, 512], F32))
                        ps_o = sta3.enter_context(nc.psum_tensor(
                            f"ps_o_{b}", [128, 2, 512], F32))
                        stw = ExitStack()
                        if with_w:
                            plan_w_open(stw)

                        lslot, pvslot = 0, 0
                        l_cons, pv_cons = [], []
                        last_pv_tv = 0
                        for h in range(H):
                            hp, po = h // 2, (h % 2) * 64
                            for c in range(NQC):
                                if with_w:
                                    w_step()
                                nkt = 4 * (c + 1)
                                exp_marks = []
                                if last_pv_tv:
                                    wait('scalar', stn, last_pv_tv)
                                    wait('vector', stn, last_pv_tv)
                                for kt in range(nkt):
                                    slq = lslot % 2
                                    if lslot >= 2:
                                        wait('tensor', ss, l_cons[lslot - 2])
                                    tv = stn.bump()
                                    on('tensor', lambda tn, hp=hp, po=po,
                                       kt=kt, c=c, slq=slq: tn.matmul(
                                        ps_l[:, slq, :],
                                        kt_all[po:po + 64, hp,
                                               kt * 128:(kt + 1) * 128],
                                        qt_all[po:po + 64, hp,
                                               c * QC:(c + 1) * QC],
                                        start=True, stop=True)
                                       .then_inc(stn.h, 1))
                                    diag = (kt >= nkt - 4)
                                    se = ss.bump()
                                    if not diag:
                                        on('scalar', lambda s, kt=kt, slq=slq,
                                           tv=tv: (
                                            s.wait_ge(stn.h, tv),
                                            s.activation(
                                                out=pt[:, kt, :],
                                                in_=ps_l[:, slq, :],
                                                func=AF.Exp, bias=neg20[:],
                                                scale=0.125)
                                            .then_inc(ss.h, 1)))
                                        exp_marks.append(('s', se))
                                    else:
                                        on('scalar', lambda s, slq=slq,
                                           tv=tv: (
                                            s.wait_ge(stn.h, tv),
                                            s.activation(
                                                out=esc[:],
                                                in_=ps_l[:, slq, :],
                                                func=AF.Exp, bias=neg20[:],
                                                scale=0.125)
                                            .then_inc(ss.h, 1)))
                                        mv = sv.bump()
                                        mi = kt - (nkt - 4)
                                        on('vector', lambda v, kt=kt, mi=mi,
                                           se=se: (
                                            v.wait_ge(ss.h, se),
                                            v.tensor_tensor(
                                                out=pt[:, kt, :],
                                                in0=esc[:],
                                                in1=masks[:, mi, :],
                                                op=ALU.mult)
                                            .then_inc(sv.h, 1)))
                                        exp_marks.append(('v', mv))
                                        wait('scalar', sv, mv)
                                    l_cons.append(se)
                                    lslot += 1
                                for qt4 in range(4):
                                    qt = c * 4 + qt4
                                    slp = pvslot % 2
                                    if pvslot >= 2:
                                        wait('tensor', sv,
                                             pv_cons[pvslot - 2])
                                    need = exp_marks[:qt + 1]
                                    svn = max([m for k2, m in need
                                               if k2 == 'v'], default=0)
                                    ssn = max([m for k2, m in need
                                               if k2 == 's'], default=0)
                                    if ssn:
                                        wait('tensor', ss, ssn)
                                    if svn:
                                        wait('tensor', sv, svn)
                                    tv = stn.bump()
                                    on('tensor', lambda tn, h=h, qt=qt,
                                       qt4=qt4, slp=slp, nk=qt + 1: [
                                        tn.matmul(
                                            ps_pv[:, slp, 0:65],
                                            pt[:, kt2,
                                               qt4 * 128:(qt4 + 1) * 128],
                                            v_perm[:, kt2,
                                                   h * 65:h * 65 + 65],
                                            start=(kt2 == 0),
                                            stop=(kt2 == nk - 1))
                                        for kt2 in range(nk)][-1]
                                       .then_inc(stn.h, 1))
                                    vv1 = sv.bump(2)
                                    on('vector', lambda v, h=h, qt=qt,
                                       slp=slp, tv=tv: (
                                        v.wait_ge(stn.h, tv),
                                        v.tensor_copy(
                                            out=attnv[:, qt,
                                                      h * 64:(h + 1) * 64],
                                            in_=ps_pv[:, slp, 0:64]),
                                        v.tensor_copy(
                                            out=dstage[:, h * NT + qt:
                                                       h * NT + qt + 1],
                                            in_=ps_pv[:, slp, 64:65])
                                        .then_inc(sv.h, 2)))
                                    pv_cons.append(vv1)
                                    pvslot += 1
                                    last_pv_tv = stn.v
                        if with_w:
                            w_flush()
                            for (ln, vv) in dxb.totals() + dtpb.totals():
                                wait('scalar', ln, vv)
                                wait('gpsimd', ln, vv)
                        stw.close()
                        sto = sta3.enter_context(ExitStack())
                        ow_sb = sto.enter_context(nc.sbuf_tensor(
                            f"ow_sb_{b}", [128, 8, D], BF16))
                        anorm = [sto.enter_context(nc.sbuf_tensor(
                            f"anorm{i}_{b}", [128, D], BF16))
                            for i in range(2)]
                        avT = [sto.enter_context(nc.sbuf_tensor(
                            f"avT{i}_{b}", [128, 8, 128], BF16,
                            is_transpose_dest=True)) for i in range(2)]
                        ow_marks = []
                        for half in range(2):
                            lno = dcb.next()
                            on('gpsimd', lambda g, half=half, lno=lno:
                               g.dma_start(
                                out=ow_sb[half * 64:(half + 1) * 64, :, :],
                                in_=bass.AP(ow_d.tensor, half * DH * D,
                                            [[D, 64], [2 * DH * D, 8],
                                             [1, D]])).then_inc(lno.h, 16))
                            ow_marks.append((lno, lno.bump(16)))

                        sdr = ss.bump(2)
                        on('scalar', lambda s, vv=sv.v, sdr=sdr: (
                            s.wait_ge(sv.h, vv),
                            s.activation(out=drec[:], in_=dstage[:],
                                         func=AF.Ln, bias=0.0, scale=1.0)
                            .then_inc(ss.h, 1),
                            s.wait_ge(ss.h, sdr - 1),
                            s.activation(out=drec[:], in_=drec[:],
                                         func=AF.Exp, bias=0.0, scale=-1.0)
                            .then_inc(ss.h, 1),
                            s.wait_ge(ss.h, sdr)))

                        for m in ow_marks:
                            wmark('tensor', m)
                        wmark('vector', bias_marks[0])
                        tpo_done, oev_done, oproj_tv = [], [], []
                        oslot = 0
                        for qt in range(NT):
                            sm = None
                            for h in range(H):
                                sm = ss.bump()
                                on('scalar', lambda s, qt=qt, h=h, sm=sm,
                                   sdr=sdr,
                                   tpm=(tpo_done[qt - 2]
                                        if (h == 0 and qt >= 2) else None): (
                                    (s.wait_ge(ss.h, sdr)
                                     if (h == 0 and qt == 0) else None),
                                    (s.wait_ge(tpm[0].h, tpm[1])
                                     if tpm else None),
                                    s.activation(
                                        out=anorm[qt % 2][:,
                                                          h * 64:(h + 1) * 64],
                                        in_=attnv[:, qt, h * 64:(h + 1) * 64],
                                        func=AF.Identity, bias=0.0,
                                        scale=drec[:, h * NT + qt:
                                                   h * NT + qt + 1]),
                                    (s.drain() if h == H - 1 else s.nop())
                                    .then_inc(ss.h, 1)))
                            if qt >= 2:
                                wait('scalar', stn, oproj_tv[qt - 2])
                            lna = dtpb.next()
                            on('scalar', lambda s, qt=qt, sm=sm, lna=lna: (
                                s.wait_ge(ss.h, sm),
                                s.dma_start(out=avT[qt % 2][:],
                                            in_=anorm[qt % 2][:],
                                            transpose=True)
                                .then_inc(lna.h, 16)))
                            tpo_done.append((lna, lna.bump(16)))
                            for ch in range(2):
                                slo = oslot % 2
                                if oslot >= 2:
                                    wait('tensor', sv, oev_done[oslot - 2])
                                if ch == 0:
                                    wmark('tensor', tpo_done[qt])
                                tv = stn.bump()
                                on('tensor', lambda tn, qt=qt, ch=ch,
                                   slo=slo: [
                                    tn.matmul(
                                        ps_o[:, slo, :],
                                        avT[qt % 2][:, hp2, :],
                                        ow_sb[:, hp2, ch * 512:(ch + 1) * 512],
                                        start=(hp2 == 0), stop=(hp2 == 7))
                                    for hp2 in range(8)][-1]
                                   .then_inc(stn.h, 1))
                                vv = sv.bump()
                                on('vector', lambda v, qt=qt, ch=ch, slo=slo,
                                   tv=tv: (
                                    v.wait_ge(stn.h, tv),
                                    v.scalar_tensor_tensor(
                                        out=attn_out[:, qt,
                                                     ch * 512:(ch + 1) * 512],
                                        in0=ps_o[:, slo, :], scalar=0.0,
                                        in1=ob_t[:, ch * 512:(ch + 1) * 512],
                                        op0=ALU.add, op1=ALU.add)
                                    .then_inc(sv.h, 1)))
                                oev_done.append(vv)
                                oslot += 1
                            oproj_tv.append(stn.v)
                        wait('tensor', sv, oev_done[-1])
                        wait('tensor', sv, oev_done[-2])
                        if DEBUG_TAPS and b == DEBUG_TAPS_BATCH:
                            wait('sync', sv, sv.v)
                            dbl2 = [dxb.next() for _ in range(4)]
                            on('sync', lambda sync, dbl2=dbl2: (
                                sync.dma_start(out=dbg["ao"], in_=attn_out[:])
                                .then_inc(dbl2[0].h, 16),
                                sync.dma_start(out=dbg["pt"], in_=pt[:])
                                .then_inc(dbl2[1].h, 16),
                                sync.dma_start(out=dbg["dst"], in_=dstage[:])
                                .then_inc(dbl2[2].h, 16),
                                sync.dma_start(out=dbg["drc"], in_=drec[:])
                                .then_inc(dbl2[3].h, 16)))
                            for ln2 in dbl2:
                                wmark('sync', (ln2, ln2.bump(16)))
                    if with_w:
                        w_flush()
                    stw.close()
                    barrier()

                with ExitStack() as stf:
                    alloc_qs(stf, f"b{b}")
                    xqT = stf.enter_context(nc.sbuf_tensor(
                        f"xqT_{b}", [128, ND, S], BF16, is_transpose_dest=True))
                    xg = stf.enter_context(nc.sbuf_tensor(
                        f"xg_{b}", [128, 4, D], F32))
                    xnf = [stf.enter_context(nc.sbuf_tensor(
                        f"xnf{i}_{b}", [128, D], F32)) for i in range(2)]
                    xq_bf = [stf.enter_context(nc.sbuf_tensor(
                        f"xq_bf{i}_{b}", [128, D], BF16)) for i in range(2)]
                    hterm = [stf.enter_context(nc.sbuf_tensor(
                        f"hterm{i}_{b}", [128, 512], F32)) for i in range(2)]
                    silu_sb = [stf.enter_context(nc.sbuf_tensor(
                        f"silu{i}_{b}", [128, 512], F32)) for i in range(2)]
                    hq4 = stf.enter_context(nc.sbuf_tensor(
                        f"hq4_{b}", [128, 4, DI], BF16))
                    hqT = stf.enter_context(nc.sbuf_tensor(
                        f"hqT_{b}", [128, NI, 512], BF16,
                        is_transpose_dest=True))
                    wgu = [stf.enter_context(nc.sbuf_tensor(
                        f"wgu{i}_{b}", [128, ND, 1024], BF16))
                        for i in range(2)]
                    wdt_sb = [stf.enter_context(nc.sbuf_tensor(
                        f"wdt_sb{i}_{b}", [128, 512], BF16))
                        for i in range(2)]
                    out_sb = [stf.enter_context(nc.sbuf_tensor(
                        f"out_sb{i}_{b}", [128, 512], F32)) for i in range(2)]
                    ps_gu = stf.enter_context(nc.psum_tensor(
                        f"ps_gu_{b}", [128, 4, 512], F32))
                    ps_d = stf.enter_context(nc.psum_tensor(
                        f"ps_d_{b}", [128, 4, 512], F32))
                    ps_tp = ps_d[:, 0, :].bitcast(BF16)

                    NCH = DI // 256
                    hq_tp_hist = []
                    xnf_qv = []
                    for g4 in range(4):
                        t0 = g4 * 4
                        for tt in range(4):
                            t = t0 + tt
                            lnb = dxb.next()
                            on('sync', lambda sync, t=t, tt=tt, lnb=lnb:
                               sync.dma_start(
                                out=xg[:, tt, :],
                                in_=x_d[b, t * 128:(t + 1) * 128, :])
                               .then_inc(lnb.h, 16))
                            xldt = (lnb, lnb.bump(16))
                            se = ss.bump()
                            on('scalar', lambda s, t=t, tt=tt, xldt=xldt,
                               gq=(xnf_qv[-2] if len(xnf_qv) >= 2 else 0): (
                                s.wait_ge(xldt[0].h, xldt[1]),
                                (s.wait_ge(sv.h, gq) if gq else None),
                                s.activation(out=xnf[tt % 2][:],
                                             in_=xg[:, tt, :],
                                             func=AF.Identity,
                                             bias=st_nmr[:, t:t + 1],
                                             scale=st_rstd[:, t:t + 1])
                                .then_inc(ss.h, 1)))
                            wait('vector', ss, se)
                            if len(xq_tp_hist) >= 2:
                                wait('vector', sv, xq_tp_hist[-2])
                            emit_quant(xnf[tt % 2][:], xq_bf[tt % 2][:], D)
                            qv = sv.v
                            xnf_qv.append(qv)
                            if xq_tp_hist:
                                wait('tensor', sv, xq_tp_hist[-1])
                            tpv = stn.bump()
                            on('tensor', lambda tn, tt=tt, qv=qv: (
                                tn.wait_ge(sv.h, qv),
                                [tn.transpose(
                                    ps_tp[:, d * 128:(d + 1) * 128],
                                    xq_bf[tt % 2][:, d * 128:(d + 1) * 128],
                                    ident[:]) for d in range(ND)][-1]
                                .then_inc(stn.h, 1)))
                            sv.bump()
                            on('vector', lambda v, t=t, tpv=tpv: (
                                v.wait_ge(stn.h, tpv),
                                v.tensor_copy(
                                    out=bass.AP(xqT, t * 128,
                                                [[ND * S, 128], [S, ND],
                                                 [1, 128]]),
                                    in_=ps_tp[:]).then_inc(sv.h, 1)))
                            xq_tp_hist.append(sv.v)
                        wait('tensor', sv, xq_tp_hist[-1])
                        if DEBUG_TAPS and b == DEBUG_TAPS_BATCH and g4 == 3:
                            wait('sync', sv, xq_tp_hist[-1])
                            lnd = dxb.next()
                            on('sync', lambda sync, lnd=lnd: sync.dma_start(
                                out=dbg["xqT"], in_=xqT[:])
                               .then_inc(lnd.h, 16))
                            lnd.bump(16)

                        wld = []
                        gu_cons = []
                        gslot = 0
                        CHUNKS = [(o, min(512, DI - o))
                                  for o in range(0, DI, 512)]
                        for c, (coff, cw) in enumerate(CHUNKS):
                            if c >= 2:
                                wait('sync', stn, wld[c - 2])
                            lng = dxb.next()
                            lnu = dxb.next()
                            on('sync', lambda sync, coff=coff, cw=cw, c=c,
                               lng=lng, lnu=lnu: (
                                sync.dma_start(
                                    out=wgu[c % 2][:, :, 0:cw],
                                    in_=wgt_d[:, :, coff:coff + cw])
                                .then_inc(lng.h, 16),
                                sync.dma_start(
                                    out=wgu[c % 2][:, :, cw:2 * cw],
                                    in_=wut_d[:, :, coff:coff + cw])
                                .then_inc(lnu.h, 16)))
                            wldv = ((lng, lng.bump(16)), (lnu, lnu.bump(16)))
                            lastmm = None
                            for tt in range(4):
                                t = t0 + tt
                                sg0 = 2 * (gslot % 2)
                                if gslot >= 2:
                                    wait('tensor', sv, gu_cons[gslot - 2])
                                tv = stn.bump()
                                on('tensor', lambda tn, c=c, t=t, sg0=sg0,
                                   cw=cw, wldv=wldv: (
                                    tn.wait_ge(wldv[0][0].h, wldv[0][1]),
                                    tn.wait_ge(wldv[1][0].h, wldv[1][1]),
                                    [tn.matmul(
                                        ps_gu[:, sg0, 0:cw],
                                        xqT[:, d, t * 128:(t + 1) * 128],
                                        wgu[c % 2][:, d, 0:cw],
                                        start=(d == 0), stop=(d == ND - 1))
                                     for d in range(ND)],
                                    [tn.matmul(
                                        ps_gu[:, sg0 + 1, 0:cw],
                                        xqT[:, d, t * 128:(t + 1) * 128],
                                        wgu[c % 2][:, d, cw:2 * cw],
                                        start=(d == 0), stop=(d == ND - 1))
                                     for d in range(ND)][-1]
                                    .then_inc(stn.h, 1)))
                                lastmm = stn.v
                                se = ss.bump()
                                on('scalar', lambda s, sg0=sg0, tv=tv,
                                   cw=cw, gsl=gslot % 2,
                                   gg=(gu_cons[gslot - 2]
                                       if gslot >= 2 else 0): (
                                    s.wait_ge(stn.h, tv),
                                    (s.wait_ge(sv.h, gg) if gg else None),
                                    s.activation(out=silu_sb[gsl][:, 0:cw],
                                                 in_=ps_gu[:, sg0, 0:cw],
                                                 func=AF.Silu, bias=0.0,
                                                 scale=1.0)
                                    .then_inc(ss.h, 1)))
                                vv = sv.bump()
                                on('vector', lambda v, tt=tt, sg0=sg0,
                                   cw=cw, se=se, gsl=gslot % 2: (
                                    v.wait_ge(ss.h, se),
                                    v.tensor_tensor(
                                        out=hterm[tt % 2][:, 0:cw],
                                        in0=silu_sb[gsl][:, 0:cw],
                                        in1=ps_gu[:, sg0 + 1, 0:cw],
                                        op=ALU.mult).then_inc(sv.h, 1)))
                                gu_cons.append(vv)
                                emit_quant(hterm[tt % 2][:, 0:cw],
                                           hq4[:, tt, coff:coff + cw],
                                           cw)
                                gslot += 1
                            wld.append(lastmm)
                        hq_done = sv.v

                        for tt in range(4):
                            for i0 in (0, 8, 16):
                                ni = min(8, NI - i0)
                                if hq_tp_hist:
                                    wait('tensor', sv, hq_tp_hist[-1])
                                tpv = stn.bump()
                                on('tensor', lambda tn, tt=tt, i0=i0, ni=ni,
                                   hq_done=hq_done: (
                                    (tn.wait_ge(sv.h, hq_done)
                                     if (tt == 0 and i0 == 0) else None),
                                    [tn.transpose(
                                        ps_tp[:, k * 128:(k + 1) * 128],
                                        hq4[:, tt,
                                            (i0 + k) * 128:(i0 + k + 1) * 128],
                                        ident[:]) for k in range(ni)][-1]
                                    .then_inc(stn.h, 1)))
                                sv.bump()
                                on('vector', lambda v, tt=tt, i0=i0, ni=ni,
                                   tpv=tpv: (
                                    v.wait_ge(stn.h, tpv),
                                    v.tensor_copy(
                                        out=bass.AP(hqT,
                                                    i0 * 512 + tt * 128,
                                                    [[NI * 512, 128],
                                                     [512, ni], [1, 128]]),
                                        in_=ps_tp[:, 0:ni * 128])
                                    .then_inc(sv.h, 1)))
                                hq_tp_hist.append(sv.v)
                        wait('tensor', sv, hq_tp_hist[-1])
                        if DEBUG_TAPS and b == DEBUG_TAPS_BATCH and g4 == 1:
                            wait('sync', sv, hq_tp_hist[-1])
                            lnd2 = dxb.next()
                            on('sync', lambda sync, lnd2=lnd2: sync.dma_start(
                                out=dbg["hqT"], in_=hqT[:])
                               .then_inc(lnd2.h, 16))
                            lnd2.bump(16)

                        fin_marks = []
                        for ch in range(2):
                            wdld = []
                            for i in range(NI):
                                if i >= 2:
                                    wait('sync', stn, wdld[i - 2])
                                lnd3 = dxb.next()
                                on('sync', lambda sync, i=i, ch=ch, lnd3=lnd3:
                                   sync.dma_start(
                                    out=wdt_sb[i % 2][:],
                                    in_=wdt_d[:, i, ch * 512:(ch + 1) * 512])
                                   .then_inc(lnd3.h, 16))
                                wv = (lnd3, lnd3.bump(16))
                                if i == 0 and fin_marks:
                                    wait('tensor', sv, fin_marks[-1])
                                tv = stn.bump()
                                on('tensor', lambda tn, i=i, wv=wv: (
                                    tn.wait_ge(wv[0].h, wv[1]),
                                    [tn.matmul(
                                        ps_d[:, tt2, :],
                                        hqT[:, i, tt2 * 128:(tt2 + 1) * 128],
                                        wdt_sb[i % 2][:],
                                        start=(i == 0), stop=(i == NI - 1))
                                     for tt2 in range(4)][-1]
                                    .then_inc(stn.h, 1)))
                                wdld.append(stn.v)
                            for tt in range(4):
                                t = t0 + tt
                                vf = sv.bump(2)
                                on('vector', lambda v, t=t, tt=tt, ch=ch,
                                   wl=wdld[-1], qA=qs['A']: (
                                    v.wait_ge(stn.h, wl),
                                    v.scalar_tensor_tensor(
                                        out=qA[:, 0:512],
                                        in0=ps_d[:, tt, :], scalar=0.0,
                                        in1=attn_out[:, t,
                                                     ch * 512:(ch + 1) * 512],
                                        op0=ALU.add, op1=ALU.add),
                                    v.tensor_tensor(
                                        out=out_sb[tt % 2][:],
                                        in0=qA[:, 0:512],
                                        in1=xg[:, tt, ch * 512:(ch + 1) * 512],
                                        op=ALU.add),
                                    v.drain().then_inc(sv.h, 2)))
                                fin_marks.append(vf)
                                lny = dxb.next()
                                on('sync', lambda sync, t=t, tt=tt, ch=ch,
                                   vf=vf, lny=lny: (
                                    sync.wait_ge(sv.h, vf),
                                    sync.dma_start(
                                        out=y_d[b, t * 128:(t + 1) * 128,
                                                ch * 512:(ch + 1) * 512],
                                        in_=out_sb[tt % 2][:])
                                    .then_inc(lny.h, 16)))
                                out_hist.append((lny, lny.bump(16)))
                                if len(out_hist) >= 2:
                                    wmark('vector', out_hist[-2])
                        barrier()

        plan_batch(0, with_w=True)
        for b in range(1, B_PER_CORE):
            plan_batch(b)

        for (ln, vv) in dxb.totals() + dtpb.totals() + dcb.totals():
            wait('sync', ln, vv)
        barrier()

        blk = st.enter_context(nc.Block())
        for e in ENGINES:
            def make(e):
                def body(eng):
                    for fn in prog[e]:
                        fn(eng)
                return body
            getattr(blk, e)(make(e))

    return nc


_NC_CACHE = [None]


def kernel(L0_attn_x, L0_attn_Q_w, L0_attn_Q_b, L0_attn_K_w, L0_attn_K_b,
           L0_attn_V_w, L0_attn_V_b, L0_attn_O_w, L0_attn_O_b,
           L0_ffn_w_gate, L0_ffn_w_up, L0_ffn_w_down):
    if _NC_CACHE[0] is None:
        _NC_CACHE[0] = build_program()
    nc = _NC_CACHE[0]

    x = np.ascontiguousarray(np.asarray(L0_attn_x, dtype=np.float32))
    common = {
        "qw": np.ascontiguousarray(np.asarray(L0_attn_Q_w, np.float32)),
        "kw": np.ascontiguousarray(np.asarray(L0_attn_K_w, np.float32)),
        "vw": np.ascontiguousarray(np.asarray(L0_attn_V_w, np.float32)),
        "qb": np.ascontiguousarray(np.asarray(L0_attn_Q_b, np.float32)),
        "kb": np.ascontiguousarray(np.asarray(L0_attn_K_b, np.float32)),
        "vb": np.ascontiguousarray(np.asarray(L0_attn_V_b, np.float32)),
        "ow": np.ascontiguousarray(np.asarray(L0_attn_O_w, np.float32)),
        "ob": np.ascontiguousarray(np.asarray(L0_attn_O_b, np.float32))
        .reshape(1, D),
        "wg": np.ascontiguousarray(np.asarray(L0_ffn_w_gate, np.float32)),
        "wu": np.ascontiguousarray(np.asarray(L0_ffn_w_up, np.float32)),
        "wd": np.ascontiguousarray(np.asarray(L0_ffn_w_down, np.float32)),
    }
    in_maps = []
    for c in range(NCORES):
        m = dict(common)
        m["x"] = np.ascontiguousarray(x[c * B_PER_CORE:(c + 1) * B_PER_CORE])
        in_maps.append(m)

    res = run_bass_kernel_spmd(nc, in_maps, core_ids=list(range(NCORES)))
    out = np.concatenate([np.asarray(r["y"]) for r in res.results], axis=0)
    return out.astype(np.float32)
